# version 02
# baseline (speedup 1.0000x reference)
import numpy as np
import ml_dtypes
import concourse.bass as bass
import concourse.mybir as mybir
from concourse.bass_utils import run_bass_kernel_spmd
from concourse.bass_types import AP
from contextlib import ExitStack

F32 = mybir.dt.float32
BF16 = mybir.dt.bfloat16
AF = mybir.ActivationFunctionType
ALU = mybir.AluOpType
AX = mybir.AxisListType

P = 128
S = 2048
D = 1024
NCH = D // P
NB = 2
NCORE = 8
PP = 2 * S - 1
H0 = 16
HD = 64

NEG = -1.0e30


class FW:
    def __init__(self, nc):
        self.nc = nc
        self.sems = {}
        self.counts = {}
        self.hwm = {}
        self.serialize = False
        self.tagmap = {}
        self.pool_i = {"s": 0, "g": 0}
        self.engines = []
        self.strict = False

    def sem(self, name):
        if name not in self.sems:
            self.sems[name] = self.nc.alloc_semaphore(name=name)
            self.counts[name] = 0
        return self.sems[name]

    def h(self, name):
        return (name, self.counts[name])

    def wait(self, eng, *handles):
        if self.serialize:
            handles = tuple(handles) + self.all_handles()
        for hh in handles:
            if hh is None:
                continue
            name, cnt = hh
            if cnt <= 0:
                continue
            key = (eng.engine, name)
            if self.hwm.get(key, 0) < cnt:
                eng.wait_ge(self.sems[name], cnt)
                self.hwm[key] = cnt

    def inc(self, inst, name, amount=1):
        inst.then_inc(self.sems[name], amount)
        self.counts[name] += amount
        return (name, self.counts[name])

    def incn(self, eng, name):
        return self.inc(eng.nop(nofuse=True), name)

    def dma(self, eng, out, in_, tag, waits=(), **kw):
        kind = "g" if eng.engine == mybir.EngineType.Pool else "s"
        s = self._tag_sem(tag, kind)
        self.wait(eng, *waits)
        inst = eng.dma_start(out=out, in_=in_, **kw)
        return self.inc(inst, s, 16)

    def dmat(self, eng, out, in_, tag, waits=()):
        s = self._tag_sem(tag, "s")
        self.wait(eng, *waits)
        inst = eng.dma_start_transpose(out, in_)
        return self.inc(inst, s, 16)

    _ENG_SEM = {}

    def op(self, eng, fn, *args, sem_name=None, waits=(), **kw):
        if self.serialize and sem_name is None:
            sem_name = "dve" if eng.engine == mybir.EngineType.DVE else None
        if self.strict:
            en = {mybir.EngineType.DVE: "dve",
                  mybir.EngineType.Activation: "act"}.get(eng.engine)
            if en is not None:
                self.wait(eng, (en, self.counts[en]))
                if sem_name is None:
                    sem_name = en
        self.wait(eng, *waits)
        inst = fn(*args, **kw)
        if sem_name is not None:
            return self.inc(inst, sem_name, 1)
        return None

    def all_handles(self):
        return tuple(self.h(n) for n in self.sems)

    def phase_barrier(self, engines=None):
        hs = self.all_handles()
        for eng in (engines or self.engines):
            self.wait(eng, *hs)

    def new_phase(self):
        self.phase_barrier()
        self.tagmap = {}
        self.pool_i = {"s": 0, "g": 0}

    def _tag_sem(self, tag, kind="s"):
        key = kind + tag
        if key not in self.tagmap:
            name = f"{kind}{self.pool_i[kind]}"
            self.pool_i[kind] += 1
            self.sem(name)
            self.tagmap[key] = name
        return self.tagmap[key]

    def ht(self, tag, kind="s"):
        return self.h(self._tag_sem(tag, kind))


def _host_consts():
    pos = np.arange(S - 1, -S, -1, dtype=np.float64)
    inv_freq = 1.0 / (10000.0 ** (np.arange(0, D, 2, dtype=np.float64) / D))
    ang = pos[:, None] * inv_freq[None, :]
    pe = np.concatenate([np.sin(ang), np.cos(ang)], axis=-1)
    pe_rev = pe[::-1]
    peT = np.zeros((D, PP + 1), dtype=np.float32)
    peT[:, :PP] = pe_rev.T.astype(np.float32)

    dec = np.exp(-np.arange(S, dtype=np.float64) / S).astype(np.float32)
    dec_pc = dec.reshape(S // P, P).T.copy()

    r = np.arange(P)
    trineg = np.where(r[None, :] <= r[:, None], 0.0, NEG).astype(np.float32)
    eyef = np.eye(P, dtype=np.float32)
    return peT, dec_pc, trineg, eyef


def _pc(v):
    return np.ascontiguousarray(v.reshape(-1, P).T)


def _host_prep(inputs):
    g = lambda k: np.asarray(inputs[k], np.float32)
    peT, dec_pc, trineg, eyef = _host_consts()
    com = {
        "WqT": np.ascontiguousarray(g("L0_attn_W_q").T),
        "WkT": np.ascontiguousarray(g("L0_attn_W_k").T),
        "WvT": np.ascontiguousarray(g("L0_attn_W_v").T),
        "WoT": np.ascontiguousarray(g("L0_attn_W_o").T),
        "bq_row": g("L0_attn_b_q")[None, :],
        "bk_row": g("L0_attn_b_k")[None, :],
        "bv_row": g("L0_attn_b_v")[None, :],
        "bo_pc": _pc(g("L0_attn_b_o")),
        "fwq": g("L0_ffn_wq"), "fwk": g("L0_ffn_wk"), "fwv": g("L0_ffn_wv"),
        "fwp": g("L0_ffn_wp"), "fwo": g("L0_ffn_wo"),
        "ln1g_pc": _pc(g("L0_ffn_ln1_g")), "ln1b_pc": _pc(g("L0_ffn_ln1_b")),
        "ln2g_pc": _pc(g("L0_ffn_ln2_g")), "ln2b_pc": _pc(g("L0_ffn_ln2_b")),
        "sel_w": g("L0_ffn_sel_w"),
        "keys": g("L0_ffn_keys"), "values": g("L0_ffn_values"),
        "a1wq": g("L1_attn_wq"), "a1wk": g("L1_attn_wk"), "a1wv": g("L1_attn_wv"),
        "a1bq_row": g("L1_attn_bq")[None, :],
        "a1bk_row": g("L1_attn_bk")[None, :],
        "a1bv_row": g("L1_attn_bv")[None, :],
        "a1wo": g("L1_attn_wo"),
        "a1bo_pc": _pc(g("L1_attn_bo")),
        "gam_pc": _pc(g("L1_ffn_gamma")), "bet_pc": _pc(g("L1_ffn_beta")),
        "WgT": np.ascontiguousarray(g("L1_ffn_Wg").T),
        "W1": g("L1_ffn_W1"),
        "b1_pc": np.ascontiguousarray(
            g("L1_ffn_b1").reshape(8, 32, P).transpose(0, 2, 1)),
        "W2": g("L1_ffn_W2"),
        "b2pc": np.ascontiguousarray(
            g("L1_ffn_b2").reshape(8, 8, P).transpose(2, 0, 1)),
        "peT": peT, "dec_pc": dec_pc, "trineg": trineg, "eyef": eyef,
    }
    x = np.asarray(inputs["L0_attn_x"], np.float32)
    in_maps = []
    for c in range(NCORE):
        xb = x[2 * c:2 * c + 2]
        m = dict(com)
        m["x"] = np.ascontiguousarray(xb.transpose(0, 2, 1))
        in_maps.append(m)
    return in_maps


def build_program(dbg=(), limit=99, serialize=False, p2_stop=False, sim_strict=False):
    nc = bass.Bass("TRN2", target_bir_lowering=False, debug=False)
    fw = FW(nc)
    fw.serialize = serialize
    fw.strict = sim_strict
    for s_ in ["pe", "act", "dve"]:
        fw.sem(s_)

    def din(name, shape):
        return nc.dram_tensor(name, shape, F32, kind="ExternalInput").ap()

    x_in = din("x", [NB, D, S])
    WqT, WkT, WvT, WoT = (din(n, [D, D]) for n in ["WqT", "WkT", "WvT", "WoT"])
    bq_row, bk_row, bv_row = (din(n, [1, D]) for n in ["bq_row", "bk_row", "bv_row"])
    bo_pc = din("bo_pc", [P, NCH])
    fwq, fwk, fwv, fwp, fwo = (din(n, [D, D]) for n in
                               ["fwq", "fwk", "fwv", "fwp", "fwo"])
    ln1g_pc, ln1b_pc = din("ln1g_pc", [P, NCH]), din("ln1b_pc", [P, NCH])
    ln2g_pc, ln2b_pc = din("ln2g_pc", [P, NCH]), din("ln2b_pc", [P, NCH])
    sel_w = din("sel_w", [D, 32])
    keys = din("keys", [32, D, 128])
    values = din("values", [32, 128, D])
    a1wq, a1wk, a1wv = din("a1wq", [D, 64]), din("a1wk", [D, 64]), din("a1wv", [D, 64])
    a1bq_row, a1bk_row, a1bv_row = (din(n, [1, 64]) for n in
                                    ["a1bq_row", "a1bk_row", "a1bv_row"])
    a1wo = din("a1wo", [64, D])
    a1bo_pc = din("a1bo_pc", [P, NCH])
    gam_pc, bet_pc = din("gam_pc", [P, NCH]), din("bet_pc", [P, NCH])
    WgT = din("WgT", [D, 8])
    W1 = din("W1", [8, D, 4096])
    b1_pc = din("b1_pc", [8, P, 32])
    W2 = din("W2", [8, 4096, D])
    b2pc = din("b2pc", [P, 8, NCH])
    peT = din("peT", [D, PP + 1])
    dec_pc = din("dec_pc", [P, S // P])
    trineg_in = din("trineg", [P, P])
    eyef_in = din("eyef", [P, P])

    y_out = nc.dram_tensor("y", [NB, D, S], F32, kind="ExternalOutput").ap()

    scr = {}

    def dscr(name, shape, dtype):
        kind = "ExternalOutput" if name in dbg else "Internal"
        t = nc.dram_tensor(name, shape, dtype, kind=kind)
        scr[name] = t.ap()
        return scr[name]

    xn_d = dscr("xn_d", [NB, D, S], BF16)
    ccos_d = dscr("ccos_d", [NB, D, S], BF16)
    lnr_d = dscr("lnr_d", [NB, 4, 3, 512], F32)
    p2_d = dscr("p2_d", [D, PP + 1], BF16)
    qf_d = dscr("qf_d", [NB, S, D], BF16)
    s_d = dscr("s_d", [NB, D, S], BF16)
    x1_d = dscr("x1_d", [NB, D, S], BF16)
    qa_d = dscr("qa_d", [NB, D, S], BF16)
    ka_d = dscr("ka_d", [NB, D, S], BF16)
    v_d = dscr("v_d", [NB, NCH, S, P], BF16)
    bd_d = dscr("bd_d", [4, P, 2176], BF16)
    P_d = dscr("P_d", [2, S, S], BF16)
    den_d = dscr("den_d", [16, P], F32)
    src2_d = dscr("src2_d", [NB, D, S], BF16)
    x2_d = dscr("x2_d", [NB, D, S], BF16)
    gates_d = dscr("gates_d", [NB, 32, S], F32)
    xnew_d = dscr("xnew_d", [NB, D, S], F32)
    xnewb_d = dscr("xnewb_d", [NB, D, S], BF16)
    xn1_d = dscr("xn1_d", [NB, D, S], BF16)
    s1_d = dscr("s1_d", [NB, D, S], BF16)
    xnorm_d = dscr("xnorm_d", [NB, D, S], BF16)
    g8_d = dscr("g8_d", [NB, 8, S], F32)

    ctx = ExitStack()
    e_ = ctx.enter_context

    PS = [e_(nc.psum_tensor(f"ps{i}", [P, 512], F32)) for i in range(8)]
    ps_free = [None] * 8

    ones_bf = e_(nc.sbuf_tensor("ones_bf", [1, P], BF16))
    onesc_bf = e_(nc.sbuf_tensor("onesc_bf", [P, 1], BF16))
    ones_f = e_(nc.sbuf_tensor("ones_f", [1, P], F32))
    eps5 = e_(nc.sbuf_tensor("eps5", [P, 1], F32))
    eps6 = e_(nc.sbuf_tensor("eps6", [P, 1], F32))
    zcol = e_(nc.sbuf_tensor("zcol", [P, 1], F32))
    trineg_s = e_(nc.sbuf_tensor("trineg_s", [P, P], F32))
    eyef_s = e_(nc.sbuf_tensor("eyef_s", [P, P], F32))
    eyeb_s = e_(nc.sbuf_tensor("eyeb_s", [P, P], BF16))
    dec_s = e_(nc.sbuf_tensor("dec_s", [P, S // P], F32))
    pcs = {}
    for nm, apin in [("bo_pc", bo_pc), ("ln1g_pc", ln1g_pc), ("ln1b_pc", ln1b_pc),
                     ("ln2g_pc", ln2g_pc), ("ln2b_pc", ln2b_pc),
                     ("a1bo_pc", a1bo_pc), ("gam_pc", gam_pc), ("bet_pc", bet_pc)]:
        pcs[nm] = e_(nc.sbuf_tensor(nm + "_s", [P, NCH], F32))
    maxp2 = e_(nc.sbuf_tensor("maxp2", [1, NCH], F32))

    V = nc.vector
    A = nc.scalar
    T = nc.tensor
    G = nc.gpsimd
    SY = nc.sync
    fw.engines = [T, A, V, SY, G]

    fw.op(V, V.memset, ones_bf[:], 1.0, sem_name="dve")
    fw.op(V, V.memset, onesc_bf[:], 1.0, sem_name="dve")
    fw.op(V, V.memset, ones_f[:], 1.0, sem_name="dve")
    fw.op(V, V.memset, eps5[:], 1e-5, sem_name="dve")
    fw.op(V, V.memset, eps6[:], 1e-6, sem_name="dve")
    fw.op(V, V.memset, zcol[:], 0.0, sem_name="dve")
    fw.dma(SY, trineg_s[:], trineg_in, "cst")
    fw.dma(SY, eyef_s[:], eyef_in, "cst")
    fw.dma(G, eyeb_s[:], eyef_in, "cstp")
    fw.dma(SY, dec_s[:], dec_pc, "cst")
    for nm, apin in [("bo_pc", bo_pc), ("ln1g_pc", ln1g_pc), ("ln1b_pc", ln1b_pc),
                     ("ln2g_pc", ln2g_pc), ("ln2b_pc", ln2b_pc),
                     ("a1bo_pc", a1bo_pc), ("gam_pc", gam_pc), ("bet_pc", bet_pc)]:
        fw.dma(SY, pcs[nm][:], apin, "cst")
    consts_ld = (fw.ht("cst"), fw.ht("cstp", "g"), fw.h("dve"))

    with (nc.sbuf_tensor("peb", [P, NCH, PP + 1], BF16) as peb,
          nc.sbuf_tensor("wpb", [P, NCH, D], BF16) as wpb,
          nc.sbuf_tensor("p2t", [P, PP + 1], BF16) as p2t,
          nc.sbuf_tensor("p2sq", [P, PP + 1], BF16) as p2sq,
          nc.sbuf_tensor("prow", [1, 512], F32) as prow,
          nc.sbuf_tensor("pmax", [1, 1], F32) as pmax):
        p2t_free = None
        p2sq_free = None
        hpe = fw.dma(G, peb[:], peT.rearrange("(c p) q -> p c q", p=P), "peb")
        hwp = fw.dma(G, wpb[:], fwp.rearrange("(c p) m -> p c m", p=P), "wpb")
        NSL = (PP + 1) // 512
        for mc in range(NCH):
            for sl in range(NSL):
                psi = sl % 2
                fw.wait(T, hpe, hwp, ps_free[psi], *consts_ld)
                for dc in range(NCH):
                    inst = T.matmul(PS[psi][:], wpb[:, dc, mc * P:(mc + 1) * P],
                                    peb[:, dc, sl * 512:(sl + 1) * 512],
                                    start=(dc == 0), stop=(dc == NCH - 1))
                hmm = fw.inc(inst, "pe")
                hact = fw.op(A, A.activation, p2t[:, sl * 512:(sl + 1) * 512],
                             PS[psi][:], AF.Identity, sem_name="act",
                             waits=(hmm, p2t_free))
                ps_free[psi] = hact
            hst = fw.dma(SY, p2_d[mc * P:(mc + 1) * P, :], p2t[:],
                         "p2s", waits=(fw.h("act"),))
            hsq = fw.op(A, A.activation, p2sq[:], p2t[:], AF.Square,
                        sem_name="act", waits=(fw.h("act"), p2sq_free))
            p2t_free = hst
            fw.op(V, V.memset, pmax[:], 0.0, sem_name="dve")
            for sl in range(NSL):
                psi = 2 + sl % 2
                fw.wait(T, hsq, ps_free[psi], *consts_ld)
                inst = T.matmul(PS[psi][0:1, :], onesc_bf[:],
                                p2sq[:, sl * 512:(sl + 1) * 512],
                                start=True, stop=True)
                hmm = fw.inc(inst, "pe")
                p2sq_free = hmm
                fw.op(V, V.reduce_max, prow[:, 0:1], PS[psi][0:1, :], axis=AX.X,
                      sem_name="dve", waits=(hmm,))
                hdv = fw.op(V, V.tensor_max, pmax[:], pmax[:], prow[:, 0:1],
                            sem_name="dve")
                ps_free[psi] = hdv
            fw.op(V, V.tensor_copy, maxp2[:, mc:mc + 1], pmax[:],
                  sem_name="dve")
        p0_done = (fw.h("dve"), fw.h("act"), fw.h("pe"))

    def finish():
        for hh in fw.all_handles():
            fw.wait(SY, hh)
        ctx.close()
        return nc

    if limit < 1:
        return finish()

    ln_ctr = [0]

    def emit_ln(src_tile_dma, dst_d, gain=None, bias=None, src_f32=False):
        ln_ctr[0] += 1
        fw.new_phase()
        tg = f"ln{ln_ctr[0]}"
        with (nc.sbuf_tensor(tg + "_x", [P, NCH, 512], F32 if src_f32 else BF16) as xt,
              nc.sbuf_tensor(tg + "_xb", [P, NCH, 512], BF16) as xb,
              nc.sbuf_tensor(tg + "_sq", [P, NCH, 512], BF16) as sqt,
              nc.sbuf_tensor(tg + "_o", [P, NCH, 512], BF16) as ot,
              nc.sbuf_tensor(tg + "_rb", [1, 512], BF16) as rbr,
              nc.sbuf_tensor(tg + "_cb", [1, 512], BF16) as cbr,
              nc.sbuf_tensor(tg + "_m", [1, 512], F32) as mr,
              nc.sbuf_tensor(tg + "_q", [1, 512], F32) as qr,
              nc.sbuf_tensor(tg + "_rr", [1, 512], F32) as rr,
              nc.sbuf_tensor(tg + "_cr", [1, 512], F32) as cr):
            xt_free = [None] * NCH
            ot_free = [None] * NCH
            sq_free = [None] * NCH
            xb_free = [None] * NCH
            row_free = None
            ps_s, ps_q, ps_r, ps_c = PS[0], PS[1], PS[2], PS[3]
            for b in range(NB):
                for tt in range(S // 512):
                    t0 = tt * 512
                    hld = []
                    for dc in range(NCH):
                        hld.append(src_tile_dma(b, dc, t0, xt[:, dc, :],
                                                (xt_free[dc],)))
                    for dc in range(NCH):
                        if src_f32:
                            fw.op(A, A.activation, xb[:, dc, :], xt[:, dc, :],
                                  AF.Identity, sem_name="act",
                                  waits=(hld[dc], xb_free[dc]))
                            src_bf = xb[:, dc, :]
                            hsrc = fw.h("act")
                        else:
                            src_bf = xt[:, dc, :]
                            hsrc = hld[dc]
                        hsq = fw.op(A, A.activation, sqt[:, dc, :], src_bf,
                                    AF.Square, sem_name="act",
                                    waits=(hsrc, sq_free[dc]))
                        fw.wait(T, ps_free[0], ps_free[1], hsrc, hsq)
                        T.matmul(ps_s[0:1, :], onesc_bf[:], src_bf,
                                 start=(dc == 0), stop=(dc == NCH - 1))
                        inst2 = T.matmul(ps_q[0:1, :], onesc_bf[:], sqt[:, dc, :],
                                         start=(dc == 0), stop=(dc == NCH - 1))
                    hmm = fw.inc(inst2, "pe")
                    for dc in range(NCH):
                        sq_free[dc] = hmm
                        xb_free[dc] = hmm
                    fw.op(V, V.tensor_scalar, mr[:], ps_s[0:1, :], 1.0 / D, None,
                          ALU.mult, sem_name="dve", waits=(hmm,))
                    fw.op(V, V.tensor_scalar, qr[:], ps_q[0:1, :], 1.0 / D, None,
                          ALU.mult, sem_name="dve")
                    ps_free[0] = ps_free[1] = fw.h("dve")
                    fw.op(V, V.tensor_mul, rr[:], mr[:], mr[:], sem_name="dve")
                    fw.op(V, V.tensor_sub, qr[:], qr[:], rr[:], sem_name="dve")
                    fw.op(A, A.activation, qr[:], qr[:], AF.Ln, bias=eps5[0:1, :],
                          sem_name="act", waits=(fw.h("dve"),))
                    fw.op(A, A.activation, rr[:], qr[:], AF.Exp, scale=-0.5,
                          sem_name="act")
                    fw.wait(V, fw.h("act"))
                    fw.op(V, V.tensor_mul, cr[:], mr[:], rr[:], sem_name="dve")
                    fw.op(V, V.tensor_scalar, cr[:], cr[:], -1.0, None, ALU.mult,
                          sem_name="dve")
                    fw.op(A, A.activation, rbr[:], rr[:], AF.Identity,
                          sem_name="act", waits=(row_free, fw.h("dve")))
                    fw.op(A, A.activation, cbr[:], cr[:], AF.Identity,
                          sem_name="act")
                    hrows = fw.h("act")
                    fw.wait(T, hrows, ps_free[2], ps_free[3])
                    T.matmul(ps_r[:], ones_bf[:], rbr[:], start=True, stop=True)
                    inst = T.matmul(ps_c[:], ones_bf[:], cbr[:],
                                    start=True, stop=True)
                    hbc = fw.inc(inst, "pe")
                    row_free = hbc
                    for dc in range(NCH):
                        fw.op(V, V.tensor_mul, xt[:, dc, :], xt[:, dc, :], ps_r[:],
                              sem_name="dve", waits=(hbc, hld[dc]))
                        if gain is None:
                            hn = fw.op(V, V.tensor_add, ot[:, dc, :], xt[:, dc, :],
                                       ps_c[:], sem_name="dve",
                                       waits=(ot_free[dc],))
                        else:
                            fw.op(V, V.tensor_add, xt[:, dc, :], xt[:, dc, :],
                                  ps_c[:], sem_name="dve")
                            hn = fw.op(V, V.tensor_scalar, ot[:, dc, :],
                                       xt[:, dc, :], gain[:, dc:dc + 1],
                                       bias[:, dc:dc + 1], ALU.mult, ALU.add,
                                       sem_name="dve", waits=(ot_free[dc],))
                        xt_free[dc] = hn
                        ot_free[dc] = fw.dma(
                            SY, dst_d[b, dc * P:(dc + 1) * P, t0:t0 + 512],
                            ot[:, dc, :], f"lno{dc}", waits=(hn,))
                    ps_free[2] = ps_free[3] = fw.h("dve")

    fw.new_phase()

    def x_tile(b, dc, t0, tile_ap, waits):
        return fw.dma(SY, tile_ap, x_in[b, dc * P:(dc + 1) * P, t0:t0 + 512],
                      f"lnx{dc}", waits=waits)

    emit_ln(x_tile, xn_d, src_f32=True)

    if limit < 2:
        return finish()

    class Rot:
        def __init__(self, idxs):
            self.idxs = list(idxs)
            self.i = 0

        def get(self):
            idx = self.idxs[self.i % len(self.idxs)]
            self.i += 1
            fw.wait(T, ps_free[idx])
            return idx

    fw.new_phase()
    ctx2 = ExitStack()
    e2 = ctx2.enter_context
    if True:
        wqTb = e2(nc.sbuf_tensor("wqTb", [P, NCH, D], BF16))
        wkTb = e2(nc.sbuf_tensor("wkTb", [P, NCH, D], BF16))
        wvTb = e2(nc.sbuf_tensor("wvTb", [P, NCH, D], BF16))
        woTb = e2(nc.sbuf_tensor("woTb", [P, NCH, D], BF16))
        bqr = e2(nc.sbuf_tensor("bqr", [1, D], BF16))
        bkr = e2(nc.sbuf_tensor("bkr", [1, D], BF16))
        bvr = e2(nc.sbuf_tensor("bvr", [1, D], BF16))
        xnf = e2(nc.sbuf_tensor("xnf", [P, NCH, S], BF16))
        kfr = e2(nc.sbuf_tensor("kfr", [P, 16, D], BF16))
        vfr = e2(nc.sbuf_tensor("vfr", [P, 16, D], BF16))
        kvall = e2(nc.sbuf_tensor("kvall", [P, 8 * 65], BF16))
        qtile = e2(nc.sbuf_tensor("qtile", [P, 2, D], BF16))
        qf2 = e2(nc.sbuf_tensor("qf2", [P, S], BF16))
        nume = e2(nc.sbuf_tensor("nume", [64, S], BF16))
        denr = e2(nc.sbuf_tensor("denr", [1, S], F32))
        rdenb = e2(nc.sbuf_tensor("rdenb", [1, S], BF16))
        cmul = e2(nc.sbuf_tensor("cmul", [64, 512], BF16))
        cct = e2(nc.sbuf_tensor("cct", [P, NCH, 512], BF16))
        stile = e2(nc.sbuf_tensor("stile", [P, 2, 512], BF16))
        for wb_, wT in [(wqTb, WqT), (wkTb, WkT), (wvTb, WvT), (woTb, WoT)]:
            fw.dma(G, wb_[:], wT.rearrange("(c p) m -> p c m", p=P), "w2l")
        for rb_, rr_ in [(bqr, bq_row), (bkr, bk_row), (bvr, bv_row)]:
            fw.dma(G, rb_[:], rr_, "w2l")
        hwld = fw.ht("w2l", "g")
        rot6 = Rot(range(6))
        qt_free = [None, None]
        kv_free = None
        qf2_free = None
        nume_free = None
        denr_free = None
        rdenb_free = None
        cmul_free = None
        cct_free = [None] * NCH
        st_free = [None, None]
        for b in range(NB):
            hxn = fw.dma(SY, xnf[:], xn_d[b].rearrange("(c p) t -> p c t", p=P),
                         "xnf", waits=(fw.h("pe"), fw.h("act")))
            for tc in range(16):
                for proj in range(3):
                    wb_ = (wqTb, wkTb, wvTb)[proj]
                    br_ = (bqr, bkr, bvr)[proj]
                    for ms in range(2):
                        psi = rot6.get()
                        fw.wait(T, hwld, hxn)
                        BIAS_MM = True
                        if BIAS_MM:
                            T.matmul(PS[psi][:], ones_bf[:],
                                     br_[0:1, ms * 512:(ms + 1) * 512],
                                     start=True, stop=False)
                        for dc in range(NCH):
                            inst = T.matmul(PS[psi][:],
                                            xnf[:, dc, tc * P:(tc + 1) * P],
                                            wb_[:, dc, ms * 512:(ms + 1) * 512],
                                            start=(dc == 0 and not BIAS_MM),
                                            stop=(dc == NCH - 1))
                        hmm = fw.inc(inst, "pe")
                        if proj == 0:
                            dst = qtile[:, tc % 2, ms * 512:(ms + 1) * 512]
                            wfree = qt_free[tc % 2]
                        elif proj == 1:
                            dst = kfr[:, tc, ms * 512:(ms + 1) * 512]
                            wfree = None
                        else:
                            dst = vfr[:, tc, ms * 512:(ms + 1) * 512]
                            wfree = None
                        if proj < 2:
                            hev = fw.op(A, A.activation, dst, PS[psi][:], AF.Relu,
                                        scale=dec_s[:, tc:tc + 1], sem_name="act",
                                        waits=(hmm, wfree))
                        else:
                            hev = fw.op(A, A.activation, dst, PS[psi][:],
                                        AF.Identity, sem_name="act",
                                        waits=(hmm, wfree))
                        ps_free[psi] = hev
                qt_free[tc % 2] = fw.dma(
                    SY, qf_d[b, tc * P:(tc + 1) * P, :], qtile[:, tc % 2, :],
                    f"qt{tc % 2}", waits=(fw.h("act"),))
            if p2_stop:
                continue
            p2a_done = fw.h("act")
            for h in range(16):
                psi = rot6.get()
                fw.wait(T, kv_free, p2a_done)
                for tc in range(16):
                    T.matmul(PS[psi][0:64, 0:64],
                             kfr[:, tc, h * 64:h * 64 + 64],
                             vfr[:, tc, h * 64:h * 64 + 64],
                             start=(tc == 0), stop=(tc == 15))
                for tc in range(16):
                    inst = T.matmul(PS[psi][0:64, 64:65],
                                    kfr[:, tc, h * 64:h * 64 + 64],
                                    onesc_bf[:], start=(tc == 0), stop=(tc == 15),
                                    skip_group_check=True)
                hmm = fw.inc(inst, "pe")
                rb0 = (h % 2) * 64
                hev = fw.op(A, A.activation,
                            kvall[rb0:rb0 + 64, (h // 2) * 65:(h // 2 + 1) * 65],
                            PS[psi][0:64, 0:65], AF.Identity, sem_name="act",
                            waits=(hmm,))
                ps_free[psi] = hev
            kv_done = fw.h("act")
            for hp in range(NCH):
                hqf = fw.dmat(SY, qf2[:], qf_d[b][:, hp * P:(hp + 1) * P], "qf2",
                              waits=(qf2_free, qt_free[0], qt_free[1]))
                for head in range(2):
                    hq = head * 64
                    hh = hp * 2 + head
                    pss = [rot6.get() for _ in range(2)]
                    for sl in range(4):
                        psi = pss[sl % 2]
                        if sl >= 2:
                            fw.wait(T, ps_free[psi])
                        fw.wait(T, kv_done, hqf)
                        inst = T.matmul(PS[psi][0:65, :],
                                        kvall[hq:hq + 64, hp * 65:(hp + 1) * 65],
                                        qf2[hq:hq + 64, sl * 512:(sl + 1) * 512],
                                        start=True, stop=True)
                        hmm = fw.inc(inst, "pe")
                        fw.op(A, A.activation, denr[0:1, sl * 512:(sl + 1) * 512],
                              PS[psi][64:65, :], AF.Identity, bias=eps6[0:1, :],
                              sem_name="act", waits=(hmm, denr_free))
                        hev = fw.op(A, A.activation,
                                    nume[:, sl * 512:(sl + 1) * 512],
                                    PS[psi][0:64, :], AF.Identity,
                                    sem_name="act", waits=(nume_free,))
                        ps_free[psi] = hev
                    hrec = fw.op(V, V.reciprocal, denr[:], denr[:],
                                 sem_name="dve", waits=(fw.h("act"),))
                    denr_free = hrec
                    hrd = fw.op(A, A.activation, rdenb[:], denr[:], AF.Identity,
                                sem_name="act", waits=(hrec, rdenb_free))
                    for sl in range(4):
                        psi = rot6.get()
                        fw.wait(T, hrd)
                        inst = T.matmul(PS[psi][0:64, :], ones_bf[0:1, 0:64],
                                        rdenb[0:1, sl * 512:(sl + 1) * 512],
                                        start=True, stop=True)
                        hmm = fw.inc(inst, "pe")
                        rdenb_free = hmm
                        hmul = fw.op(V, V.tensor_mul, cmul[:],
                                     nume[:, sl * 512:(sl + 1) * 512],
                                     PS[psi][0:64, :], sem_name="dve",
                                     waits=(hmm, cmul_free))
                        ps_free[psi] = hmul
                        nume_free = hmul
                        cmul_free = fw.dma(
                            SY, scr["ccos_d"][b, hp * P + hq:hp * P + hq + 64,
                                              sl * 512:(sl + 1) * 512],
                            cmul[:], "cm", waits=(hmul,))
                qf2_free = fw.h("pe")
            cc_done = fw.ht("cm")
            for ts in range(4):
                hcc = []
                for hc in range(NCH):
                    hcc.append(fw.dma(
                        SY, cct[:, hc, :],
                        scr["ccos_d"][b, hc * P:(hc + 1) * P,
                                      ts * 512:(ts + 1) * 512],
                        f"cct{hc}", waits=(cc_done, cct_free[hc])))
                for dc in range(NCH):
                    psi = rot6.get()
                    fw.wait(T, *hcc)
                    for hc in range(NCH):
                        inst = T.matmul(PS[psi][:], woTb[:, hc, dc * P:(dc + 1) * P],
                                        cct[:, hc, :], start=(hc == 0),
                                        stop=(hc == NCH - 1))
                    hmm = fw.inc(inst, "pe")
                    sb = dc % 2
                    hev = fw.op(A, A.activation, stile[:, sb, :], PS[psi][:],
                                AF.Identity, bias=pcs["bo_pc"][:, dc:dc + 1],
                                sem_name="act", waits=(hmm, st_free[sb]))
                    ps_free[psi] = hev
                    st_free[sb] = fw.dma(
                        SY, s_d[b, dc * P:(dc + 1) * P, ts * 512:(ts + 1) * 512],
                        stile[:, sb, :], f"sst{sb}", waits=(hev,))
                for hc in range(NCH):
                    cct_free[hc] = fw.h("pe")

    ctx2.close()
    fw.new_phase()

    if limit < 3:
        return finish()

    def xn_tile(b, dc, t0, tile_ap, waits):
        return fw.dma(SY, tile_ap, xn_d[b, dc * P:(dc + 1) * P, t0:t0 + 512],
                      f"lnx{dc}", waits=waits)

    fw.new_phase()
    emit_ln(xn_tile, x1_d, gain=pcs["ln1g_pc"], bias=pcs["ln1b_pc"])

    fw.new_phase()
    ctx3 = ExitStack()
    e3 = ctx3.enter_context
    if True:
        wqb3 = e3(nc.sbuf_tensor("wqb3", [P, NCH, D], BF16))
        wkb3 = e3(nc.sbuf_tensor("wkb3", [P, NCH, D], BF16))
        wvb3 = e3(nc.sbuf_tensor("wvb3", [P, NCH, D], BF16))
        x1f = e3(nc.sbuf_tensor("x1f", [P, NCH, S], BF16))
        qmt = e3(nc.sbuf_tensor("qmt", [P, 2, S], BF16))
        vt3 = e3(nc.sbuf_tensor("vt3", [P, 2, D], BF16))
        for wb_, win in [(wqb3, fwq), (wkb3, fwk), (wvb3, fwv)]:
            fw.dma(G, wb_[:], win.rearrange("(c p) m -> p c m", p=P), "w3l")
        hwld3 = fw.ht("w3l", "g")
        rot = Rot(range(6))
        qm_free = [None, None]
        vt_free = [None, None]
        for b in range(NB):
            hx1 = fw.dma(SY, x1f[:], x1_d[b].rearrange("(c p) t -> p c t", p=P),
                         "x1f", waits=(fw.h("pe"),))
            for wb_, dst_d in [(wqb3, qa_d), (wkb3, ka_d)]:
                for mc in range(NCH):
                    for ts in range(4):
                        psi = rot.get()
                        fw.wait(T, hwld3, hx1)
                        for dc in range(NCH):
                            inst = T.matmul(PS[psi][:],
                                            wb_[:, dc, mc * P:(mc + 1) * P],
                                            x1f[:, dc, ts * 512:(ts + 1) * 512],
                                            start=(dc == 0), stop=(dc == NCH - 1))
                        hmm = fw.inc(inst, "pe")
                        hev = fw.op(A, A.activation,
                                    qmt[:, mc % 2, ts * 512:(ts + 1) * 512],
                                    PS[psi][:], AF.Identity, sem_name="act",
                                    waits=(hmm, qm_free[mc % 2]))
                        ps_free[psi] = hev
                    qm_free[mc % 2] = fw.dma(
                        SY, dst_d[b, mc * P:(mc + 1) * P, :], qmt[:, mc % 2, :],
                        f"qm{mc % 2}", waits=(fw.h("act"),))
            for tc in range(16):
                for hp in range(NCH):
                    psi = rot.get()
                    fw.wait(T, hwld3, hx1)
                    for dc in range(NCH):
                        inst = T.matmul(PS[psi][0:P, 0:P],
                                        x1f[:, dc, tc * P:(tc + 1) * P],
                                        wvb3[:, dc, hp * P:(hp + 1) * P],
                                        start=(dc == 0), stop=(dc == NCH - 1))
                    hmm = fw.inc(inst, "pe")
                    hev = fw.op(A, A.activation, vt3[:, tc % 2, hp * P:(hp + 1) * P],
                                PS[psi][0:P, 0:P], AF.Identity, sem_name="act",
                                waits=(hmm, vt_free[tc % 2]))
                    ps_free[psi] = hev
                for hp in range(NCH):
                    fw.dma(SY, v_d[b, hp, tc * P:(tc + 1) * P, :],
                           vt3[:, tc % 2, hp * P:(hp + 1) * P], f"vt{tc % 2}",
                           waits=(fw.h("act"),))
                vt_free[tc % 2] = fw.ht(f"vt{tc % 2}")
    ctx3.close()

    fw.new_phase()
    ctx4 = ExitStack()
    e4 = ctx4.enter_context
    if True:
        q2 = e4(nc.sbuf_tensor("q2", [P, S], BF16))
        k2 = e4(nc.sbuf_tensor("k2", [P, S], BF16))
        v2 = e4(nc.sbuf_tensor("v2", [P, 16, P], BF16))
        p2sl = e4(nc.sbuf_tensor("p2sl", [P, PP + 1], BF16))
        sqb = e4(nc.sbuf_tensor("sqb", [P, S], BF16))
        S_f = e4(nc.sbuf_tensor("S_f", [P, S], F32))
        ful = e4(nc.sbuf_tensor("ful", [P, 2176], BF16))
        Ptile = e4(nc.sbuf_tensor("Ptile", [P, S], BF16))
        denall = e4(nc.sbuf_tensor("denall", [P, 16], F32))
        den16 = e4(nc.sbuf_tensor("den16", [16, P], F32))
        rdenr = e4(nc.sbuf_tensor("rdenr", [1, S], F32))
        rdenb2 = e4(nc.sbuf_tensor("rdenb2", [1, S], BF16))
        negBc = e4(nc.sbuf_tensor("negBc", [P, 1], F32))
        bnds = e4(nc.sbuf_tensor("bnds", [1, 8], F32))
        nrow = e4(nc.sbuf_tensor("nrow", [1, S], F32))
        PT = [e4(nc.sbuf_tensor(f"PT{i}", [P, S], BF16)) for i in range(2)]
        zsl = e4(nc.sbuf_tensor("zsl", [64, 512], BF16))
        zmul = e4(nc.sbuf_tensor("zmul", [64, 512], BF16))
        bd_t = scr["bd_d"].tensor
        den_t = scr["den_d"].tensor
        rot = Rot(range(4))
        ld_free = None
        denall_free = None
        den16_free = None
        rdenr_free = None
        rdenb2_free = None
        nrow_free = None
        bnds_free = None
        ful_free = None
        ring_free = [None] * 4
        Ptile_free = None
        PT_free = [None, None]
        zsl_free = None
        zmul_free = None
        pd_free = [None, None]
        pd = 0
        for b in range(NB):
            for hp in range(NCH):
                hq2 = fw.dma(SY, q2[:], qa_d[b, hp * P:(hp + 1) * P, :], "q2",
                             waits=(ld_free,))
                hk2 = fw.dma(SY, k2[:], ka_d[b, hp * P:(hp + 1) * P, :], "k2")
                hv2 = fw.dma(SY, v2[:],
                             v_d[b, hp].rearrange("(c p) f -> p c f", p=P), "v2")
                hp2 = fw.dma(SY, p2sl[:], p2_d[hp * P:(hp + 1) * P, :], "p2l")
                for src_, col in [(q2, 0), (k2, 1)]:
                    hsq = fw.op(A, A.activation, sqb[:], src_[:], AF.Square,
                                sem_name="act",
                                waits=(hq2 if col == 0 else hk2, fw.h("pe")))
                    for sl in range(4):
                        psi = rot.get()
                        fw.wait(T, hsq)
                        inst = T.matmul(PS[psi][0:1, 0:512], onesc_bf[:],
                                        sqb[:, sl * 512:(sl + 1) * 512],
                                        start=True, stop=True)
                        hmm = fw.inc(inst, "pe")
                        hev = fw.op(A, A.activation,
                                    nrow[0:1, sl * 512:(sl + 1) * 512],
                                    PS[psi][0:1, 0:512], AF.Identity,
                                    sem_name="act", waits=(hmm, nrow_free))
                        ps_free[psi] = hev
                    nrow_free = fw.op(V, V.reduce_max, bnds[0:1, col:col + 1],
                                      nrow[:], axis=AX.X, sem_name="dve",
                                      waits=(fw.h("act"), bnds_free))
                fw.op(A, A.activation, bnds[0:1, 3:4], bnds[0:1, 0:1], AF.Sqrt,
                      sem_name="act", waits=(fw.h("dve"),))
                fw.op(A, A.activation, bnds[0:1, 4:5], bnds[0:1, 1:2], AF.Sqrt,
                      sem_name="act")
                fw.op(A, A.activation, bnds[0:1, 5:6], maxp2[0:1, hp:hp + 1],
                      AF.Sqrt, sem_name="act")
                fw.op(V, V.tensor_add, bnds[0:1, 6:7], bnds[0:1, 4:5],
                      bnds[0:1, 5:6], sem_name="dve", waits=(fw.h("act"),))
                fw.op(V, V.tensor_mul, bnds[0:1, 6:7], bnds[0:1, 6:7],
                      bnds[0:1, 3:4], sem_name="dve")
                hB = fw.op(V, V.tensor_scalar, bnds[0:1, 7:8], bnds[0:1, 6:7],
                           -0.125, None, ALU.mult, sem_name="dve")
                psi = rot.get()
                fw.wait(T, hB)
                inst = T.matmul(PS[psi][0:P, 0:1], ones_f[:], bnds[0:1, 7:8],
                                start=True, stop=True)
                hmm = fw.inc(inst, "pe")
                bnds_free = hmm
                hnB = fw.op(A, A.activation, negBc[:], PS[psi][0:P, 0:1],
                            AF.Identity, sem_name="act", waits=(hmm,))
                ps_free[psi] = hnB
                for head in range(2):
                    hq = head * 64
                    for it in range(16):
                        base = (S - P) - it * P
                        off = 0
                        for w_ in (512, 512, 512, 512, 128):
                            psi = rot.get()
                            fw.wait(T, hq2, hp2)
                            inst = T.matmul(PS[psi][0:P, 0:w_],
                                            q2[hq:hq + 64, it * P:(it + 1) * P],
                                            p2sl[hq:hq + 64,
                                                 base + off:base + off + w_],
                                            start=True, stop=True)
                            hmm = fw.inc(inst, "pe")
                            hev = fw.op(A, A.activation, ful[:, off:off + w_],
                                        PS[psi][0:P, 0:w_], AF.Identity,
                                        sem_name="act", waits=(hmm, ful_free))
                            ps_free[psi] = hev
                            off += w_
                        rg = it % 4
                        hfs = fw.dma(SY, scr["bd_d"][rg], ful[:], "bds",
                                     waits=(fw.h("act"), ring_free[rg]))
                        ful_free = hfs
                        for sl in range(4):
                            psi = rot.get()
                            fw.wait(T, hk2)
                            inst = T.matmul(PS[psi][0:P, 0:512],
                                            q2[hq:hq + 64, it * P:(it + 1) * P],
                                            k2[hq:hq + 64, sl * 512:(sl + 1) * 512],
                                            start=True, stop=True)
                            hmm = fw.inc(inst, "pe")
                            hev = fw.op(A, A.activation,
                                        S_f[:, sl * 512:(sl + 1) * 512],
                                        PS[psi][0:P, 0:512], AF.Identity,
                                        sem_name="act", waits=(hmm,))
                            ps_free[psi] = hev
                        shift_ap = AP(bd_t, rg * P * 2176 + 127,
                                      [[2175, P], [1, S]])
                        hacc = fw.dma(G, S_f[:], shift_ap, "acc",
                                      waits=(hfs, fw.h("act")),
                                      accum_op=ALU.add)
                        ring_free[rg] = hacc
                        hexp = fw.op(A, A.activation, Ptile[:], S_f[:], AF.Exp,
                                     bias=negBc[:], scale=0.125,
                                     accum_out=denall[:, it:it + 1],
                                     sem_name="act",
                                     waits=(hacc, hnB, Ptile_free, denall_free))
                        Ptile_free = fw.dma(SY, P_d[pd, it * P:(it + 1) * P, :],
                                            Ptile[:], "Pst",
                                            waits=(hexp, pd_free[pd]))
                    psi = rot.get()
                    fw.wait(T, fw.h("act"))
                    inst = T.transpose(PS[psi][0:16, 0:P], denall[:], eyef_s[:])
                    hmm = fw.inc(inst, "pe")
                    denall_free = hmm
                    hd16 = fw.op(A, A.activation, den16[:], PS[psi][0:16, 0:P],
                                 AF.Identity, sem_name="act",
                                 waits=(hmm, den16_free))
                    ps_free[psi] = hd16
                    den16_free = fw.dma(SY, den_d[:], den16[:], "dns",
                                        waits=(hd16,))
                    flat_ap = AP(den_t, 0, [[S, 1], [1, S]])
                    hdf = fw.dma(SY, rdenr[:], flat_ap, "dnl",
                                 waits=(rdenr_free, den16_free))
                    hrc = fw.op(V, V.reciprocal, rdenr[:], rdenr[:],
                                sem_name="dve", waits=(hdf,))
                    hrd = fw.op(A, A.activation, rdenb2[:], rdenr[:], AF.Identity,
                                sem_name="act", waits=(hrc, rdenb2_free))
                    rdenr_free = hrd
                    for jc in range(16):
                        pb = jc % 2
                        hpt = fw.dmat(SY, PT[pb][:], P_d[pd, 0:S, jc * P:(jc + 1) * P],
                                      f"pt{pb}", waits=(PT_free[pb], Ptile_free))
                        fw.wait(T, hpt, hv2)
                        if jc == 0:
                            fw.wait(T, ps_free[4], ps_free[5], ps_free[6],
                                    ps_free[7])
                        for isl in range(4):
                            inst = T.matmul(PS[4 + isl][0:64, 0:512],
                                            v2[:, jc, hq:hq + 64],
                                            PT[pb][:, isl * 512:(isl + 1) * 512],
                                            start=(jc == 0), stop=(jc == 15))
                        PT_free[pb] = fw.inc(inst, "pe")
                    hV = fw.h("pe")
                    for isl in range(4):
                        psi = rot.get()
                        fw.wait(T, hrd)
                        inst = T.matmul(PS[psi][0:64, 0:512], ones_bf[0:1, 0:64],
                                        rdenb2[0:1, isl * 512:(isl + 1) * 512],
                                        start=True, stop=True)
                        hmm = fw.inc(inst, "pe")
                        rdenb2_free = hmm
                        hz = fw.op(A, A.activation, zsl[:],
                                   PS[4 + isl][0:64, 0:512], AF.Identity,
                                   sem_name="act", waits=(hV, zsl_free))
                        ps_free[4 + isl] = hz
                        hmul = fw.op(V, V.tensor_mul, zmul[:], zsl[:],
                                     PS[psi][0:64, 0:512], sem_name="dve",
                                     waits=(hmm, hz, zmul_free))
                        ps_free[psi] = hmul
                        zsl_free = hmul
                        zmul_free = fw.dma(
                            SY, scr["ccos_d"][b, hp * P + hq:hp * P + hq + 64,
                                              isl * 512:(isl + 1) * 512],
                            zmul[:], "zst", waits=(hmul,))
                    pd_free[pd] = hV
                    pd = 1 - pd
                ld_free = fw.h("pe")
    ctx4.close()

    fw.new_phase()
    ctx5 = ExitStack()
    e5 = ctx5.enter_context
    if True:
        wob = e5(nc.sbuf_tensor("wob", [P, NCH, D], BF16))
        cc3 = e5(nc.sbuf_tensor("cc3", [P, NCH, 512], BF16))
        xnt3 = e5(nc.sbuf_tensor("xnt3", [P, 2, 512], BF16))
        srct = e5(nc.sbuf_tensor("srct", [P, 2, 512], BF16))
        hwo = fw.dma(G, wob[:], fwo.rearrange("(c p) m -> p c m", p=P), "wol")
        rot = Rot(range(6))
        cc3_free = [None] * NCH
        xnt_free = [None, None]
        srct_free = [None, None]
        for b in range(NB):
            for ts in range(4):
                hcc = []
                for hc in range(NCH):
                    hcc.append(fw.dma(
                        SY, cc3[:, hc, :],
                        scr["ccos_d"][b, hc * P:(hc + 1) * P,
                                      ts * 512:(ts + 1) * 512],
                        f"cc{hc}", waits=(cc3_free[hc],)))
                for dc in range(NCH):
                    psi = rot.get()
                    fw.wait(T, hwo, *hcc)
                    for hc in range(NCH):
                        inst = T.matmul(PS[psi][:], wob[:, hc, dc * P:(dc + 1) * P],
                                        cc3[:, hc, :], start=(hc == 0),
                                        stop=(hc == NCH - 1))
                    hmm = fw.inc(inst, "pe")
                    sb = dc % 2
                    hxnt = fw.dma(SY, xnt3[:, sb, :],
                                  xn_d[b, dc * P:(dc + 1) * P,
                                       ts * 512:(ts + 1) * 512],
                                  f"xt{sb}", waits=(xnt_free[sb],))
                    hadd = fw.op(V, V.tensor_add, srct[:, sb, :], xnt3[:, sb, :],
                                 PS[psi][:], sem_name="dve",
                                 waits=(hmm, hxnt, srct_free[sb]))
                    ps_free[psi] = hadd
                    xnt_free[sb] = hadd
                    srct_free[sb] = fw.dma(
                        SY, src2_d[b, dc * P:(dc + 1) * P, ts * 512:(ts + 1) * 512],
                        srct[:, sb, :], f"sr{sb}", waits=(hadd,))
                for hc in range(NCH):
                    cc3_free[hc] = fw.h("pe")
    ctx5.close()

    def src2_tile(b, dc, t0, tile_ap, waits):
        return fw.dma(SY, tile_ap, src2_d[b, dc * P:(dc + 1) * P, t0:t0 + 512],
                      f"lnx{dc}", waits=waits)

    fw.new_phase()
    emit_ln(src2_tile, x2_d, gain=pcs["ln2g_pc"], bias=pcs["ln2b_pc"])

    fw.new_phase()
    ctx6 = ExitStack()
    e6 = ctx6.enter_context
    if True:
        x2f = e6(nc.sbuf_tensor("x2f", [P, NCH, S], BF16))
        selb = e6(nc.sbuf_tensor("selb", [P, NCH, 32], BF16))
        selt = e6(nc.sbuf_tensor("selt", [P, 32], F32))
        curt = e6(nc.sbuf_tensor("curt", [P, 32], F32))
        mc1 = e6(nc.sbuf_tensor("mc1", [P, 1], F32))
        gatesV = e6(nc.sbuf_tensor("gatesV", [P, 32], F32))
        gatesA = e6(nc.sbuf_tensor("gatesA", [P, 32], F32))
        gall = e6(nc.sbuf_tensor("gall", [32, S], F32))
        grow = e6(nc.sbuf_tensor("grow", [1, 32 * S], BF16))
        keyb = [e6(nc.sbuf_tensor(f"keyb{i}", [P, NCH, P], BF16)) for i in range(2)]
        valb = [e6(nc.sbuf_tensor(f"valb{i}", [P, D], BF16)) for i in range(2)]
        hrb = e6(nc.sbuf_tensor("hrb", [P, 2, 256], BF16))
        hgb = e6(nc.sbuf_tensor("hgb", [P, 2, 256], BF16))
        hgl = e6(nc.sbuf_tensor("hgl", [P, 2, 256], BF16))
        s2t = e6(nc.sbuf_tensor("s2t", [P, 2, 256], BF16))
        sst = e6(nc.sbuf_tensor("sst", [P, 2, 256], BF16))
        xft = e6(nc.sbuf_tensor("xft", [P, 2, 256], F32))
        ut = e6(nc.sbuf_tensor("ut", [P, 2, 256], F32))
        gt = e6(nc.sbuf_tensor("gt", [P, 2, 256], BF16))
        t1t = e6(nc.sbuf_tensor("t1t", [P, 2, 256], F32))
        xot = e6(nc.sbuf_tensor("xot", [P, 2, 256], F32))
        hsel = fw.dma(G, selb[:], sel_w.rearrange("(c p) e -> p c e", p=P), "sel")
        rot = Rot(range(3))
        selt_free = None
        gall_free = None
        key_free = [None, None]
        hload = [None, None]
        out_free = [None] * NCH
        cmb_free = [[None, None] for _ in range(7)]
        for b in range(NB):
            hx2 = fw.dma(SY, x2f[:], x2_d[b].rearrange("(c p) t -> p c t", p=P),
                         "x2f", waits=(fw.h("pe"),))
            for tc in range(16):
                psi = rot.get()
                fw.wait(T, hsel, hx2)
                for dc in range(NCH):
                    inst = T.matmul(PS[psi][0:P, 0:32],
                                    x2f[:, dc, tc * P:(tc + 1) * P],
                                    selb[:, dc, :], start=(dc == 0),
                                    stop=(dc == NCH - 1))
                hmm = fw.inc(inst, "pe")
                fw.op(A, A.activation, selt[:], PS[psi][0:P, 0:32], AF.Sigmoid,
                      sem_name="act", waits=(hmm, selt_free))
                fw.op(V, V.tensor_copy, curt[:], selt[:], sem_name="dve",
                      waits=(fw.h("act"),))
                for _ in range(3):
                    fw.op(V, V.reduce_max, mc1[:], curt[:], axis=AX.X,
                          sem_name="dve")
                    fw.op(V, V.scalar_tensor_tensor, curt[:], curt[:], mc1[:],
                          curt[:], ALU.is_lt, ALU.mult, sem_name="dve")
                fw.op(V, V.reduce_max, mc1[:], curt[:], axis=AX.X, sem_name="dve")
                fw.op(V, V.scalar_tensor_tensor, gatesV[:], selt[:], mc1[:],
                      selt[:], ALU.is_ge, ALU.mult, sem_name="dve")
                selt_free = fw.h("dve")
                hga = fw.op(A, A.activation, gatesA[:], gatesV[:], AF.Identity,
                            sem_name="act", waits=(selt_free,))
                ps_free[psi] = hga
                psi2 = rot.get()
                fw.wait(T, hga)
                inst = T.transpose(PS[psi2][0:32, 0:P], gatesA[:], eyef_s[:])
                hmm2 = fw.inc(inst, "pe")
                hgl_ = fw.op(A, A.activation, gall[:, tc * P:(tc + 1) * P],
                             PS[psi2][0:32, 0:P], AF.Identity, sem_name="act",
                             waits=(hmm2, gall_free))
                ps_free[psi2] = hgl_
            hgst = fw.dma(SY, gates_d[b], gall[:], "gst", waits=(fw.h("act"),))
            gall_free = hgst
            for e in range(32):
                fw.dma(G, grow[0:1, e * S:(e + 1) * S], gates_d[b, e:e + 1, :],
                       "grw", waits=(hgst, fw.h("pe")))
            hgrow = fw.ht("grw", "g")
            for tt in range(NB * 4):
                to = tt * 256
                for e in range(32):
                    r = e % 2
                    if e == 0 and tt == 0 and b == 0:
                        hk0 = fw.dma(G, keyb[0][:],
                                     keys[0].rearrange("(c p) f -> p c f", p=P),
                                     "ky0")
                        hload[0] = (hk0, fw.dma(G, valb[0][:], values[0], "vl0"))
                    if not (e == 31 and tt == NB * 4 - 1 and b == NB - 1):
                        nxt = (e + 1) % 32
                        rn = nxt % 2
                        fw.wait(G, key_free[rn])
                        hkn = fw.dma(G, keyb[rn][:],
                                     keys[nxt].rearrange("(c p) f -> p c f", p=P),
                                     f"ky{rn}")
                        hload[rn] = (hkn,
                                     fw.dma(G, valb[rn][:], values[nxt], f"vl{rn}"))
                    fw.wait(T, hgrow, ps_free[3])
                    inst = T.matmul(PS[3][0:P, 0:256], ones_bf[:],
                                    grow[0:1, e * S + to:e * S + to + 256],
                                    start=True, stop=True)
                    hgb_ = fw.inc(inst, "pe")
                    psi_h = e % 2
                    fw.wait(T, ps_free[psi_h], hx2, *hload[r])
                    for dc in range(NCH):
                        inst = T.matmul(PS[psi_h][0:P, 0:256],
                                        keyb[r][:, dc, :],
                                        x2f[:, dc, to:to + 256],
                                        start=(dc == 0), stop=(dc == NCH - 1))
                    hmm = fw.inc(inst, "pe")
                    hr_ = fw.op(A, A.activation, hrb[:, e % 2, :],
                                PS[psi_h][0:P, 0:256], AF.Relu, sem_name="act",
                                waits=(hmm,))
                    ps_free[psi_h] = hr_
                    hg_ = fw.op(V, V.tensor_mul, hgb[:, e % 2, :],
                                hrb[:, e % 2, :], PS[3][0:P, 0:256],
                                sem_name="dve", waits=(hgb_, hr_))
                    ps_free[3] = hg_
                    hl_ = fw.op(A, A.activation, hgl[:, e % 2, :],
                                hgb[:, e % 2, :], AF.Identity, sem_name="act",
                                waits=(hg_,))
                    fw.wait(T, hl_)
                    if e == 0:
                        fw.wait(T, *[out_free[i] for i in range(NCH)])
                    for dc in range(NCH):
                        pt = 4 + dc % 4
                        rg0 = (dc // 4) * 256
                        inst = T.matmul(PS[pt][0:P, rg0:rg0 + 256],
                                        valb[r][:, dc * P:(dc + 1) * P],
                                        hgl[:, e % 2, :],
                                        start=(e == 0 and dc < 4),
                                        stop=(e == 31 and dc >= 4),
                                        skip_group_check=True)
                    key_free[r] = fw.inc(inst, "pe")
                hMoE = fw.h("pe")
                for dc in range(NCH):
                    pt = 4 + dc % 4
                    rg0 = (dc // 4) * 256
                    cb = dc % 2
                    h1 = fw.dma(SY, s2t[:, cb, :],
                                src2_d[b, dc * P:(dc + 1) * P, to:to + 256],
                                f"cb1{cb}", waits=(cmb_free[0][cb],))
                    h2 = fw.dma(SY, sst[:, cb, :],
                                s_d[b, dc * P:(dc + 1) * P, to:to + 256],
                                f"cb2{cb}", waits=(cmb_free[1][cb],))
                    h3 = fw.dma(SY, xft[:, cb, :],
                                x_in[b, dc * P:(dc + 1) * P, to:to + 256],
                                f"cb3{cb}", waits=(cmb_free[2][cb],))
                    hu = fw.op(V, V.tensor_add, ut[:, cb, :], s2t[:, cb, :],
                               PS[pt][0:P, rg0:rg0 + 256], sem_name="dve",
                               waits=(hMoE, h1, cmb_free[3][cb]))
                    out_free[dc] = hu
                    hgsl = fw.op(A, A.activation, gt[:, cb, :], ut[:, cb, :],
                                 AF.Silu, sem_name="act",
                                 waits=(hu, cmb_free[4][cb]))
                    ht1 = fw.op(V, V.tensor_mul, t1t[:, cb, :], sst[:, cb, :],
                                gt[:, cb, :], sem_name="dve",
                                waits=(hgsl, h2))
                    hxo = fw.op(V, V.tensor_add, xot[:, cb, :], xft[:, cb, :],
                                t1t[:, cb, :], sem_name="dve",
                                waits=(h3, cmb_free[5][cb], cmb_free[6][cb]))
                    hst1 = fw.dma(SY, xnew_d[b, dc * P:(dc + 1) * P, to:to + 256],
                                  xot[:, cb, :], f"xs{cb}", waits=(hxo,))
                    hst2 = fw.dma(G, xnewb_d[b, dc * P:(dc + 1) * P, to:to + 256],
                                  xot[:, cb, :], f"xp{cb}", waits=(hxo,))
                    cmb_free[0][cb] = hu
                    cmb_free[1][cb] = ht1
                    cmb_free[2][cb] = hxo
                    cmb_free[3][cb] = hgsl
                    cmb_free[4][cb] = ht1
                    cmb_free[5][cb] = hst2
                    cmb_free[6][cb] = hst1
    ctx6.close()


    if limit < 4:
        return finish()

    def xnb_tile(b, dc, t0, tile_ap, waits):
        return fw.dma(SY, tile_ap, xnewb_d[b, dc * P:(dc + 1) * P, t0:t0 + 512],
                      f"lnx{dc}", waits=waits)

    emit_ln(xnb_tile, xn1_d)

    fw.new_phase()
    ctx7 = ExitStack()
    e7 = ctx7.enter_context
    if True:
        aqb = [e7(nc.sbuf_tensor(f"aqb{i}", [P, NCH, 64], BF16)) for i in range(3)]
        brow4 = e7(nc.sbuf_tensor("brow4", [1, 3 * 64], BF16))
        awob = e7(nc.sbuf_tensor("awob", [64, D], BF16))
        xn1f = e7(nc.sbuf_tensor("xn1f", [P, NCH, S], BF16))
        qkvf = e7(nc.sbuf_tensor("qkvf", [P, 192], F32))
        v4 = e7(nc.sbuf_tensor("v4", [P, 16, 64], BF16))
        qfm4 = e7(nc.sbuf_tensor("qfm4", [64, S], BF16))
        kfm4 = e7(nc.sbuf_tensor("kfm4", [64, S], BF16))
        S4 = e7(nc.sbuf_tensor("S4", [P, S], F32))
        m4r = e7(nc.sbuf_tensor("m4r", [P, 1], F32))
        nm4 = e7(nc.sbuf_tensor("nm4", [P, 1], F32))
        Pt4 = e7(nc.sbuf_tensor("Pt4", [P, S], BF16))
        dal4 = e7(nc.sbuf_tensor("dal4", [P, 16], F32))
        d164 = e7(nc.sbuf_tensor("d164", [16, P], F32))
        rdr4 = e7(nc.sbuf_tensor("rdr4", [1, S], F32))
        rdb4 = e7(nc.sbuf_tensor("rdb4", [1, S], BF16))
        PT4 = [e7(nc.sbuf_tensor(f"PT4{i}", [P, S], BF16)) for i in range(2)]
        zsl4 = e7(nc.sbuf_tensor("zsl4", [64, 512], BF16))
        zmul4 = e7(nc.sbuf_tensor("zmul4", [64, 512], BF16))
        zln4 = e7(nc.sbuf_tensor("zln4", [64, 512], BF16))
        s1t = e7(nc.sbuf_tensor("s1t", [P, 2, 512], BF16))
        for i, (win, brs) in enumerate([(a1wq, a1bq_row), (a1wk, a1bk_row),
                                        (a1wv, a1bv_row)]):
            fw.dma(G, aqb[i][:], win.rearrange("(c p) m -> p c m", p=P), "a4l")
            fw.dma(G, brow4[0:1, i * 64:(i + 1) * 64], brs, "a4l")
        fw.dma(G, awob[:], a1wo, "a4l")
        ha4 = fw.ht("a4l", "g")
        rot = Rot(range(4))
        den_t4 = scr["den_d"].tensor
        pd = 0
        pd_free = [None, None]
        Pt4_free = None
        PT4_free = [None, None]
        dal4_free = None
        d164_free = None
        rdr4_free = None
        rdb4_free = None
        zs4_free = None
        zm4_free = None
        zl4_free = None
        s1t_free = [None, None]
        xn1f_free = None
        for b in range(NB):
            hx4 = fw.dma(SY, xn1f[:], xn1_d[b].rearrange("(c p) t -> p c t", p=P),
                         "xn1f", waits=(xn1f_free,))
            for tc in range(16):
                psi = rot.get()
                fw.wait(T, hx4, ha4)
                for proj in range(3):
                    T.matmul(PS[psi][0:P, proj * 64:(proj + 1) * 64], ones_bf[:],
                             brow4[0:1, proj * 64:(proj + 1) * 64],
                             start=True, stop=False, skip_group_check=True)
                    for dc in range(NCH):
                        inst = T.matmul(PS[psi][0:P, proj * 64:(proj + 1) * 64],
                                        xn1f[:, dc, tc * P:(tc + 1) * P],
                                        aqb[proj][:, dc, :], start=False,
                                        stop=(dc == NCH - 1),
                                        skip_group_check=True)
                hmm = fw.inc(inst, "pe")
                hqk = fw.op(A, A.activation, qkvf[:], PS[psi][0:P, 0:192],
                            AF.Identity, sem_name="act", waits=(hmm, fw.h("pe")))
                hv4 = fw.op(A, A.activation, v4[:, tc, :], PS[psi][0:P, 128:192],
                            AF.Identity, sem_name="act")
                ps_free[psi] = hv4
                for qk in range(2):
                    psi2 = rot.get()
                    fw.wait(T, hqk)
                    inst = T.transpose(PS[psi2][0:64, 0:P],
                                       qkvf[:, qk * 64:(qk + 1) * 64], eyef_s[:])
                    hmm2 = fw.inc(inst, "pe")
                    dst = (qfm4 if qk == 0 else kfm4)[:, tc * P:(tc + 1) * P]
                    hev = fw.op(A, A.activation, dst, PS[psi2][0:64, 0:P],
                                AF.Identity, sem_name="act", waits=(hmm2,))
                    ps_free[psi2] = hev
            qkv_done = fw.h("act")
            S4_free = None
            nm_free = None
            for it in range(16):
                nsl = it // 4 + 1
                cols = nsl * 512
                for sl in range(nsl):
                    psi = rot.get()
                    fw.wait(T, qkv_done)
                    inst = T.matmul(PS[psi][0:P, 0:512],
                                    qfm4[:, it * P:(it + 1) * P],
                                    kfm4[:, sl * 512:(sl + 1) * 512],
                                    start=True, stop=True)
                    hmm = fw.inc(inst, "pe")
                    hev = fw.op(A, A.activation, S4[:, sl * 512:(sl + 1) * 512],
                                PS[psi][0:P, 0:512], AF.Identity,
                                sem_name="act", waits=(hmm, S4_free))
                    ps_free[psi] = hev
                doff = it * P - (nsl - 1) * 512
                hmask = fw.op(V, V.tensor_add, S4[:, (nsl - 1) * 512 + doff:
                                                 (nsl - 1) * 512 + doff + P],
                              S4[:, (nsl - 1) * 512 + doff:
                                 (nsl - 1) * 512 + doff + P],
                              trineg_s[:], sem_name="dve", waits=(fw.h("act"),))
                if doff + P < 512:
                    hmask = fw.op(V, V.memset,
                                  S4[:, (nsl - 1) * 512 + doff + P:cols], NEG,
                                  sem_name="dve")
                hmx = fw.op(V, V.reduce_max, m4r[:], S4[:, 0:cols], axis=AX.X,
                            sem_name="dve")
                hnm = fw.op(V, V.tensor_scalar, nm4[:], m4r[:], -0.125, None,
                            ALU.mult, sem_name="dve", waits=(nm_free,))
                S4_free = hnm
                hz0 = fw.op(V, V.memset, Pt4[:], 0.0, sem_name="dve",
                            waits=(Pt4_free,))
                hexp = fw.op(A, A.activation, Pt4[:, 0:cols], S4[:, 0:cols],
                             AF.Exp, bias=nm4[:], scale=0.125,
                             accum_out=dal4[:, it:it + 1], sem_name="act",
                             waits=(hnm, hz0, dal4_free))
                nm_free = hexp
                Pt4_free = fw.dma(SY, P_d[pd, it * P:(it + 1) * P, :], Pt4[:],
                                  "Ps4", waits=(hexp, pd_free[pd]))
            psi = rot.get()
            fw.wait(T, fw.h("act"))
            inst = T.transpose(PS[psi][0:16, 0:P], dal4[:], eyef_s[:])
            hmm = fw.inc(inst, "pe")
            dal4_free = hmm
            hd16 = fw.op(A, A.activation, d164[:], PS[psi][0:16, 0:P],
                         AF.Identity, sem_name="act", waits=(hmm, d164_free))
            ps_free[psi] = hd16
            d164_free = fw.dma(SY, den_d[:], d164[:], "dn4", waits=(hd16,))
            hdf = fw.dma(SY, rdr4[:], AP(den_t4, 0, [[S, 1], [1, S]]), "dl4",
                         waits=(rdr4_free, d164_free))
            hrc = fw.op(V, V.reciprocal, rdr4[:], rdr4[:], sem_name="dve",
                        waits=(hdf,))
            hrd = fw.op(A, A.activation, rdb4[:], rdr4[:], AF.Identity,
                        sem_name="act", waits=(hrc, rdb4_free))
            rdr4_free = hrd
            for jc in range(16):
                pb = jc % 2
                hpt = fw.dmat(SY, PT4[pb][:], P_d[pd, 0:S, jc * P:(jc + 1) * P],
                              f"pq{pb}", waits=(PT4_free[pb], Pt4_free))
                fw.wait(T, hpt, qkv_done)
                if jc == 0:
                    fw.wait(T, ps_free[4], ps_free[5], ps_free[6], ps_free[7])
                for isl in range(4):
                    inst = T.matmul(PS[4 + isl][0:64, 0:512], v4[:, jc, :],
                                    PT4[pb][:, isl * 512:(isl + 1) * 512],
                                    start=(jc == 0), stop=(jc == 15))
                PT4_free[pb] = fw.inc(inst, "pe")
            hV = fw.h("pe")
            pd_free[pd] = hV
            pd = 1 - pd
            for isl in range(4):
                psi = rot.get()
                fw.wait(T, hrd)
                inst = T.matmul(PS[psi][0:64, 0:512], ones_bf[0:1, 0:64],
                                rdb4[0:1, isl * 512:(isl + 1) * 512],
                                start=True, stop=True)
                hmm = fw.inc(inst, "pe")
                rdb4_free = hmm
                hz = fw.op(A, A.activation, zsl4[:], PS[4 + isl][0:64, 0:512],
                           AF.Identity, sem_name="act", waits=(hV, zs4_free))
                ps_free[4 + isl] = hz
                hmul = fw.op(V, V.tensor_mul, zmul4[:], zsl4[:],
                             PS[psi][0:64, 0:512], sem_name="dve",
                             waits=(hmm, hz, zm4_free))
                ps_free[psi] = hmul
                zs4_free = hmul
                hzl = fw.op(A, A.activation, zln4[:], zmul4[:], AF.Identity,
                            sem_name="act", waits=(hmul, zl4_free))
                zm4_free = hzl
                for dc in range(NCH):
                    psi3 = rot.get()
                    fw.wait(T, hzl, ha4)
                    inst = T.matmul(PS[psi3][0:P, 0:512],
                                    awob[:, dc * P:(dc + 1) * P], zln4[:],
                                    start=True, stop=True)
                    hmm3 = fw.inc(inst, "pe")
                    if dc == NCH - 1:
                        zl4_free = hmm3
                    sb = dc % 2
                    hse = fw.op(A, A.activation, s1t[:, sb, :], PS[psi3][0:P, 0:512],
                                AF.Identity, bias=pcs["a1bo_pc"][:, dc:dc + 1],
                                sem_name="act", waits=(hmm3, s1t_free[sb]))
                    ps_free[psi3] = hse
                    s1t_free[sb] = fw.dma(
                        SY, s1_d[b, dc * P:(dc + 1) * P,
                                 isl * 512:(isl + 1) * 512],
                        s1t[:, sb, :], f"s1s{sb}", waits=(hse,))
            xn1f_free = fw.h("pe")
    ctx7.close()

    def xn1_tile(b, dc, t0, tile_ap, waits):
        return fw.dma(SY, tile_ap, xn1_d[b, dc * P:(dc + 1) * P, t0:t0 + 512],
                      f"lnx{dc}", waits=waits)

    emit_ln(xn1_tile, xnorm_d, gain=pcs["gam_pc"], bias=pcs["bet_pc"])

    fw.new_phase()
    ctx8 = ExitStack()
    e8 = ctx8.enter_context
    if True:
        wgb = e8(nc.sbuf_tensor("wgb", [P, NCH, 8], BF16))
        xg = [e8(nc.sbuf_tensor(f"xg{i}", [P, NCH, P], BF16)) for i in range(2)]
        eprob = e8(nc.sbuf_tensor("eprob", [P, 8], F32))
        curg = e8(nc.sbuf_tensor("curg", [P, 8], F32))
        mg1 = e8(nc.sbuf_tensor("mg1", [P, 1], F32))
        mg2 = e8(nc.sbuf_tensor("mg2", [P, 1], F32))
        sum8 = e8(nc.sbuf_tensor("sum8", [P, 1], F32))
        d2g = e8(nc.sbuf_tensor("d2g", [P, 1], F32))
        gt8 = e8(nc.sbuf_tensor("gt8", [P, 8], F32))
        gt8A = e8(nc.sbuf_tensor("gt8A", [P, 8], F32))
        g8all = e8(nc.sbuf_tensor("g8all", [8, S], F32))
        g8b = e8(nc.sbuf_tensor("g8b", [8, S], BF16))
        g8row = [e8(nc.sbuf_tensor(f"g8row{i}", [1, S], BF16)) for i in range(2)]
        b2b = e8(nc.sbuf_tensor("b2b", [P, 8, NCH], F32))
        w1r4 = [e8(nc.sbuf_tensor(f"w1r4{i}", [P, NCH, 512], BF16))
                for i in range(2)]
        w2r4 = [e8(nc.sbuf_tensor(f"w2r4{i}", [P, 4, D], BF16)) for i in range(2)]
        b1b4 = [e8(nc.sbuf_tensor(f"b1b4{i}", [P, 32], F32)) for i in range(2)]
        hall4 = e8(nc.sbuf_tensor("hall4", [P, 32, 512], BF16))
        xnt4 = [e8(nc.sbuf_tensor(f"xnt4{i}", [P, NCH, 512], BF16))
                for i in range(2)]
        gbs4 = e8(nc.sbuf_tensor("gbs4", [P, 512], BF16))
        accs4 = e8(nc.sbuf_tensor("accs4", [P, NCH, S], F32))
        uc = e8(nc.sbuf_tensor("uc", [P, 2, 512], F32))
        gc = e8(nc.sbuf_tensor("gc", [P, 2, 512], BF16))
        xn1c = e8(nc.sbuf_tensor("xn1c", [P, 2, 512], BF16))
        s1c = e8(nc.sbuf_tensor("s1c", [P, 2, 512], BF16))
        xnc = e8(nc.sbuf_tensor("xnc", [P, 2, 512], F32))
        t1c = e8(nc.sbuf_tensor("t1c", [P, 2, 512], F32))
        yc = e8(nc.sbuf_tensor("yc", [P, 2, 512], F32))
        hwg = fw.dma(G, wgb[:], WgT.rearrange("(c p) e -> p c e", p=P), "wg4")
        hb2 = fw.dma(SY, b2b[:], b2pc, "b24")
        rot = Rot(range(4))
        ep_free = None
        gt_free = None
        gt8A_free = None
        hex8_prev = None
        w1g_free = [None, None]
        hwc2 = [None, None]
        hgst = None
        g8a_free = None
        xg_free = None
        w1_free = None
        w2_free = [None, None]
        b1_free = [None, None]
        xnt_free = [None, None]
        gbs_free = None
        hall_free = None
        hw1 = None
        hb1 = None
        cmb4 = [[None, None] for _ in range(8)]
        xg_free = [None, None]
        for b in range(NB):
            for tc in range(16):
                gb_ = tc % 2
                hxg = fw.dma(SY, xg[gb_][:],
                             xnorm_d[b][:, tc * P:(tc + 1) * P].rearrange(
                                 "(c p) t -> p c t", p=P),
                             f"xg4{gb_}", waits=(xg_free[gb_],))
                psi = rot.get()
                fw.wait(T, hwg, hxg)
                for dc in range(NCH):
                    inst = T.matmul(PS[psi][0:P, 0:8],
                                    xg[gb_][:, dc, :],
                                    wgb[:, dc, :], start=(dc == 0),
                                    stop=(dc == NCH - 1))
                hmm = fw.inc(inst, "pe")
                xg_free[gb_] = hmm
                fw.wait(V, hmm, ep_free, hex8_prev)
                fw.op(V, V.reduce_max, mg1[:], PS[psi][0:P, 0:8], axis=AX.X,
                      sem_name="dve")
                fw.op(V, V.tensor_scalar, mg2[:], mg1[:], -1.0, None, ALU.mult,
                      sem_name="dve")
                hex8 = fw.op(A, A.activation, eprob[:], PS[psi][0:P, 0:8],
                             AF.Exp, bias=mg2[:], accum_out=sum8[:],
                             sem_name="act", waits=(fw.h("dve"),))
                ps_free[psi] = hex8
                hex8_prev = hex8
                fw.op(V, V.reduce_max, mg1[:], eprob[:], axis=AX.X,
                      sem_name="dve", waits=(hex8,))
                fw.op(V, V.scalar_tensor_tensor, curg[:], eprob[:], mg1[:],
                      eprob[:], ALU.is_lt, ALU.mult, sem_name="dve")
                fw.op(V, V.reduce_max, mg2[:], curg[:], axis=AX.X,
                      sem_name="dve")
                fw.op(V, V.tensor_add, d2g[:], mg1[:], mg2[:], sem_name="dve")
                fw.op(V, V.scalar_tensor_tensor, d2g[:], sum8[:], 1e-9,
                      d2g[:], ALU.mult, ALU.add, sem_name="dve")
                fw.op(V, V.reciprocal, d2g[:], d2g[:], sem_name="dve")
                fw.op(V, V.scalar_tensor_tensor, gt8[:], eprob[:], mg2[:],
                      eprob[:], ALU.is_ge, ALU.mult, sem_name="dve",
                      waits=(gt_free,))
                ep_free = fw.op(V, V.tensor_scalar, gt8[:], gt8[:], d2g[:],
                                None, ALU.mult, sem_name="dve")
                hga = fw.op(A, A.activation, gt8A[:], gt8[:], AF.Identity,
                            sem_name="act", waits=(ep_free, gt8A_free))
                psi2 = rot.get()
                fw.wait(T, hga)
                inst = T.transpose(PS[psi2][0:8, 0:P], gt8A[:], eyef_s[:])
                hmm2 = fw.inc(inst, "pe")
                gt_free = hmm2
                gt8A_free = hmm2
                hgl = fw.op(A, A.activation, g8all[:, tc * P:(tc + 1) * P],
                            PS[psi2][0:8, 0:P], AF.Identity, sem_name="act",
                            waits=(hmm2, g8a_free))
                ps_free[psi2] = hgl
            hg8b = fw.op(A, A.activation, g8b[:], g8all[:], AF.Identity,
                         sem_name="act")
            hgst = fw.dma(SY, g8_d[b], g8all[:], "g84", waits=(fw.h("act"),))
            g8a_free = hgst

            hacz = fw.op(V, V.memset, accs4[:], 0.0, sem_name="dve",
                         waits=(fw.h("pe"),))
            hgr = [None, None]
            hb1r = [None, None]
            for e in range(8):
                if e == 0:
                    hb1r[0] = fw.dma(SY, b1b4[0][:], b1_pc[0], "b14a",
                                     waits=(fw.h("act"),))
                    hgr[0] = fw.dma(G, g8row[0][:], g8_d[b, 0:1, :], "gr40",
                                    waits=(hgst, fw.h("pe")))
                if e + 1 < 8:
                    hb1r[(e + 1) % 2] = fw.dma(
                        SY, b1b4[(e + 1) % 2][:], b1_pc[e + 1],
                        f"b14{(e + 1) % 2}", waits=(fw.h("act"),))
                    hgr[(e + 1) % 2] = fw.dma(
                        G, g8row[(e + 1) % 2][:], g8_d[b, e + 1:e + 2, :],
                        f"gr4{(e + 1) % 2}", waits=(hgst, fw.h("pe")))
                for tt in range(4):
                    to = tt * 512
                    xb_ = tt % 2
                    hxt = []
                    for dc in range(NCH):
                        hxt.append(fw.dma(
                            SY, xnt4[xb_][:, dc, :],
                            xnorm_d[b, dc * P:(dc + 1) * P, to:to + 512],
                            f"xq{xb_}_{dc}", waits=(xnt_free[xb_],)))
                    psi = rot.get()
                    fw.wait(T, hgr[e % 2])
                    inst = T.matmul(PS[psi][0:P, 0:512], ones_bf[:],
                                    g8row[e % 2][0:1, to:to + 512],
                                    start=True, stop=True)
                    hmm = fw.inc(inst, "pe")
                    hgbs = fw.op(A, A.activation, gbs4[:], PS[psi][0:P, 0:512],
                                 AF.Identity, sem_name="act",
                                 waits=(hmm, gbs_free))
                    ps_free[psi] = hgbs
                    hw1g = [None, None]
                    for fc in range(32):
                        psi_h = fc % 2
                        if fc % 4 == 0:
                            fg = fc // 4
                            wr1 = fg % 2
                            fw.wait(G, w1g_free[wr1])
                            hw1g[wr1] = fw.dma(
                                G, w1r4[wr1][:],
                                W1[e][:, fg * 512:(fg + 1) * 512].rearrange(
                                    "(c p) f -> p c f", p=P),
                                f"w14{wr1}")
                        fw.wait(T, ps_free[psi_h], hw1g[(fc // 4) % 2], *hxt)
                        for dc in range(NCH):
                            inst = T.matmul(PS[psi_h][0:P, 0:512],
                                            w1r4[(fc // 4) % 2][:, dc,
                                                (fc % 4) * P:(fc % 4 + 1) * P],
                                            xnt4[xb_][:, dc, :],
                                            start=(dc == 0), stop=(dc == NCH - 1))
                        hmm = fw.inc(inst, "pe")
                        if fc % 4 == 3:
                            w1g_free[(fc // 4) % 2] = hmm
                        hge = fw.op(A, A.activation, hall4[:, fc, :],
                                    PS[psi_h][0:P, 0:512], AF.Gelu_apprx_tanh,
                                    bias=b1b4[e % 2][:, fc:fc + 1],
                                    sem_name="act",
                                    waits=(hmm, hb1r[e % 2], hall_free))
                        ps_free[psi_h] = hge
                    hW1done = fw.h("pe")
                    xnt_free[xb_] = hW1done
                    for fc in range(32):
                        if fc % 4 == 0:
                            fg = fc // 4
                            wr = fg % 2
                            fw.wait(G, w2_free[wr])
                            hwc2[wr] = fw.dma(
                                G, w2r4[wr][:],
                                W2[e, fg * 512:(fg + 1) * 512, :].rearrange(
                                    "(c p) d -> p c d", p=P),
                                f"w24{wr}")
                        wr = (fc // 4) % 2
                        fw.wait(T, hwc2[wr], fw.h("act"))
                        if fc == 0:
                            fw.wait(T, *[ps_free[i] for i in range(8)])
                        for dc in range(NCH):
                            inst = T.matmul(PS[dc][0:P, 0:512],
                                            w2r4[wr][:, fc % 4,
                                                dc * P:(dc + 1) * P],
                                            hall4[:, fc, :],
                                            start=(fc == 0), stop=(fc == 31))
                        hmmw = fw.inc(inst, "pe")
                        if fc % 4 == 3:
                            w2_free[wr] = hmmw
                    hall_free = fw.h("pe")
                    for dc in range(NCH):
                        hu = fw.op(V, V.scalar_tensor_tensor, uc[:, dc % 2, :],
                                   PS[dc][0:P, 0:512], b2b[:, e, dc:dc + 1],
                                   gbs4[:], ALU.add, ALU.mult, sem_name="dve",
                                   waits=(fw.h("pe"), hgbs, hacz, hb2,
                                          cmb4[0][dc % 2]))
                        hac = fw.op(V, V.tensor_add,
                                    accs4[:, dc, to:to + 512],
                                    accs4[:, dc, to:to + 512], uc[:, dc % 2, :],
                                    sem_name="dve")
                        cmb4[0][dc % 2] = hac
                        ps_free[dc] = hac
                    gbs_free = fw.h("dve")
            moe_done = fw.h("dve")
            w1_free = fw.h("pe")
            for dc in range(NCH):
                for ts in range(4):
                    to = ts * 512
                    cb = ts % 2
                    h1 = fw.dma(SY, xn1c[:, cb, :],
                                xn1_d[b, dc * P:(dc + 1) * P, to:to + 512],
                                f"f1{cb}", waits=(cmb4[1][cb],))
                    h2 = fw.dma(SY, s1c[:, cb, :],
                                s1_d[b, dc * P:(dc + 1) * P, to:to + 512],
                                f"f2{cb}", waits=(cmb4[2][cb],))
                    h3 = fw.dma(SY, xnc[:, cb, :],
                                xnew_d[b, dc * P:(dc + 1) * P, to:to + 512],
                                f"f3{cb}", waits=(cmb4[3][cb],))
                    hu2 = fw.op(V, V.tensor_add, uc[:, cb, :],
                                xn1c[:, cb, :], accs4[:, dc, to:to + 512],
                                sem_name="dve", waits=(moe_done, h1,
                                                       cmb4[4][cb]))
                    hsi = fw.op(A, A.activation, gc[:, cb, :], uc[:, cb, :],
                                AF.Silu, sem_name="act",
                                waits=(hu2, cmb4[5][cb]))
                    ht1 = fw.op(V, V.tensor_mul, t1c[:, cb, :], s1c[:, cb, :],
                                gc[:, cb, :], sem_name="dve",
                                waits=(hsi, h2, cmb4[6][cb]))
                    hy = fw.op(V, V.tensor_add, yc[:, cb, :], xnc[:, cb, :],
                               t1c[:, cb, :], sem_name="dve",
                               waits=(h3, cmb4[7][cb]))
                    hys = fw.dma(SY, y_out[b, dc * P:(dc + 1) * P, to:to + 512],
                                 yc[:, cb, :], f"ys{cb}", waits=(hy,))
                    cmb4[1][cb] = hu2
                    cmb4[2][cb] = ht1
                    cmb4[3][cb] = hy
                    cmb4[4][cb] = hsi
                    cmb4[5][cb] = ht1
                    cmb4[6][cb] = hy
                    cmb4[7][cb] = hys
    ctx8.close()

    return finish()


def kernel(**inputs):
    in_maps = _host_prep(inputs)
    nc = build_program(sim_strict=True)
    res = run_bass_kernel_spmd(nc, in_maps, core_ids=list(range(NCORE)))
    out = np.concatenate(
        [r["y"].transpose(0, 2, 1) for r in res.results], axis=0)
    return np.ascontiguousarray(out)


def _run_debug(inputs, dbg=(), limit=99, cores=(0,), serialize=False, p2_stop=False):
    in_maps = _host_prep(inputs)
    nc = build_program(dbg=dbg, limit=limit, serialize=serialize, p2_stop=p2_stop)
    res = run_bass_kernel_spmd(nc, [in_maps[c] for c in cores],
                               core_ids=list(cores))
    return res.results


# version 03
# speedup vs baseline: 1.0600x; 1.0600x over previous
import numpy as np
import ml_dtypes
import concourse.bass as bass
import concourse.mybir as mybir
from concourse.bass_utils import run_bass_kernel_spmd
from concourse.bass_types import AP
from contextlib import ExitStack

F32 = mybir.dt.float32
BF16 = mybir.dt.bfloat16
AF = mybir.ActivationFunctionType
ALU = mybir.AluOpType
AX = mybir.AxisListType

P = 128
S = 2048
D = 1024
NCH = D // P
NB = 2
NCORE = 8
PP = 2 * S - 1
H0 = 16
HD = 64

NEG = -1.0e30


class FW:
    def __init__(self, nc):
        self.nc = nc
        self.sems = {}
        self.counts = {}
        self.hwm = {}
        self.serialize = False
        self.tagmap = {}
        self.pool_i = {"s": 0, "g": 0}
        self.engines = []
        self.strict = False

    def sem(self, name):
        if name not in self.sems:
            self.sems[name] = self.nc.alloc_semaphore(name=name)
            self.counts[name] = 0
        return self.sems[name]

    def h(self, name):
        return (name, self.counts[name])

    def wait(self, eng, *handles):
        if self.serialize:
            handles = tuple(handles) + self.all_handles()
        for hh in handles:
            if hh is None:
                continue
            name, cnt = hh
            if cnt <= 0:
                continue
            key = (eng.engine, name)
            if self.hwm.get(key, 0) < cnt:
                eng.wait_ge(self.sems[name], cnt)
                self.hwm[key] = cnt

    def inc(self, inst, name, amount=1):
        inst.then_inc(self.sems[name], amount)
        self.counts[name] += amount
        return (name, self.counts[name])

    def incn(self, eng, name):
        return self.inc(eng.nop(nofuse=True), name)

    def dma(self, eng, out, in_, tag, waits=(), global_tag=False, **kw):
        if global_tag:
            s = "GT_" + tag + ("g" if eng.engine == mybir.EngineType.Pool
                               else "s")
            self.sem(s)
        else:
            kind = "g" if eng.engine == mybir.EngineType.Pool else "s"
            s = self._tag_sem(tag, kind)
        self.wait(eng, *waits)
        inst = eng.dma_start(out=out, in_=in_, **kw)
        return self.inc(inst, s, 16)

    def dmat(self, eng, out, in_, tag, waits=()):
        s = self._tag_sem(tag, "s")
        self.wait(eng, *waits)
        inst = eng.dma_start_transpose(out, in_)
        return self.inc(inst, s, 16)

    _ENG_SEM = {}

    def op(self, eng, fn, *args, sem_name=None, waits=(), **kw):
        if self.serialize and sem_name is None:
            sem_name = "dve" if eng.engine == mybir.EngineType.DVE else None
        if self.strict:
            en = {mybir.EngineType.DVE: "dve",
                  mybir.EngineType.Activation: "act"}.get(eng.engine)
            if en is not None:
                self.wait(eng, (en, self.counts[en]))
                if sem_name is None:
                    sem_name = en
        self.wait(eng, *waits)
        inst = fn(*args, **kw)
        if sem_name is not None:
            return self.inc(inst, sem_name, 1)
        return None

    def all_handles(self, include_global=False):
        return tuple(self.h(n) for n in self.sems
                     if include_global or not n.startswith("GT_"))

    def phase_barrier(self, engines=None):
        hs = self.all_handles()
        for eng in (engines or self.engines):
            self.wait(eng, *hs)

    def new_phase(self):
        self.phase_barrier()
        self.tagmap = {}
        self.pool_i = {"s": 0, "g": 0}

    def _tag_sem(self, tag, kind="s"):
        key = kind + tag
        if key not in self.tagmap:
            name = f"{kind}{self.pool_i[kind]}"
            self.pool_i[kind] += 1
            self.sem(name)
            self.tagmap[key] = name
        return self.tagmap[key]

    def ht(self, tag, kind="s"):
        return self.h(self._tag_sem(tag, kind))


def _host_consts():
    pos = np.arange(S - 1, -S, -1, dtype=np.float64)
    inv_freq = 1.0 / (10000.0 ** (np.arange(0, D, 2, dtype=np.float64) / D))
    ang = pos[:, None] * inv_freq[None, :]
    pe = np.concatenate([np.sin(ang), np.cos(ang)], axis=-1)
    pe_rev = pe[::-1]
    peT = np.zeros((D, PP + 1), dtype=np.float32)
    peT[:, :PP] = pe_rev.T.astype(np.float32)

    dec = np.exp(-np.arange(S, dtype=np.float64) / S).astype(np.float32)
    dec_pc = dec.reshape(S // P, P).T.copy()

    r = np.arange(P)
    trineg = np.where(r[None, :] <= r[:, None], 0.0, NEG).astype(np.float32)
    eyef = np.eye(P, dtype=np.float32)
    return peT, dec_pc, trineg, eyef


def _pc(v):
    return np.ascontiguousarray(v.reshape(-1, P).T)


def _host_prep(inputs):
    g = lambda k: np.asarray(inputs[k], np.float32)
    peT, dec_pc, trineg, eyef = _host_consts()
    com = {
        "WqT": np.ascontiguousarray(g("L0_attn_W_q").T),
        "WkT": np.ascontiguousarray(g("L0_attn_W_k").T),
        "WvT": np.ascontiguousarray(g("L0_attn_W_v").T),
        "WoT": np.ascontiguousarray(g("L0_attn_W_o").T),
        "bq_row": g("L0_attn_b_q")[None, :],
        "bk_row": g("L0_attn_b_k")[None, :],
        "bv_row": g("L0_attn_b_v")[None, :],
        "bo_pc": _pc(g("L0_attn_b_o")),
        "fwq": g("L0_ffn_wq"), "fwk": g("L0_ffn_wk"), "fwv": g("L0_ffn_wv"),
        "fwp": g("L0_ffn_wp"), "fwo": g("L0_ffn_wo"),
        "ln1g_pc": _pc(g("L0_ffn_ln1_g")), "ln1b_pc": _pc(g("L0_ffn_ln1_b")),
        "ln2g_pc": _pc(g("L0_ffn_ln2_g")), "ln2b_pc": _pc(g("L0_ffn_ln2_b")),
        "sel_w": g("L0_ffn_sel_w"),
        "keys": g("L0_ffn_keys"), "values": g("L0_ffn_values"),
        "a1wq": g("L1_attn_wq"), "a1wk": g("L1_attn_wk"), "a1wv": g("L1_attn_wv"),
        "a1bq_row": g("L1_attn_bq")[None, :],
        "a1bk_row": g("L1_attn_bk")[None, :],
        "a1bv_row": g("L1_attn_bv")[None, :],
        "a1wo": g("L1_attn_wo"),
        "a1bo_pc": _pc(g("L1_attn_bo")),
        "gam_pc": _pc(g("L1_ffn_gamma")), "bet_pc": _pc(g("L1_ffn_beta")),
        "WgT": np.ascontiguousarray(g("L1_ffn_Wg").T),
        "W1": g("L1_ffn_W1"),
        "b1_pc": np.ascontiguousarray(
            g("L1_ffn_b1").reshape(8, 32, P).transpose(0, 2, 1)),
        "W2": g("L1_ffn_W2"),
        "b2pc": np.ascontiguousarray(
            g("L1_ffn_b2").reshape(8, 8, P).transpose(2, 0, 1)),
        "peT": peT, "dec_pc": dec_pc, "trineg": trineg, "eyef": eyef,
    }
    x = np.asarray(inputs["L0_attn_x"], np.float32)
    in_maps = []
    for c in range(NCORE):
        xb = x[2 * c:2 * c + 2]
        m = dict(com)
        m["x"] = np.ascontiguousarray(xb.transpose(0, 2, 1))
        in_maps.append(m)
    return in_maps


def build_program(dbg=(), limit=99, serialize=False, p2_stop=False, sim_strict=False):
    nc = bass.Bass("TRN2", target_bir_lowering=False, debug=False)
    fw = FW(nc)
    fw.serialize = serialize
    fw.strict = sim_strict
    for s_ in ["pe", "act", "dve"]:
        fw.sem(s_)

    def din(name, shape):
        return nc.dram_tensor(name, shape, F32, kind="ExternalInput").ap()

    x_in = din("x", [NB, D, S])
    WqT, WkT, WvT, WoT = (din(n, [D, D]) for n in ["WqT", "WkT", "WvT", "WoT"])
    bq_row, bk_row, bv_row = (din(n, [1, D]) for n in ["bq_row", "bk_row", "bv_row"])
    bo_pc = din("bo_pc", [P, NCH])
    fwq, fwk, fwv, fwp, fwo = (din(n, [D, D]) for n in
                               ["fwq", "fwk", "fwv", "fwp", "fwo"])
    ln1g_pc, ln1b_pc = din("ln1g_pc", [P, NCH]), din("ln1b_pc", [P, NCH])
    ln2g_pc, ln2b_pc = din("ln2g_pc", [P, NCH]), din("ln2b_pc", [P, NCH])
    sel_w = din("sel_w", [D, 32])
    keys = din("keys", [32, D, 128])
    values = din("values", [32, 128, D])
    a1wq, a1wk, a1wv = din("a1wq", [D, 64]), din("a1wk", [D, 64]), din("a1wv", [D, 64])
    a1bq_row, a1bk_row, a1bv_row = (din(n, [1, 64]) for n in
                                    ["a1bq_row", "a1bk_row", "a1bv_row"])
    a1wo = din("a1wo", [64, D])
    a1bo_pc = din("a1bo_pc", [P, NCH])
    gam_pc, bet_pc = din("gam_pc", [P, NCH]), din("bet_pc", [P, NCH])
    WgT = din("WgT", [D, 8])
    W1 = din("W1", [8, D, 4096])
    b1_pc = din("b1_pc", [8, P, 32])
    W2 = din("W2", [8, 4096, D])
    b2pc = din("b2pc", [P, 8, NCH])
    peT = din("peT", [D, PP + 1])
    dec_pc = din("dec_pc", [P, S // P])
    trineg_in = din("trineg", [P, P])
    eyef_in = din("eyef", [P, P])

    y_out = nc.dram_tensor("y", [NB, D, S], F32, kind="ExternalOutput").ap()

    scr = {}

    def dscr(name, shape, dtype):
        kind = "ExternalOutput" if name in dbg else "Internal"
        t = nc.dram_tensor(name, shape, dtype, kind=kind)
        scr[name] = t.ap()
        return scr[name]

    xn_d = dscr("xn_d", [NB, D, S], BF16)
    ccos_d = dscr("ccos_d", [NB, D, S], BF16)
    lnr_d = dscr("lnr_d", [NB, 4, 3, 512], F32)
    p2_d = dscr("p2_d", [D, PP + 1], BF16)
    qf_d = dscr("qf_d", [NB, S, D], BF16)
    s_d = dscr("s_d", [NB, D, S], BF16)
    x1_d = dscr("x1_d", [NB, D, S], BF16)
    qa_d = dscr("qa_d", [NB, D, S], BF16)
    ka_d = dscr("ka_d", [NB, D, S], BF16)
    v_d = dscr("v_d", [NB, NCH, S, P], BF16)
    bd_d = dscr("bd_d", [4, P, 2176], BF16)
    P_d = dscr("P_d", [2, S, S], BF16)
    den_d = dscr("den_d", [16, P], F32)
    src2_d = dscr("src2_d", [NB, D, S], BF16)
    x2_d = dscr("x2_d", [NB, D, S], BF16)
    gates_d = dscr("gates_d", [NB, 32, S], F32)
    xnew_d = dscr("xnew_d", [NB, D, S], F32)
    xnewb_d = dscr("xnewb_d", [NB, D, S], BF16)
    xn1_d = dscr("xn1_d", [NB, D, S], BF16)
    s1_d = dscr("s1_d", [NB, D, S], BF16)
    xnorm_d = dscr("xnorm_d", [NB, D, S], BF16)
    g8_d = dscr("g8_d", [NB, 8, S], F32)
    w1p_d = dscr("w1p_d", [8, 8, P, 4096], BF16)
    w2p_d = dscr("w2p_d", [8, 8, P, 4 * D], BF16)
    keyp_d = dscr("keyp_d", [32, P, NCH * P], BF16)
    valp_d = dscr("valp_d", [32, P, D], BF16)

    ctx = ExitStack()
    e_ = ctx.enter_context

    PS = [e_(nc.psum_tensor(f"ps{i}", [P, 512], F32)) for i in range(8)]
    ps_free = [None] * 8

    ones_bf = e_(nc.sbuf_tensor("ones_bf", [1, P], BF16))
    onesc_bf = e_(nc.sbuf_tensor("onesc_bf", [P, 1], BF16))
    ones_f = e_(nc.sbuf_tensor("ones_f", [1, P], F32))
    eps5 = e_(nc.sbuf_tensor("eps5", [P, 1], F32))
    eps6 = e_(nc.sbuf_tensor("eps6", [P, 1], F32))
    zcol = e_(nc.sbuf_tensor("zcol", [P, 1], F32))
    trineg_s = e_(nc.sbuf_tensor("trineg_s", [P, P], F32))
    eyef_s = e_(nc.sbuf_tensor("eyef_s", [P, P], F32))
    eyeb_s = e_(nc.sbuf_tensor("eyeb_s", [P, P], BF16))
    dec_s = e_(nc.sbuf_tensor("dec_s", [P, S // P], F32))
    pcs = {}
    for nm, apin in [("bo_pc", bo_pc), ("ln1g_pc", ln1g_pc), ("ln1b_pc", ln1b_pc),
                     ("ln2g_pc", ln2g_pc), ("ln2b_pc", ln2b_pc),
                     ("a1bo_pc", a1bo_pc), ("gam_pc", gam_pc), ("bet_pc", bet_pc)]:
        pcs[nm] = e_(nc.sbuf_tensor(nm + "_s", [P, NCH], F32))
    maxp2 = e_(nc.sbuf_tensor("maxp2", [1, NCH], F32))

    V = nc.vector
    A = nc.scalar
    T = nc.tensor
    G = nc.gpsimd
    SY = nc.sync
    fw.engines = [T, A, V, SY, G]

    fw.op(V, V.memset, ones_bf[:], 1.0, sem_name="dve")
    fw.op(V, V.memset, onesc_bf[:], 1.0, sem_name="dve")
    fw.op(V, V.memset, ones_f[:], 1.0, sem_name="dve")
    fw.op(V, V.memset, eps5[:], 1e-5, sem_name="dve")
    fw.op(V, V.memset, eps6[:], 1e-6, sem_name="dve")
    fw.op(V, V.memset, zcol[:], 0.0, sem_name="dve")
    fw.dma(SY, trineg_s[:], trineg_in, "cst")
    fw.dma(SY, eyef_s[:], eyef_in, "cst")
    fw.dma(G, eyeb_s[:], eyef_in, "cstp")
    fw.dma(SY, dec_s[:], dec_pc, "cst")
    for nm, apin in [("bo_pc", bo_pc), ("ln1g_pc", ln1g_pc), ("ln1b_pc", ln1b_pc),
                     ("ln2g_pc", ln2g_pc), ("ln2b_pc", ln2b_pc),
                     ("a1bo_pc", a1bo_pc), ("gam_pc", gam_pc), ("bet_pc", bet_pc)]:
        fw.dma(SY, pcs[nm][:], apin, "cst")
    consts_ld = (fw.ht("cst"), fw.ht("cstp", "g"), fw.h("dve"))

    with (nc.sbuf_tensor("peb", [P, NCH, PP + 1], BF16) as peb,
          nc.sbuf_tensor("wpb", [P, NCH, D], BF16) as wpb,
          nc.sbuf_tensor("p2t", [P, PP + 1], BF16) as p2t,
          nc.sbuf_tensor("p2sq", [P, PP + 1], BF16) as p2sq,
          nc.sbuf_tensor("prow", [1, 512], F32) as prow,
          nc.sbuf_tensor("pmax", [1, 1], F32) as pmax):
        p2t_free = None
        p2sq_free = None
        hpe = fw.dma(G, peb[:], peT.rearrange("(c p) q -> p c q", p=P), "peb")
        hwp = fw.dma(G, wpb[:], fwp.rearrange("(c p) m -> p c m", p=P), "wpb")
        for e_w in range(8):
            for fg_ in range(8):
                fw.dma(G, w1p_d[e_w, fg_].rearrange("p (c f) -> p c f", c=NCH),
                       W1[e_w][:, fg_ * 512:(fg_ + 1) * 512].rearrange(
                           "(c p) f -> p c f", p=P), "wpc", global_tag=True)
                fw.dma(G, w2p_d[e_w, fg_].rearrange("p (c d) -> p c d", c=4),
                       W2[e_w, fg_ * 512:(fg_ + 1) * 512, :].rearrange(
                           "(c p) d -> p c d", p=P), "wpc", global_tag=True)
        for e_w in range(32):
            fw.dma(G, keyp_d[e_w].rearrange("p (c f) -> p c f", c=NCH),
                   keys[e_w].rearrange("(c p) f -> p c f", p=P), "kpc",
                   global_tag=True)
            fw.dma(G, valp_d[e_w], values[e_w], "kpc", global_tag=True)
        wpc_done = fw.h("GT_wpcg")
        kpc_done = fw.h("GT_kpcg")
        NSL = (PP + 1) // 512
        for mc in range(NCH):
            for sl in range(NSL):
                psi = sl % 2
                fw.wait(T, hpe, hwp, ps_free[psi], *consts_ld)
                for dc in range(NCH):
                    inst = T.matmul(PS[psi][:], wpb[:, dc, mc * P:(mc + 1) * P],
                                    peb[:, dc, sl * 512:(sl + 1) * 512],
                                    start=(dc == 0), stop=(dc == NCH - 1))
                hmm = fw.inc(inst, "pe")
                hact = fw.op(A, A.activation, p2t[:, sl * 512:(sl + 1) * 512],
                             PS[psi][:], AF.Identity, sem_name="act",
                             waits=(hmm, p2t_free))
                ps_free[psi] = hact
            hst = fw.dma(SY, p2_d[mc * P:(mc + 1) * P, :], p2t[:],
                         "p2s", waits=(fw.h("act"),))
            hsq = fw.op(A, A.activation, p2sq[:], p2t[:], AF.Square,
                        sem_name="act", waits=(fw.h("act"), p2sq_free))
            p2t_free = hst
            fw.op(V, V.memset, pmax[:], 0.0, sem_name="dve")
            for sl in range(NSL):
                psi = 2 + sl % 2
                fw.wait(T, hsq, ps_free[psi], *consts_ld)
                inst = T.matmul(PS[psi][0:1, :], onesc_bf[:],
                                p2sq[:, sl * 512:(sl + 1) * 512],
                                start=True, stop=True)
                hmm = fw.inc(inst, "pe")
                p2sq_free = hmm
                fw.op(V, V.reduce_max, prow[:, 0:1], PS[psi][0:1, :], axis=AX.X,
                      sem_name="dve", waits=(hmm,))
                hdv = fw.op(V, V.tensor_max, pmax[:], pmax[:], prow[:, 0:1],
                            sem_name="dve")
                ps_free[psi] = hdv
            fw.op(V, V.tensor_copy, maxp2[:, mc:mc + 1], pmax[:],
                  sem_name="dve")
        p0_done = (fw.h("dve"), fw.h("act"), fw.h("pe"))

    def finish():
        for hh in fw.all_handles(include_global=True):
            fw.wait(SY, hh)
        ctx.close()
        return nc

    if limit < 1:
        return finish()

    ln_ctr = [0]

    def emit_ln(src_tile_dma, dst_d, gain=None, bias=None, src_f32=False):
        ln_ctr[0] += 1
        fw.new_phase()
        tg = f"ln{ln_ctr[0]}"
        with (nc.sbuf_tensor(tg + "_x", [P, NCH, 512], F32 if src_f32 else BF16) as xt,
              nc.sbuf_tensor(tg + "_xb", [P, NCH, 512], BF16) as xb,
              nc.sbuf_tensor(tg + "_sq", [P, NCH, 512], BF16) as sqt,
              nc.sbuf_tensor(tg + "_o", [P, NCH, 512], BF16) as ot,
              nc.sbuf_tensor(tg + "_rb", [1, 512], BF16) as rbr,
              nc.sbuf_tensor(tg + "_cb", [1, 512], BF16) as cbr,
              nc.sbuf_tensor(tg + "_m", [1, 512], F32) as mr,
              nc.sbuf_tensor(tg + "_q", [1, 512], F32) as qr,
              nc.sbuf_tensor(tg + "_rr", [1, 512], F32) as rr,
              nc.sbuf_tensor(tg + "_cr", [1, 512], F32) as cr):
            xt_free = [None] * NCH
            ot_free = [None] * NCH
            sq_free = [None] * NCH
            xb_free = [None] * NCH
            row_free = None
            ps_s, ps_q, ps_r, ps_c = PS[0], PS[1], PS[2], PS[3]
            for b in range(NB):
                for tt in range(S // 512):
                    t0 = tt * 512
                    hld = []
                    for dc in range(NCH):
                        hld.append(src_tile_dma(b, dc, t0, xt[:, dc, :],
                                                (xt_free[dc],)))
                    for dc in range(NCH):
                        if src_f32:
                            fw.op(A, A.activation, xb[:, dc, :], xt[:, dc, :],
                                  AF.Identity, sem_name="act",
                                  waits=(hld[dc], xb_free[dc]))
                            src_bf = xb[:, dc, :]
                            hsrc = fw.h("act")
                        else:
                            src_bf = xt[:, dc, :]
                            hsrc = hld[dc]
                        hsq = fw.op(A, A.activation, sqt[:, dc, :], src_bf,
                                    AF.Square, sem_name="act",
                                    waits=(hsrc, sq_free[dc]))
                        fw.wait(T, ps_free[0], ps_free[1], hsrc, hsq)
                        T.matmul(ps_s[0:1, :], onesc_bf[:], src_bf,
                                 start=(dc == 0), stop=(dc == NCH - 1))
                        inst2 = T.matmul(ps_q[0:1, :], onesc_bf[:], sqt[:, dc, :],
                                         start=(dc == 0), stop=(dc == NCH - 1))
                    hmm = fw.inc(inst2, "pe")
                    for dc in range(NCH):
                        sq_free[dc] = hmm
                        xb_free[dc] = hmm
                    fw.op(V, V.tensor_scalar, mr[:], ps_s[0:1, :], 1.0 / D, None,
                          ALU.mult, sem_name="dve", waits=(hmm,))
                    fw.op(V, V.tensor_scalar, qr[:], ps_q[0:1, :], 1.0 / D, None,
                          ALU.mult, sem_name="dve")
                    ps_free[0] = ps_free[1] = fw.h("dve")
                    fw.op(V, V.tensor_mul, rr[:], mr[:], mr[:], sem_name="dve")
                    fw.op(V, V.tensor_sub, qr[:], qr[:], rr[:], sem_name="dve")
                    fw.op(A, A.activation, qr[:], qr[:], AF.Ln, bias=eps5[0:1, :],
                          sem_name="act", waits=(fw.h("dve"),))
                    fw.op(A, A.activation, rr[:], qr[:], AF.Exp, scale=-0.5,
                          sem_name="act")
                    fw.wait(V, fw.h("act"))
                    fw.op(V, V.tensor_mul, cr[:], mr[:], rr[:], sem_name="dve")
                    fw.op(V, V.tensor_scalar, cr[:], cr[:], -1.0, None, ALU.mult,
                          sem_name="dve")
                    fw.op(A, A.activation, rbr[:], rr[:], AF.Identity,
                          sem_name="act", waits=(row_free, fw.h("dve")))
                    fw.op(A, A.activation, cbr[:], cr[:], AF.Identity,
                          sem_name="act")
                    hrows = fw.h("act")
                    fw.wait(T, hrows, ps_free[2], ps_free[3])
                    T.matmul(ps_r[:], ones_bf[:], rbr[:], start=True, stop=True)
                    inst = T.matmul(ps_c[:], ones_bf[:], cbr[:],
                                    start=True, stop=True)
                    hbc = fw.inc(inst, "pe")
                    row_free = hbc
                    for dc in range(NCH):
                        fw.op(V, V.tensor_mul, xt[:, dc, :], xt[:, dc, :], ps_r[:],
                              sem_name="dve", waits=(hbc, hld[dc]))
                        if gain is None:
                            hn = fw.op(V, V.tensor_add, ot[:, dc, :], xt[:, dc, :],
                                       ps_c[:], sem_name="dve",
                                       waits=(ot_free[dc],))
                        else:
                            fw.op(V, V.tensor_add, xt[:, dc, :], xt[:, dc, :],
                                  ps_c[:], sem_name="dve")
                            hn = fw.op(V, V.tensor_scalar, ot[:, dc, :],
                                       xt[:, dc, :], gain[:, dc:dc + 1],
                                       bias[:, dc:dc + 1], ALU.mult, ALU.add,
                                       sem_name="dve", waits=(ot_free[dc],))
                        xt_free[dc] = hn
                        ot_free[dc] = fw.dma(
                            SY, dst_d[b, dc * P:(dc + 1) * P, t0:t0 + 512],
                            ot[:, dc, :], f"lno{dc}", waits=(hn,))
                    ps_free[2] = ps_free[3] = fw.h("dve")

    fw.new_phase()

    def x_tile(b, dc, t0, tile_ap, waits):
        return fw.dma(SY, tile_ap, x_in[b, dc * P:(dc + 1) * P, t0:t0 + 512],
                      f"lnx{dc}", waits=waits)

    emit_ln(x_tile, xn_d, src_f32=True)

    if limit < 2:
        return finish()

    class Rot:
        def __init__(self, idxs):
            self.idxs = list(idxs)
            self.i = 0

        def get(self):
            idx = self.idxs[self.i % len(self.idxs)]
            self.i += 1
            fw.wait(T, ps_free[idx])
            return idx

    fw.new_phase()
    ctx2 = ExitStack()
    e2 = ctx2.enter_context
    if True:
        wqTb = e2(nc.sbuf_tensor("wqTb", [P, NCH, D], BF16))
        wkTb = e2(nc.sbuf_tensor("wkTb", [P, NCH, D], BF16))
        wvTb = e2(nc.sbuf_tensor("wvTb", [P, NCH, D], BF16))
        woTb = e2(nc.sbuf_tensor("woTb", [P, NCH, D], BF16))
        bqr = e2(nc.sbuf_tensor("bqr", [1, D], BF16))
        bkr = e2(nc.sbuf_tensor("bkr", [1, D], BF16))
        bvr = e2(nc.sbuf_tensor("bvr", [1, D], BF16))
        xnf = e2(nc.sbuf_tensor("xnf", [P, NCH, S], BF16))
        kfr = e2(nc.sbuf_tensor("kfr", [P, 16, D], BF16))
        vfr = e2(nc.sbuf_tensor("vfr", [P, 16, D], BF16))
        kvall = e2(nc.sbuf_tensor("kvall", [P, 8 * 65], BF16))
        qtile = e2(nc.sbuf_tensor("qtile", [P, 2, D], BF16))
        qf2 = e2(nc.sbuf_tensor("qf2", [P, S], BF16))
        nume = e2(nc.sbuf_tensor("nume", [64, S], BF16))
        denr = e2(nc.sbuf_tensor("denr", [1, S], F32))
        rdenb = e2(nc.sbuf_tensor("rdenb", [1, S], BF16))
        cmul = e2(nc.sbuf_tensor("cmul", [64, 512], BF16))
        cct = e2(nc.sbuf_tensor("cct", [P, NCH, 512], BF16))
        stile = e2(nc.sbuf_tensor("stile", [P, 2, 512], BF16))
        for wb_, wT in [(wqTb, WqT), (wkTb, WkT), (wvTb, WvT), (woTb, WoT)]:
            fw.dma(G, wb_[:], wT.rearrange("(c p) m -> p c m", p=P), "w2l")
        for rb_, rr_ in [(bqr, bq_row), (bkr, bk_row), (bvr, bv_row)]:
            fw.dma(G, rb_[:], rr_, "w2l")
        hwld = fw.ht("w2l", "g")
        rot6 = Rot(range(6))
        qt_free = [None, None]
        kv_free = None
        qf2_free = None
        nume_free = None
        denr_free = None
        rdenb_free = None
        cmul_free = None
        cct_free = [None] * NCH
        st_free = [None, None]
        for b in range(NB):
            hxn = fw.dma(SY, xnf[:], xn_d[b].rearrange("(c p) t -> p c t", p=P),
                         "xnf", waits=(fw.h("pe"), fw.h("act")))
            for tc in range(16):
                for proj in range(3):
                    wb_ = (wqTb, wkTb, wvTb)[proj]
                    br_ = (bqr, bkr, bvr)[proj]
                    for ms in range(2):
                        psi = rot6.get()
                        fw.wait(T, hwld, hxn)
                        BIAS_MM = True
                        if BIAS_MM:
                            T.matmul(PS[psi][:], ones_bf[:],
                                     br_[0:1, ms * 512:(ms + 1) * 512],
                                     start=True, stop=False)
                        for dc in range(NCH):
                            inst = T.matmul(PS[psi][:],
                                            xnf[:, dc, tc * P:(tc + 1) * P],
                                            wb_[:, dc, ms * 512:(ms + 1) * 512],
                                            start=(dc == 0 and not BIAS_MM),
                                            stop=(dc == NCH - 1))
                        hmm = fw.inc(inst, "pe")
                        if proj == 0:
                            dst = qtile[:, tc % 2, ms * 512:(ms + 1) * 512]
                            wfree = qt_free[tc % 2]
                        elif proj == 1:
                            dst = kfr[:, tc, ms * 512:(ms + 1) * 512]
                            wfree = None
                        else:
                            dst = vfr[:, tc, ms * 512:(ms + 1) * 512]
                            wfree = None
                        if proj < 2:
                            hev = fw.op(A, A.activation, dst, PS[psi][:], AF.Relu,
                                        scale=dec_s[:, tc:tc + 1], sem_name="act",
                                        waits=(hmm, wfree))
                        else:
                            hev = fw.op(A, A.activation, dst, PS[psi][:],
                                        AF.Identity, sem_name="act",
                                        waits=(hmm, wfree))
                        ps_free[psi] = hev
                qt_free[tc % 2] = fw.dma(
                    SY, qf_d[b, tc * P:(tc + 1) * P, :], qtile[:, tc % 2, :],
                    f"qt{tc % 2}", waits=(fw.h("act"),))
            if p2_stop:
                continue
            p2a_done = fw.h("act")
            for h in range(16):
                psi = rot6.get()
                fw.wait(T, kv_free, p2a_done)
                for tc in range(16):
                    T.matmul(PS[psi][0:64, 0:64],
                             kfr[:, tc, h * 64:h * 64 + 64],
                             vfr[:, tc, h * 64:h * 64 + 64],
                             start=(tc == 0), stop=(tc == 15))
                for tc in range(16):
                    inst = T.matmul(PS[psi][0:64, 64:65],
                                    kfr[:, tc, h * 64:h * 64 + 64],
                                    onesc_bf[:], start=(tc == 0), stop=(tc == 15),
                                    skip_group_check=True)
                hmm = fw.inc(inst, "pe")
                rb0 = (h % 2) * 64
                hev = fw.op(A, A.activation,
                            kvall[rb0:rb0 + 64, (h // 2) * 65:(h // 2 + 1) * 65],
                            PS[psi][0:64, 0:65], AF.Identity, sem_name="act",
                            waits=(hmm,))
                ps_free[psi] = hev
            kv_done = fw.h("act")
            for hp in range(NCH):
                hqf = fw.dmat(SY, qf2[:], qf_d[b][:, hp * P:(hp + 1) * P], "qf2",
                              waits=(qf2_free, qt_free[0], qt_free[1]))
                for head in range(2):
                    hq = head * 64
                    hh = hp * 2 + head
                    pss = [rot6.get() for _ in range(2)]
                    for sl in range(4):
                        psi = pss[sl % 2]
                        if sl >= 2:
                            fw.wait(T, ps_free[psi])
                        fw.wait(T, kv_done, hqf)
                        inst = T.matmul(PS[psi][0:65, :],
                                        kvall[hq:hq + 64, hp * 65:(hp + 1) * 65],
                                        qf2[hq:hq + 64, sl * 512:(sl + 1) * 512],
                                        start=True, stop=True)
                        hmm = fw.inc(inst, "pe")
                        fw.op(A, A.activation, denr[0:1, sl * 512:(sl + 1) * 512],
                              PS[psi][64:65, :], AF.Identity, bias=eps6[0:1, :],
                              sem_name="act", waits=(hmm, denr_free))
                        hev = fw.op(A, A.activation,
                                    nume[:, sl * 512:(sl + 1) * 512],
                                    PS[psi][0:64, :], AF.Identity,
                                    sem_name="act", waits=(nume_free,))
                        ps_free[psi] = hev
                    hrec = fw.op(V, V.reciprocal, denr[:], denr[:],
                                 sem_name="dve", waits=(fw.h("act"),))
                    denr_free = hrec
                    hrd = fw.op(A, A.activation, rdenb[:], denr[:], AF.Identity,
                                sem_name="act", waits=(hrec, rdenb_free))
                    for sl in range(4):
                        psi = rot6.get()
                        fw.wait(T, hrd)
                        inst = T.matmul(PS[psi][0:64, :], ones_bf[0:1, 0:64],
                                        rdenb[0:1, sl * 512:(sl + 1) * 512],
                                        start=True, stop=True)
                        hmm = fw.inc(inst, "pe")
                        rdenb_free = hmm
                        hmul = fw.op(V, V.tensor_mul, cmul[:],
                                     nume[:, sl * 512:(sl + 1) * 512],
                                     PS[psi][0:64, :], sem_name="dve",
                                     waits=(hmm, cmul_free))
                        ps_free[psi] = hmul
                        nume_free = hmul
                        cmul_free = fw.dma(
                            SY, scr["ccos_d"][b, hp * P + hq:hp * P + hq + 64,
                                              sl * 512:(sl + 1) * 512],
                            cmul[:], "cm", waits=(hmul,))
                qf2_free = fw.h("pe")
            cc_done = fw.ht("cm")
            for ts in range(4):
                hcc = []
                for hc in range(NCH):
                    hcc.append(fw.dma(
                        SY, cct[:, hc, :],
                        scr["ccos_d"][b, hc * P:(hc + 1) * P,
                                      ts * 512:(ts + 1) * 512],
                        f"cct{hc}", waits=(cc_done, cct_free[hc])))
                for dc in range(NCH):
                    psi = rot6.get()
                    fw.wait(T, *hcc)
                    for hc in range(NCH):
                        inst = T.matmul(PS[psi][:], woTb[:, hc, dc * P:(dc + 1) * P],
                                        cct[:, hc, :], start=(hc == 0),
                                        stop=(hc == NCH - 1))
                    hmm = fw.inc(inst, "pe")
                    sb = dc % 2
                    hev = fw.op(A, A.activation, stile[:, sb, :], PS[psi][:],
                                AF.Identity, bias=pcs["bo_pc"][:, dc:dc + 1],
                                sem_name="act", waits=(hmm, st_free[sb]))
                    ps_free[psi] = hev
                    st_free[sb] = fw.dma(
                        SY, s_d[b, dc * P:(dc + 1) * P, ts * 512:(ts + 1) * 512],
                        stile[:, sb, :], f"sst{sb}", waits=(hev,))
                for hc in range(NCH):
                    cct_free[hc] = fw.h("pe")

    ctx2.close()
    fw.new_phase()

    if limit < 3:
        return finish()

    def xn_tile(b, dc, t0, tile_ap, waits):
        return fw.dma(SY, tile_ap, xn_d[b, dc * P:(dc + 1) * P, t0:t0 + 512],
                      f"lnx{dc}", waits=waits)

    fw.new_phase()
    emit_ln(xn_tile, x1_d, gain=pcs["ln1g_pc"], bias=pcs["ln1b_pc"])

    fw.new_phase()
    ctx3 = ExitStack()
    e3 = ctx3.enter_context
    if True:
        wqb3 = e3(nc.sbuf_tensor("wqb3", [P, NCH, D], BF16))
        wkb3 = e3(nc.sbuf_tensor("wkb3", [P, NCH, D], BF16))
        wvb3 = e3(nc.sbuf_tensor("wvb3", [P, NCH, D], BF16))
        x1f = e3(nc.sbuf_tensor("x1f", [P, NCH, S], BF16))
        qmt = e3(nc.sbuf_tensor("qmt", [P, 2, S], BF16))
        vt3 = e3(nc.sbuf_tensor("vt3", [P, 2, D], BF16))
        for wb_, win in [(wqb3, fwq), (wkb3, fwk), (wvb3, fwv)]:
            fw.dma(G, wb_[:], win.rearrange("(c p) m -> p c m", p=P), "w3l")
        hwld3 = fw.ht("w3l", "g")
        rot = Rot(range(6))
        qm_free = [None, None]
        vt_free = [None, None]
        for b in range(NB):
            hx1 = fw.dma(SY, x1f[:], x1_d[b].rearrange("(c p) t -> p c t", p=P),
                         "x1f", waits=(fw.h("pe"),))
            for wb_, dst_d in [(wqb3, qa_d), (wkb3, ka_d)]:
                for mc in range(NCH):
                    for ts in range(4):
                        psi = rot.get()
                        fw.wait(T, hwld3, hx1)
                        for dc in range(NCH):
                            inst = T.matmul(PS[psi][:],
                                            wb_[:, dc, mc * P:(mc + 1) * P],
                                            x1f[:, dc, ts * 512:(ts + 1) * 512],
                                            start=(dc == 0), stop=(dc == NCH - 1))
                        hmm = fw.inc(inst, "pe")
                        hev = fw.op(A, A.activation,
                                    qmt[:, mc % 2, ts * 512:(ts + 1) * 512],
                                    PS[psi][:], AF.Identity, sem_name="act",
                                    waits=(hmm, qm_free[mc % 2]))
                        ps_free[psi] = hev
                    qm_free[mc % 2] = fw.dma(
                        SY, dst_d[b, mc * P:(mc + 1) * P, :], qmt[:, mc % 2, :],
                        f"qm{mc % 2}", waits=(fw.h("act"),))
            for tc in range(16):
                for hp in range(NCH):
                    psi = rot.get()
                    fw.wait(T, hwld3, hx1)
                    for dc in range(NCH):
                        inst = T.matmul(PS[psi][0:P, 0:P],
                                        x1f[:, dc, tc * P:(tc + 1) * P],
                                        wvb3[:, dc, hp * P:(hp + 1) * P],
                                        start=(dc == 0), stop=(dc == NCH - 1))
                    hmm = fw.inc(inst, "pe")
                    hev = fw.op(A, A.activation, vt3[:, tc % 2, hp * P:(hp + 1) * P],
                                PS[psi][0:P, 0:P], AF.Identity, sem_name="act",
                                waits=(hmm, vt_free[tc % 2]))
                    ps_free[psi] = hev
                for hp in range(NCH):
                    fw.dma(SY, v_d[b, hp, tc * P:(tc + 1) * P, :],
                           vt3[:, tc % 2, hp * P:(hp + 1) * P], f"vt{tc % 2}",
                           waits=(fw.h("act"),))
                vt_free[tc % 2] = fw.ht(f"vt{tc % 2}")
    ctx3.close()

    fw.new_phase()
    ctx4 = ExitStack()
    e4 = ctx4.enter_context
    if True:
        q2 = e4(nc.sbuf_tensor("q2", [P, S], BF16))
        k2 = e4(nc.sbuf_tensor("k2", [P, S], BF16))
        v2 = e4(nc.sbuf_tensor("v2", [P, 16, P], BF16))
        p2sl = e4(nc.sbuf_tensor("p2sl", [P, PP + 1], BF16))
        sqb = e4(nc.sbuf_tensor("sqb", [P, S], BF16))
        S_f = e4(nc.sbuf_tensor("S_f", [P, S], F32))
        ful = e4(nc.sbuf_tensor("ful", [P, 2176], BF16))
        Ptile = e4(nc.sbuf_tensor("Ptile", [P, S], BF16))
        denall = e4(nc.sbuf_tensor("denall", [P, 16], F32))
        den16 = e4(nc.sbuf_tensor("den16", [16, P], F32))
        rdenr = e4(nc.sbuf_tensor("rdenr", [1, S], F32))
        rdenb2 = e4(nc.sbuf_tensor("rdenb2", [1, S], BF16))
        negBc = e4(nc.sbuf_tensor("negBc", [P, 1], F32))
        bnds = e4(nc.sbuf_tensor("bnds", [1, 8], F32))
        nrow = e4(nc.sbuf_tensor("nrow", [1, S], F32))
        PT = [e4(nc.sbuf_tensor(f"PT{i}", [P, S], BF16)) for i in range(2)]
        zsl = e4(nc.sbuf_tensor("zsl", [64, 512], BF16))
        zmul = e4(nc.sbuf_tensor("zmul", [64, 512], BF16))
        bd_t = scr["bd_d"].tensor
        den_t = scr["den_d"].tensor
        rot = Rot(range(4))
        ld_free = None
        denall_free = None
        den16_free = None
        rdenr_free = None
        rdenb2_free = None
        nrow_free = None
        bnds_free = None
        ful_free = None
        ring_free = [None] * 4
        Ptile_free = None
        PT_free = [None, None]
        zsl_free = None
        zmul_free = None
        pd_free = [None, None]
        pd = 0
        for b in range(NB):
            for hp in range(NCH):
                hq2 = fw.dma(SY, q2[:], qa_d[b, hp * P:(hp + 1) * P, :], "q2",
                             waits=(ld_free,))
                hk2 = fw.dma(SY, k2[:], ka_d[b, hp * P:(hp + 1) * P, :], "k2")
                hv2 = fw.dma(SY, v2[:],
                             v_d[b, hp].rearrange("(c p) f -> p c f", p=P), "v2")
                hp2 = fw.dma(SY, p2sl[:], p2_d[hp * P:(hp + 1) * P, :], "p2l")
                for src_, col in [(q2, 0), (k2, 1)]:
                    hsq = fw.op(A, A.activation, sqb[:], src_[:], AF.Square,
                                sem_name="act",
                                waits=(hq2 if col == 0 else hk2, fw.h("pe")))
                    for sl in range(4):
                        psi = rot.get()
                        fw.wait(T, hsq)
                        inst = T.matmul(PS[psi][0:1, 0:512], onesc_bf[:],
                                        sqb[:, sl * 512:(sl + 1) * 512],
                                        start=True, stop=True)
                        hmm = fw.inc(inst, "pe")
                        hev = fw.op(A, A.activation,
                                    nrow[0:1, sl * 512:(sl + 1) * 512],
                                    PS[psi][0:1, 0:512], AF.Identity,
                                    sem_name="act", waits=(hmm, nrow_free))
                        ps_free[psi] = hev
                    nrow_free = fw.op(V, V.reduce_max, bnds[0:1, col:col + 1],
                                      nrow[:], axis=AX.X, sem_name="dve",
                                      waits=(fw.h("act"), bnds_free))
                fw.op(A, A.activation, bnds[0:1, 3:4], bnds[0:1, 0:1], AF.Sqrt,
                      sem_name="act", waits=(fw.h("dve"),))
                fw.op(A, A.activation, bnds[0:1, 4:5], bnds[0:1, 1:2], AF.Sqrt,
                      sem_name="act")
                fw.op(A, A.activation, bnds[0:1, 5:6], maxp2[0:1, hp:hp + 1],
                      AF.Sqrt, sem_name="act")
                fw.op(V, V.tensor_add, bnds[0:1, 6:7], bnds[0:1, 4:5],
                      bnds[0:1, 5:6], sem_name="dve", waits=(fw.h("act"),))
                fw.op(V, V.tensor_mul, bnds[0:1, 6:7], bnds[0:1, 6:7],
                      bnds[0:1, 3:4], sem_name="dve")
                hB = fw.op(V, V.tensor_scalar, bnds[0:1, 7:8], bnds[0:1, 6:7],
                           -0.125, None, ALU.mult, sem_name="dve")
                psi = rot.get()
                fw.wait(T, hB)
                inst = T.matmul(PS[psi][0:P, 0:1], ones_f[:], bnds[0:1, 7:8],
                                start=True, stop=True)
                hmm = fw.inc(inst, "pe")
                bnds_free = hmm
                hnB = fw.op(A, A.activation, negBc[:], PS[psi][0:P, 0:1],
                            AF.Identity, sem_name="act", waits=(hmm,))
                ps_free[psi] = hnB
                for head in range(2):
                    hq = head * 64
                    for it in range(16):
                        base = (S - P) - it * P
                        off = 0
                        for w_ in (512, 512, 512, 512, 128):
                            psi = rot.get()
                            fw.wait(T, hq2, hp2)
                            inst = T.matmul(PS[psi][0:P, 0:w_],
                                            q2[hq:hq + 64, it * P:(it + 1) * P],
                                            p2sl[hq:hq + 64,
                                                 base + off:base + off + w_],
                                            start=True, stop=True)
                            hmm = fw.inc(inst, "pe")
                            hev = fw.op(A, A.activation, ful[:, off:off + w_],
                                        PS[psi][0:P, 0:w_], AF.Identity,
                                        sem_name="act", waits=(hmm, ful_free))
                            ps_free[psi] = hev
                            off += w_
                        rg = it % 4
                        hfs = fw.dma(SY, scr["bd_d"][rg], ful[:], "bds",
                                     waits=(fw.h("act"), ring_free[rg]))
                        ful_free = hfs
                        for sl in range(4):
                            psi = rot.get()
                            fw.wait(T, hk2)
                            inst = T.matmul(PS[psi][0:P, 0:512],
                                            q2[hq:hq + 64, it * P:(it + 1) * P],
                                            k2[hq:hq + 64, sl * 512:(sl + 1) * 512],
                                            start=True, stop=True)
                            hmm = fw.inc(inst, "pe")
                            hev = fw.op(A, A.activation,
                                        S_f[:, sl * 512:(sl + 1) * 512],
                                        PS[psi][0:P, 0:512], AF.Identity,
                                        sem_name="act", waits=(hmm,))
                            ps_free[psi] = hev
                        shift_ap = AP(bd_t, rg * P * 2176 + 127,
                                      [[2175, P], [1, S]])
                        hacc = fw.dma(G, S_f[:], shift_ap, "acc",
                                      waits=(hfs, fw.h("act")),
                                      accum_op=ALU.add)
                        ring_free[rg] = hacc
                        hexp = fw.op(A, A.activation, Ptile[:], S_f[:], AF.Exp,
                                     bias=negBc[:], scale=0.125,
                                     accum_out=denall[:, it:it + 1],
                                     sem_name="act",
                                     waits=(hacc, hnB, Ptile_free, denall_free))
                        Ptile_free = fw.dma(SY, P_d[pd, it * P:(it + 1) * P, :],
                                            Ptile[:], "Pst",
                                            waits=(hexp, pd_free[pd]))
                    psi = rot.get()
                    fw.wait(T, fw.h("act"))
                    inst = T.transpose(PS[psi][0:16, 0:P], denall[:], eyef_s[:])
                    hmm = fw.inc(inst, "pe")
                    denall_free = hmm
                    hd16 = fw.op(A, A.activation, den16[:], PS[psi][0:16, 0:P],
                                 AF.Identity, sem_name="act",
                                 waits=(hmm, den16_free))
                    ps_free[psi] = hd16
                    den16_free = fw.dma(SY, den_d[:], den16[:], "dns",
                                        waits=(hd16,))
                    flat_ap = AP(den_t, 0, [[S, 1], [1, S]])
                    hdf = fw.dma(SY, rdenr[:], flat_ap, "dnl",
                                 waits=(rdenr_free, den16_free))
                    hrc = fw.op(V, V.reciprocal, rdenr[:], rdenr[:],
                                sem_name="dve", waits=(hdf,))
                    hrd = fw.op(A, A.activation, rdenb2[:], rdenr[:], AF.Identity,
                                sem_name="act", waits=(hrc, rdenb2_free))
                    rdenr_free = hrd
                    for jc in range(16):
                        pb = jc % 2
                        hpt = fw.dmat(SY, PT[pb][:], P_d[pd, 0:S, jc * P:(jc + 1) * P],
                                      f"pt{pb}", waits=(PT_free[pb], Ptile_free))
                        fw.wait(T, hpt, hv2)
                        if jc == 0:
                            fw.wait(T, ps_free[4], ps_free[5], ps_free[6],
                                    ps_free[7])
                        for isl in range(4):
                            inst = T.matmul(PS[4 + isl][0:64, 0:512],
                                            v2[:, jc, hq:hq + 64],
                                            PT[pb][:, isl * 512:(isl + 1) * 512],
                                            start=(jc == 0), stop=(jc == 15))
                        PT_free[pb] = fw.inc(inst, "pe")
                    hV = fw.h("pe")
                    for isl in range(4):
                        psi = rot.get()
                        fw.wait(T, hrd)
                        inst = T.matmul(PS[psi][0:64, 0:512], ones_bf[0:1, 0:64],
                                        rdenb2[0:1, isl * 512:(isl + 1) * 512],
                                        start=True, stop=True)
                        hmm = fw.inc(inst, "pe")
                        rdenb2_free = hmm
                        hz = fw.op(A, A.activation, zsl[:],
                                   PS[4 + isl][0:64, 0:512], AF.Identity,
                                   sem_name="act", waits=(hV, zsl_free))
                        ps_free[4 + isl] = hz
                        hmul = fw.op(V, V.tensor_mul, zmul[:], zsl[:],
                                     PS[psi][0:64, 0:512], sem_name="dve",
                                     waits=(hmm, hz, zmul_free))
                        ps_free[psi] = hmul
                        zsl_free = hmul
                        zmul_free = fw.dma(
                            SY, scr["ccos_d"][b, hp * P + hq:hp * P + hq + 64,
                                              isl * 512:(isl + 1) * 512],
                            zmul[:], "zst", waits=(hmul,))
                    pd_free[pd] = hV
                    pd = 1 - pd
                ld_free = fw.h("pe")
    ctx4.close()

    fw.new_phase()
    ctx5 = ExitStack()
    e5 = ctx5.enter_context
    if True:
        wob = e5(nc.sbuf_tensor("wob", [P, NCH, D], BF16))
        cc3 = e5(nc.sbuf_tensor("cc3", [P, NCH, 512], BF16))
        xnt3 = e5(nc.sbuf_tensor("xnt3", [P, 2, 512], BF16))
        srct = e5(nc.sbuf_tensor("srct", [P, 2, 512], BF16))
        hwo = fw.dma(G, wob[:], fwo.rearrange("(c p) m -> p c m", p=P), "wol")
        rot = Rot(range(6))
        cc3_free = [None] * NCH
        xnt_free = [None, None]
        srct_free = [None, None]
        for b in range(NB):
            for ts in range(4):
                hcc = []
                for hc in range(NCH):
                    hcc.append(fw.dma(
                        SY, cc3[:, hc, :],
                        scr["ccos_d"][b, hc * P:(hc + 1) * P,
                                      ts * 512:(ts + 1) * 512],
                        f"cc{hc}", waits=(cc3_free[hc],)))
                for dc in range(NCH):
                    psi = rot.get()
                    fw.wait(T, hwo, *hcc)
                    for hc in range(NCH):
                        inst = T.matmul(PS[psi][:], wob[:, hc, dc * P:(dc + 1) * P],
                                        cc3[:, hc, :], start=(hc == 0),
                                        stop=(hc == NCH - 1))
                    hmm = fw.inc(inst, "pe")
                    sb = dc % 2
                    hxnt = fw.dma(SY, xnt3[:, sb, :],
                                  xn_d[b, dc * P:(dc + 1) * P,
                                       ts * 512:(ts + 1) * 512],
                                  f"xt{sb}", waits=(xnt_free[sb],))
                    hadd = fw.op(V, V.tensor_add, srct[:, sb, :], xnt3[:, sb, :],
                                 PS[psi][:], sem_name="dve",
                                 waits=(hmm, hxnt, srct_free[sb]))
                    ps_free[psi] = hadd
                    xnt_free[sb] = hadd
                    srct_free[sb] = fw.dma(
                        SY, src2_d[b, dc * P:(dc + 1) * P, ts * 512:(ts + 1) * 512],
                        srct[:, sb, :], f"sr{sb}", waits=(hadd,))
                for hc in range(NCH):
                    cc3_free[hc] = fw.h("pe")
    ctx5.close()

    def src2_tile(b, dc, t0, tile_ap, waits):
        return fw.dma(SY, tile_ap, src2_d[b, dc * P:(dc + 1) * P, t0:t0 + 512],
                      f"lnx{dc}", waits=waits)

    fw.new_phase()
    emit_ln(src2_tile, x2_d, gain=pcs["ln2g_pc"], bias=pcs["ln2b_pc"])

    fw.new_phase()
    ctx6 = ExitStack()
    e6 = ctx6.enter_context
    if True:
        x2f = e6(nc.sbuf_tensor("x2f", [P, NCH, S], BF16))
        selb = e6(nc.sbuf_tensor("selb", [P, NCH, 32], BF16))
        selt = e6(nc.sbuf_tensor("selt", [P, 32], F32))
        curt = e6(nc.sbuf_tensor("curt", [P, 32], F32))
        mc1 = e6(nc.sbuf_tensor("mc1", [P, 1], F32))
        gatesV = e6(nc.sbuf_tensor("gatesV", [P, 32], F32))
        gatesA = e6(nc.sbuf_tensor("gatesA", [P, 32], F32))
        gall = e6(nc.sbuf_tensor("gall", [32, S], F32))
        grow = e6(nc.sbuf_tensor("grow", [1, 32 * S], BF16))
        keyb = [e6(nc.sbuf_tensor(f"keyb{i}", [P, NCH, P], BF16)) for i in range(2)]
        valb = [e6(nc.sbuf_tensor(f"valb{i}", [P, D], BF16)) for i in range(2)]
        hrb = e6(nc.sbuf_tensor("hrb", [P, 2, 256], BF16))
        hgb = e6(nc.sbuf_tensor("hgb", [P, 2, 256], BF16))
        hgl = e6(nc.sbuf_tensor("hgl", [P, 2, 256], BF16))
        s2t = e6(nc.sbuf_tensor("s2t", [P, 2, 256], BF16))
        sst = e6(nc.sbuf_tensor("sst", [P, 2, 256], BF16))
        xft = e6(nc.sbuf_tensor("xft", [P, 2, 256], F32))
        ut = e6(nc.sbuf_tensor("ut", [P, 2, 256], F32))
        gt = e6(nc.sbuf_tensor("gt", [P, 2, 256], BF16))
        t1t = e6(nc.sbuf_tensor("t1t", [P, 2, 256], F32))
        xot = e6(nc.sbuf_tensor("xot", [P, 2, 256], F32))
        hsel = fw.dma(G, selb[:], sel_w.rearrange("(c p) e -> p c e", p=P), "sel")
        rot = Rot(range(3))
        selt_free = None
        gall_free = None
        key_free = [None, None]
        hload = [None, None]
        out_free = [None] * NCH
        cmb_free = [[None, None] for _ in range(7)]
        for b in range(NB):
            hx2 = fw.dma(SY, x2f[:], x2_d[b].rearrange("(c p) t -> p c t", p=P),
                         "x2f", waits=(fw.h("pe"),))
            for tc in range(16):
                psi = rot.get()
                fw.wait(T, hsel, hx2)
                for dc in range(NCH):
                    inst = T.matmul(PS[psi][0:P, 0:32],
                                    x2f[:, dc, tc * P:(tc + 1) * P],
                                    selb[:, dc, :], start=(dc == 0),
                                    stop=(dc == NCH - 1))
                hmm = fw.inc(inst, "pe")
                fw.op(A, A.activation, selt[:], PS[psi][0:P, 0:32], AF.Sigmoid,
                      sem_name="act", waits=(hmm, selt_free))
                fw.op(V, V.tensor_copy, curt[:], selt[:], sem_name="dve",
                      waits=(fw.h("act"),))
                for _ in range(3):
                    fw.op(V, V.reduce_max, mc1[:], curt[:], axis=AX.X,
                          sem_name="dve")
                    fw.op(V, V.scalar_tensor_tensor, curt[:], curt[:], mc1[:],
                          curt[:], ALU.is_lt, ALU.mult, sem_name="dve")
                fw.op(V, V.reduce_max, mc1[:], curt[:], axis=AX.X, sem_name="dve")
                fw.op(V, V.scalar_tensor_tensor, gatesV[:], selt[:], mc1[:],
                      selt[:], ALU.is_ge, ALU.mult, sem_name="dve")
                selt_free = fw.h("dve")
                hga = fw.op(A, A.activation, gatesA[:], gatesV[:], AF.Identity,
                            sem_name="act", waits=(selt_free,))
                ps_free[psi] = hga
                psi2 = rot.get()
                fw.wait(T, hga)
                inst = T.transpose(PS[psi2][0:32, 0:P], gatesA[:], eyef_s[:])
                hmm2 = fw.inc(inst, "pe")
                hgl_ = fw.op(A, A.activation, gall[:, tc * P:(tc + 1) * P],
                             PS[psi2][0:32, 0:P], AF.Identity, sem_name="act",
                             waits=(hmm2, gall_free))
                ps_free[psi2] = hgl_
            hgst = fw.dma(SY, gates_d[b], gall[:], "gst", waits=(fw.h("act"),))
            gall_free = hgst
            for e in range(32):
                fw.dma(G, grow[0:1, e * S:(e + 1) * S], gates_d[b, e:e + 1, :],
                       "grw", waits=(hgst, fw.h("pe")))
            hgrow = fw.ht("grw", "g")
            for tt in range(NB * 4):
                to = tt * 256
                for e in range(32):
                    r = e % 2
                    if e == 0 and tt == 0 and b == 0:
                        hk0 = fw.dma(SY, keyb[0][:].rearrange(
                            "p c f -> p (c f)"), keyp_d[0], "ky0",
                            waits=(kpc_done,))
                        hload[0] = (hk0, fw.dma(SY, valb[0][:], valp_d[0],
                                                "vl0"))
                    if not (e == 31 and tt == NB * 4 - 1 and b == NB - 1):
                        nxt = (e + 1) % 32
                        rn = nxt % 2
                        fw.wait(SY, key_free[rn])
                        hkn = fw.dma(SY, keyb[rn][:].rearrange(
                            "p c f -> p (c f)"), keyp_d[nxt], f"ky{rn}",
                            waits=(kpc_done,))
                        hload[rn] = (hkn,
                                     fw.dma(SY, valb[rn][:], valp_d[nxt],
                                            f"vl{rn}"))
                    fw.wait(T, hgrow, ps_free[3])
                    inst = T.matmul(PS[3][0:P, 0:256], ones_bf[:],
                                    grow[0:1, e * S + to:e * S + to + 256],
                                    start=True, stop=True)
                    hgb_ = fw.inc(inst, "pe")
                    psi_h = e % 2
                    fw.wait(T, ps_free[psi_h], hx2, *hload[r])
                    for dc in range(NCH):
                        inst = T.matmul(PS[psi_h][0:P, 0:256],
                                        keyb[r][:, dc, :],
                                        x2f[:, dc, to:to + 256],
                                        start=(dc == 0), stop=(dc == NCH - 1))
                    hmm = fw.inc(inst, "pe")
                    hr_ = fw.op(A, A.activation, hrb[:, e % 2, :],
                                PS[psi_h][0:P, 0:256], AF.Relu, sem_name="act",
                                waits=(hmm,))
                    ps_free[psi_h] = hr_
                    hg_ = fw.op(V, V.tensor_mul, hgb[:, e % 2, :],
                                hrb[:, e % 2, :], PS[3][0:P, 0:256],
                                sem_name="dve", waits=(hgb_, hr_))
                    ps_free[3] = hg_
                    hl_ = fw.op(A, A.activation, hgl[:, e % 2, :],
                                hgb[:, e % 2, :], AF.Identity, sem_name="act",
                                waits=(hg_,))
                    fw.wait(T, hl_)
                    if e == 0:
                        fw.wait(T, *[out_free[i] for i in range(NCH)])
                    for dc in range(NCH):
                        pt = 4 + dc % 4
                        rg0 = (dc // 4) * 256
                        inst = T.matmul(PS[pt][0:P, rg0:rg0 + 256],
                                        valb[r][:, dc * P:(dc + 1) * P],
                                        hgl[:, e % 2, :],
                                        start=(e == 0 and dc < 4),
                                        stop=(e == 31 and dc >= 4),
                                        skip_group_check=True)
                    key_free[r] = fw.inc(inst, "pe")
                hMoE = fw.h("pe")
                for dc in range(NCH):
                    pt = 4 + dc % 4
                    rg0 = (dc // 4) * 256
                    cb = dc % 2
                    h1 = fw.dma(SY, s2t[:, cb, :],
                                src2_d[b, dc * P:(dc + 1) * P, to:to + 256],
                                f"cb1{cb}", waits=(cmb_free[0][cb],))
                    h2 = fw.dma(SY, sst[:, cb, :],
                                s_d[b, dc * P:(dc + 1) * P, to:to + 256],
                                f"cb2{cb}", waits=(cmb_free[1][cb],))
                    h3 = fw.dma(SY, xft[:, cb, :],
                                x_in[b, dc * P:(dc + 1) * P, to:to + 256],
                                f"cb3{cb}", waits=(cmb_free[2][cb],))
                    hu = fw.op(V, V.tensor_add, ut[:, cb, :], s2t[:, cb, :],
                               PS[pt][0:P, rg0:rg0 + 256], sem_name="dve",
                               waits=(hMoE, h1, cmb_free[3][cb]))
                    out_free[dc] = hu
                    hgsl = fw.op(A, A.activation, gt[:, cb, :], ut[:, cb, :],
                                 AF.Silu, sem_name="act",
                                 waits=(hu, cmb_free[4][cb]))
                    ht1 = fw.op(V, V.tensor_mul, t1t[:, cb, :], sst[:, cb, :],
                                gt[:, cb, :], sem_name="dve",
                                waits=(hgsl, h2))
                    hxo = fw.op(V, V.tensor_add, xot[:, cb, :], xft[:, cb, :],
                                t1t[:, cb, :], sem_name="dve",
                                waits=(h3, cmb_free[5][cb], cmb_free[6][cb]))
                    hst1 = fw.dma(SY, xnew_d[b, dc * P:(dc + 1) * P, to:to + 256],
                                  xot[:, cb, :], f"xs{cb}", waits=(hxo,))
                    hst2 = fw.dma(G, xnewb_d[b, dc * P:(dc + 1) * P, to:to + 256],
                                  xot[:, cb, :], f"xp{cb}", waits=(hxo,))
                    cmb_free[0][cb] = hu
                    cmb_free[1][cb] = ht1
                    cmb_free[2][cb] = hxo
                    cmb_free[3][cb] = hgsl
                    cmb_free[4][cb] = ht1
                    cmb_free[5][cb] = hst2
                    cmb_free[6][cb] = hst1
    ctx6.close()


    if limit < 4:
        return finish()

    def xnb_tile(b, dc, t0, tile_ap, waits):
        return fw.dma(SY, tile_ap, xnewb_d[b, dc * P:(dc + 1) * P, t0:t0 + 512],
                      f"lnx{dc}", waits=waits)

    emit_ln(xnb_tile, xn1_d)

    fw.new_phase()
    ctx7 = ExitStack()
    e7 = ctx7.enter_context
    if True:
        aqb = [e7(nc.sbuf_tensor(f"aqb{i}", [P, NCH, 64], BF16)) for i in range(3)]
        brow4 = e7(nc.sbuf_tensor("brow4", [1, 3 * 64], BF16))
        awob = e7(nc.sbuf_tensor("awob", [64, D], BF16))
        xn1f = e7(nc.sbuf_tensor("xn1f", [P, NCH, S], BF16))
        qkvf = e7(nc.sbuf_tensor("qkvf", [P, 192], F32))
        v4 = e7(nc.sbuf_tensor("v4", [P, 16, 64], BF16))
        qfm4 = e7(nc.sbuf_tensor("qfm4", [64, S], BF16))
        kfm4 = e7(nc.sbuf_tensor("kfm4", [64, S], BF16))
        S4 = e7(nc.sbuf_tensor("S4", [P, S], F32))
        m4r = e7(nc.sbuf_tensor("m4r", [P, 1], F32))
        nm4 = e7(nc.sbuf_tensor("nm4", [P, 1], F32))
        Pt4 = e7(nc.sbuf_tensor("Pt4", [P, S], BF16))
        dal4 = e7(nc.sbuf_tensor("dal4", [P, 16], F32))
        d164 = e7(nc.sbuf_tensor("d164", [16, P], F32))
        rdr4 = e7(nc.sbuf_tensor("rdr4", [1, S], F32))
        rdb4 = e7(nc.sbuf_tensor("rdb4", [1, S], BF16))
        PT4 = [e7(nc.sbuf_tensor(f"PT4{i}", [P, S], BF16)) for i in range(2)]
        zsl4 = e7(nc.sbuf_tensor("zsl4", [64, 512], BF16))
        zmul4 = e7(nc.sbuf_tensor("zmul4", [64, 512], BF16))
        zln4 = e7(nc.sbuf_tensor("zln4", [64, 512], BF16))
        s1t = e7(nc.sbuf_tensor("s1t", [P, 2, 512], BF16))
        for i, (win, brs) in enumerate([(a1wq, a1bq_row), (a1wk, a1bk_row),
                                        (a1wv, a1bv_row)]):
            fw.dma(G, aqb[i][:], win.rearrange("(c p) m -> p c m", p=P), "a4l")
            fw.dma(G, brow4[0:1, i * 64:(i + 1) * 64], brs, "a4l")
        fw.dma(G, awob[:], a1wo, "a4l")
        ha4 = fw.ht("a4l", "g")
        rot = Rot(range(4))
        den_t4 = scr["den_d"].tensor
        pd = 0
        pd_free = [None, None]
        Pt4_free = None
        PT4_free = [None, None]
        dal4_free = None
        d164_free = None
        rdr4_free = None
        rdb4_free = None
        zs4_free = None
        zm4_free = None
        zl4_free = None
        s1t_free = [None, None]
        xn1f_free = None
        for b in range(NB):
            hx4 = fw.dma(SY, xn1f[:], xn1_d[b].rearrange("(c p) t -> p c t", p=P),
                         "xn1f", waits=(xn1f_free,))
            for tc in range(16):
                psi = rot.get()
                fw.wait(T, hx4, ha4)
                for proj in range(3):
                    T.matmul(PS[psi][0:P, proj * 64:(proj + 1) * 64], ones_bf[:],
                             brow4[0:1, proj * 64:(proj + 1) * 64],
                             start=True, stop=False, skip_group_check=True)
                    for dc in range(NCH):
                        inst = T.matmul(PS[psi][0:P, proj * 64:(proj + 1) * 64],
                                        xn1f[:, dc, tc * P:(tc + 1) * P],
                                        aqb[proj][:, dc, :], start=False,
                                        stop=(dc == NCH - 1),
                                        skip_group_check=True)
                hmm = fw.inc(inst, "pe")
                hqk = fw.op(A, A.activation, qkvf[:], PS[psi][0:P, 0:192],
                            AF.Identity, sem_name="act", waits=(hmm, fw.h("pe")))
                hv4 = fw.op(A, A.activation, v4[:, tc, :], PS[psi][0:P, 128:192],
                            AF.Identity, sem_name="act")
                ps_free[psi] = hv4
                for qk in range(2):
                    psi2 = rot.get()
                    fw.wait(T, hqk)
                    inst = T.transpose(PS[psi2][0:64, 0:P],
                                       qkvf[:, qk * 64:(qk + 1) * 64], eyef_s[:])
                    hmm2 = fw.inc(inst, "pe")
                    dst = (qfm4 if qk == 0 else kfm4)[:, tc * P:(tc + 1) * P]
                    hev = fw.op(A, A.activation, dst, PS[psi2][0:64, 0:P],
                                AF.Identity, sem_name="act", waits=(hmm2,))
                    ps_free[psi2] = hev
            qkv_done = fw.h("act")
            S4_free = None
            nm_free = None
            for it in range(16):
                nsl = it // 4 + 1
                cols = nsl * 512
                for sl in range(nsl):
                    psi = rot.get()
                    fw.wait(T, qkv_done)
                    inst = T.matmul(PS[psi][0:P, 0:512],
                                    qfm4[:, it * P:(it + 1) * P],
                                    kfm4[:, sl * 512:(sl + 1) * 512],
                                    start=True, stop=True)
                    hmm = fw.inc(inst, "pe")
                    hev = fw.op(A, A.activation, S4[:, sl * 512:(sl + 1) * 512],
                                PS[psi][0:P, 0:512], AF.Identity,
                                sem_name="act", waits=(hmm, S4_free))
                    ps_free[psi] = hev
                doff = it * P - (nsl - 1) * 512
                hmask = fw.op(V, V.tensor_add, S4[:, (nsl - 1) * 512 + doff:
                                                 (nsl - 1) * 512 + doff + P],
                              S4[:, (nsl - 1) * 512 + doff:
                                 (nsl - 1) * 512 + doff + P],
                              trineg_s[:], sem_name="dve", waits=(fw.h("act"),))
                if doff + P < 512:
                    hmask = fw.op(V, V.memset,
                                  S4[:, (nsl - 1) * 512 + doff + P:cols], NEG,
                                  sem_name="dve")
                hmx = fw.op(V, V.reduce_max, m4r[:], S4[:, 0:cols], axis=AX.X,
                            sem_name="dve")
                hnm = fw.op(V, V.tensor_scalar, nm4[:], m4r[:], -0.125, None,
                            ALU.mult, sem_name="dve", waits=(nm_free,))
                S4_free = hnm
                hz0 = fw.op(V, V.memset, Pt4[:], 0.0, sem_name="dve",
                            waits=(Pt4_free,))
                hexp = fw.op(A, A.activation, Pt4[:, 0:cols], S4[:, 0:cols],
                             AF.Exp, bias=nm4[:], scale=0.125,
                             accum_out=dal4[:, it:it + 1], sem_name="act",
                             waits=(hnm, hz0, dal4_free))
                nm_free = hexp
                Pt4_free = fw.dma(SY, P_d[pd, it * P:(it + 1) * P, :], Pt4[:],
                                  "Ps4", waits=(hexp, pd_free[pd]))
            psi = rot.get()
            fw.wait(T, fw.h("act"))
            inst = T.transpose(PS[psi][0:16, 0:P], dal4[:], eyef_s[:])
            hmm = fw.inc(inst, "pe")
            dal4_free = hmm
            hd16 = fw.op(A, A.activation, d164[:], PS[psi][0:16, 0:P],
                         AF.Identity, sem_name="act", waits=(hmm, d164_free))
            ps_free[psi] = hd16
            d164_free = fw.dma(SY, den_d[:], d164[:], "dn4", waits=(hd16,))
            hdf = fw.dma(SY, rdr4[:], AP(den_t4, 0, [[S, 1], [1, S]]), "dl4",
                         waits=(rdr4_free, d164_free))
            hrc = fw.op(V, V.reciprocal, rdr4[:], rdr4[:], sem_name="dve",
                        waits=(hdf,))
            hrd = fw.op(A, A.activation, rdb4[:], rdr4[:], AF.Identity,
                        sem_name="act", waits=(hrc, rdb4_free))
            rdr4_free = hrd
            for jc in range(16):
                pb = jc % 2
                hpt = fw.dmat(SY, PT4[pb][:], P_d[pd, 0:S, jc * P:(jc + 1) * P],
                              f"pq{pb}", waits=(PT4_free[pb], Pt4_free))
                fw.wait(T, hpt, qkv_done)
                if jc == 0:
                    fw.wait(T, ps_free[4], ps_free[5], ps_free[6], ps_free[7])
                for isl in range(4):
                    inst = T.matmul(PS[4 + isl][0:64, 0:512], v4[:, jc, :],
                                    PT4[pb][:, isl * 512:(isl + 1) * 512],
                                    start=(jc == 0), stop=(jc == 15))
                PT4_free[pb] = fw.inc(inst, "pe")
            hV = fw.h("pe")
            pd_free[pd] = hV
            pd = 1 - pd
            for isl in range(4):
                psi = rot.get()
                fw.wait(T, hrd)
                inst = T.matmul(PS[psi][0:64, 0:512], ones_bf[0:1, 0:64],
                                rdb4[0:1, isl * 512:(isl + 1) * 512],
                                start=True, stop=True)
                hmm = fw.inc(inst, "pe")
                rdb4_free = hmm
                hz = fw.op(A, A.activation, zsl4[:], PS[4 + isl][0:64, 0:512],
                           AF.Identity, sem_name="act", waits=(hV, zs4_free))
                ps_free[4 + isl] = hz
                hmul = fw.op(V, V.tensor_mul, zmul4[:], zsl4[:],
                             PS[psi][0:64, 0:512], sem_name="dve",
                             waits=(hmm, hz, zm4_free))
                ps_free[psi] = hmul
                zs4_free = hmul
                hzl = fw.op(A, A.activation, zln4[:], zmul4[:], AF.Identity,
                            sem_name="act", waits=(hmul, zl4_free))
                zm4_free = hzl
                for dc in range(NCH):
                    psi3 = rot.get()
                    fw.wait(T, hzl, ha4)
                    inst = T.matmul(PS[psi3][0:P, 0:512],
                                    awob[:, dc * P:(dc + 1) * P], zln4[:],
                                    start=True, stop=True)
                    hmm3 = fw.inc(inst, "pe")
                    if dc == NCH - 1:
                        zl4_free = hmm3
                    sb = dc % 2
                    hse = fw.op(A, A.activation, s1t[:, sb, :], PS[psi3][0:P, 0:512],
                                AF.Identity, bias=pcs["a1bo_pc"][:, dc:dc + 1],
                                sem_name="act", waits=(hmm3, s1t_free[sb]))
                    ps_free[psi3] = hse
                    s1t_free[sb] = fw.dma(
                        SY, s1_d[b, dc * P:(dc + 1) * P,
                                 isl * 512:(isl + 1) * 512],
                        s1t[:, sb, :], f"s1s{sb}", waits=(hse,))
            xn1f_free = fw.h("pe")
    ctx7.close()

    def xn1_tile(b, dc, t0, tile_ap, waits):
        return fw.dma(SY, tile_ap, xn1_d[b, dc * P:(dc + 1) * P, t0:t0 + 512],
                      f"lnx{dc}", waits=waits)

    emit_ln(xn1_tile, xnorm_d, gain=pcs["gam_pc"], bias=pcs["bet_pc"])

    fw.new_phase()
    ctx8 = ExitStack()
    e8 = ctx8.enter_context
    if True:
        wgb = e8(nc.sbuf_tensor("wgb", [P, NCH, 8], BF16))
        xg = [e8(nc.sbuf_tensor(f"xg{i}", [P, NCH, P], BF16)) for i in range(2)]
        eprob = e8(nc.sbuf_tensor("eprob", [P, 8], F32))
        curg = e8(nc.sbuf_tensor("curg", [P, 8], F32))
        mg1 = e8(nc.sbuf_tensor("mg1", [P, 1], F32))
        mg2 = e8(nc.sbuf_tensor("mg2", [P, 1], F32))
        sum8 = e8(nc.sbuf_tensor("sum8", [P, 1], F32))
        d2g = e8(nc.sbuf_tensor("d2g", [P, 1], F32))
        gt8 = e8(nc.sbuf_tensor("gt8", [P, 8], F32))
        gt8A = e8(nc.sbuf_tensor("gt8A", [P, 8], F32))
        g8all = e8(nc.sbuf_tensor("g8all", [8, S], F32))
        g8b = e8(nc.sbuf_tensor("g8b", [8, S], BF16))
        g8row = [e8(nc.sbuf_tensor(f"g8row{i}", [1, S], BF16)) for i in range(2)]
        b2b = e8(nc.sbuf_tensor("b2b", [P, 8, NCH], F32))
        w1r4 = [e8(nc.sbuf_tensor(f"w1r4{i}", [P, NCH, 512], BF16))
                for i in range(2)]
        w2r4 = [e8(nc.sbuf_tensor(f"w2r4{i}", [P, 4, D], BF16)) for i in range(2)]
        b1b4 = [e8(nc.sbuf_tensor(f"b1b4{i}", [P, 32], F32)) for i in range(2)]
        hall4 = e8(nc.sbuf_tensor("hall4", [P, 32, 512], BF16))
        xnt4 = [e8(nc.sbuf_tensor(f"xnt4{i}", [P, NCH, 512], BF16))
                for i in range(2)]
        gbs4 = e8(nc.sbuf_tensor("gbs4", [P, 512], BF16))
        accs4 = e8(nc.sbuf_tensor("accs4", [P, NCH, S], F32))
        uc = e8(nc.sbuf_tensor("uc", [P, 2, 512], F32))
        gc = e8(nc.sbuf_tensor("gc", [P, 2, 512], BF16))
        xn1c = e8(nc.sbuf_tensor("xn1c", [P, 2, 512], BF16))
        s1c = e8(nc.sbuf_tensor("s1c", [P, 2, 512], BF16))
        xnc = e8(nc.sbuf_tensor("xnc", [P, 2, 512], F32))
        t1c = e8(nc.sbuf_tensor("t1c", [P, 2, 512], F32))
        yc = e8(nc.sbuf_tensor("yc", [P, 2, 512], F32))
        hwg = fw.dma(G, wgb[:], WgT.rearrange("(c p) e -> p c e", p=P), "wg4")
        hb2 = fw.dma(SY, b2b[:], b2pc, "b24")
        rot = Rot(range(4))
        ep_free = None
        gt_free = None
        gt8A_free = None
        hex8_prev = None
        w1g_free = [None, None]
        hwc2 = [None, None]
        hgst = None
        g8a_free = None
        xg_free = None
        w1_free = None
        w2_free = [None, None]
        b1_free = [None, None]
        xnt_free = [None, None]
        gbs_free = None
        hall_free = None
        hw1 = None
        hb1 = None
        cmb4 = [[None, None] for _ in range(8)]
        xg_free = [None, None]
        for b in range(NB):
            for tc in range(16):
                gb_ = tc % 2
                hxg = fw.dma(SY, xg[gb_][:],
                             xnorm_d[b][:, tc * P:(tc + 1) * P].rearrange(
                                 "(c p) t -> p c t", p=P),
                             f"xg4{gb_}", waits=(xg_free[gb_],))
                psi = rot.get()
                fw.wait(T, hwg, hxg)
                for dc in range(NCH):
                    inst = T.matmul(PS[psi][0:P, 0:8],
                                    xg[gb_][:, dc, :],
                                    wgb[:, dc, :], start=(dc == 0),
                                    stop=(dc == NCH - 1))
                hmm = fw.inc(inst, "pe")
                xg_free[gb_] = hmm
                fw.wait(V, hmm, ep_free, hex8_prev)
                fw.op(V, V.reduce_max, mg1[:], PS[psi][0:P, 0:8], axis=AX.X,
                      sem_name="dve")
                fw.op(V, V.tensor_scalar, mg2[:], mg1[:], -1.0, None, ALU.mult,
                      sem_name="dve")
                hex8 = fw.op(A, A.activation, eprob[:], PS[psi][0:P, 0:8],
                             AF.Exp, bias=mg2[:], accum_out=sum8[:],
                             sem_name="act", waits=(fw.h("dve"),))
                ps_free[psi] = hex8
                hex8_prev = hex8
                fw.op(V, V.reduce_max, mg1[:], eprob[:], axis=AX.X,
                      sem_name="dve", waits=(hex8,))
                fw.op(V, V.scalar_tensor_tensor, curg[:], eprob[:], mg1[:],
                      eprob[:], ALU.is_lt, ALU.mult, sem_name="dve")
                fw.op(V, V.reduce_max, mg2[:], curg[:], axis=AX.X,
                      sem_name="dve")
                fw.op(V, V.tensor_add, d2g[:], mg1[:], mg2[:], sem_name="dve")
                fw.op(V, V.scalar_tensor_tensor, d2g[:], sum8[:], 1e-9,
                      d2g[:], ALU.mult, ALU.add, sem_name="dve")
                fw.op(V, V.reciprocal, d2g[:], d2g[:], sem_name="dve")
                fw.op(V, V.scalar_tensor_tensor, gt8[:], eprob[:], mg2[:],
                      eprob[:], ALU.is_ge, ALU.mult, sem_name="dve",
                      waits=(gt_free,))
                ep_free = fw.op(V, V.tensor_scalar, gt8[:], gt8[:], d2g[:],
                                None, ALU.mult, sem_name="dve")
                hga = fw.op(A, A.activation, gt8A[:], gt8[:], AF.Identity,
                            sem_name="act", waits=(ep_free, gt8A_free))
                psi2 = rot.get()
                fw.wait(T, hga)
                inst = T.transpose(PS[psi2][0:8, 0:P], gt8A[:], eyef_s[:])
                hmm2 = fw.inc(inst, "pe")
                gt_free = hmm2
                gt8A_free = hmm2
                hgl = fw.op(A, A.activation, g8all[:, tc * P:(tc + 1) * P],
                            PS[psi2][0:8, 0:P], AF.Identity, sem_name="act",
                            waits=(hmm2, g8a_free))
                ps_free[psi2] = hgl
            hg8b = fw.op(A, A.activation, g8b[:], g8all[:], AF.Identity,
                         sem_name="act")
            hgst = fw.dma(SY, g8_d[b], g8all[:], "g84", waits=(fw.h("act"),))
            g8a_free = hgst

            hacz = fw.op(V, V.memset, accs4[:], 0.0, sem_name="dve",
                         waits=(fw.h("pe"),))
            hgr = [None, None]
            hb1r = [None, None]
            for e in range(8):
                if e == 0:
                    hb1r[0] = fw.dma(SY, b1b4[0][:], b1_pc[0], "b14a",
                                     waits=(fw.h("act"),))
                    hgr[0] = fw.dma(G, g8row[0][:], g8_d[b, 0:1, :], "gr40",
                                    waits=(hgst, fw.h("pe")))
                if e + 1 < 8:
                    hb1r[(e + 1) % 2] = fw.dma(
                        SY, b1b4[(e + 1) % 2][:], b1_pc[e + 1],
                        f"b14{(e + 1) % 2}", waits=(fw.h("act"),))
                    hgr[(e + 1) % 2] = fw.dma(
                        G, g8row[(e + 1) % 2][:], g8_d[b, e + 1:e + 2, :],
                        f"gr4{(e + 1) % 2}", waits=(hgst, fw.h("pe")))
                for tt in range(4):
                    to = tt * 512
                    xb_ = tt % 2
                    hxt = []
                    for dc in range(NCH):
                        hxt.append(fw.dma(
                            SY, xnt4[xb_][:, dc, :],
                            xnorm_d[b, dc * P:(dc + 1) * P, to:to + 512],
                            f"xq{xb_}_{dc}", waits=(xnt_free[xb_],)))
                    psi = rot.get()
                    fw.wait(T, hgr[e % 2])
                    inst = T.matmul(PS[psi][0:P, 0:512], ones_bf[:],
                                    g8row[e % 2][0:1, to:to + 512],
                                    start=True, stop=True)
                    hmm = fw.inc(inst, "pe")
                    hgbs = fw.op(A, A.activation, gbs4[:], PS[psi][0:P, 0:512],
                                 AF.Identity, sem_name="act",
                                 waits=(hmm, gbs_free))
                    ps_free[psi] = hgbs
                    hw1g = [None, None]
                    for fc in range(32):
                        psi_h = fc % 2
                        if fc % 4 == 0:
                            fg = fc // 4
                            wr1 = fg % 2
                            fw.wait(SY, w1g_free[wr1])
                            hw1g[wr1] = fw.dma(
                                SY, w1r4[wr1][:].rearrange("p c f -> p (c f)"),
                                w1p_d[e, fg], f"w14{wr1}", waits=(wpc_done,))
                        fw.wait(T, ps_free[psi_h], hw1g[(fc // 4) % 2], *hxt)
                        for dc in range(NCH):
                            inst = T.matmul(PS[psi_h][0:P, 0:512],
                                            w1r4[(fc // 4) % 2][:, dc,
                                                (fc % 4) * P:(fc % 4 + 1) * P],
                                            xnt4[xb_][:, dc, :],
                                            start=(dc == 0), stop=(dc == NCH - 1))
                        hmm = fw.inc(inst, "pe")
                        if fc % 4 == 3:
                            w1g_free[(fc // 4) % 2] = hmm
                        hge = fw.op(A, A.activation, hall4[:, fc, :],
                                    PS[psi_h][0:P, 0:512], AF.Gelu_apprx_tanh,
                                    bias=b1b4[e % 2][:, fc:fc + 1],
                                    sem_name="act",
                                    waits=(hmm, hb1r[e % 2], hall_free))
                        ps_free[psi_h] = hge
                    hW1done = fw.h("pe")
                    xnt_free[xb_] = hW1done
                    for fc in range(32):
                        if fc % 4 == 0:
                            fg = fc // 4
                            wr = fg % 2
                            fw.wait(SY, w2_free[wr])
                            hwc2[wr] = fw.dma(
                                SY, w2r4[wr][:].rearrange("p c d -> p (c d)"),
                                w2p_d[e, fg], f"w24{wr}", waits=(wpc_done,))
                        wr = (fc // 4) % 2
                        fw.wait(T, hwc2[wr], fw.h("act"))
                        if fc == 0:
                            fw.wait(T, *[ps_free[i] for i in range(8)])
                        for dc in range(NCH):
                            inst = T.matmul(PS[dc][0:P, 0:512],
                                            w2r4[wr][:, fc % 4,
                                                dc * P:(dc + 1) * P],
                                            hall4[:, fc, :],
                                            start=(fc == 0), stop=(fc == 31))
                        hmmw = fw.inc(inst, "pe")
                        if fc % 4 == 3:
                            w2_free[wr] = hmmw
                    hall_free = fw.h("pe")
                    for dc in range(NCH):
                        hu = fw.op(V, V.scalar_tensor_tensor, uc[:, dc % 2, :],
                                   PS[dc][0:P, 0:512], b2b[:, e, dc:dc + 1],
                                   gbs4[:], ALU.add, ALU.mult, sem_name="dve",
                                   waits=(fw.h("pe"), hgbs, hacz, hb2,
                                          cmb4[0][dc % 2]))
                        hac = fw.op(V, V.tensor_add,
                                    accs4[:, dc, to:to + 512],
                                    accs4[:, dc, to:to + 512], uc[:, dc % 2, :],
                                    sem_name="dve")
                        cmb4[0][dc % 2] = hac
                        ps_free[dc] = hac
                    gbs_free = fw.h("dve")
            moe_done = fw.h("dve")
            w1_free = fw.h("pe")
            for dc in range(NCH):
                for ts in range(4):
                    to = ts * 512
                    cb = ts % 2
                    h1 = fw.dma(SY, xn1c[:, cb, :],
                                xn1_d[b, dc * P:(dc + 1) * P, to:to + 512],
                                f"f1{cb}", waits=(cmb4[1][cb],))
                    h2 = fw.dma(SY, s1c[:, cb, :],
                                s1_d[b, dc * P:(dc + 1) * P, to:to + 512],
                                f"f2{cb}", waits=(cmb4[2][cb],))
                    h3 = fw.dma(SY, xnc[:, cb, :],
                                xnew_d[b, dc * P:(dc + 1) * P, to:to + 512],
                                f"f3{cb}", waits=(cmb4[3][cb],))
                    hu2 = fw.op(V, V.tensor_add, uc[:, cb, :],
                                xn1c[:, cb, :], accs4[:, dc, to:to + 512],
                                sem_name="dve", waits=(moe_done, h1,
                                                       cmb4[4][cb]))
                    hsi = fw.op(A, A.activation, gc[:, cb, :], uc[:, cb, :],
                                AF.Silu, sem_name="act",
                                waits=(hu2, cmb4[5][cb]))
                    ht1 = fw.op(V, V.tensor_mul, t1c[:, cb, :], s1c[:, cb, :],
                                gc[:, cb, :], sem_name="dve",
                                waits=(hsi, h2, cmb4[6][cb]))
                    hy = fw.op(V, V.tensor_add, yc[:, cb, :], xnc[:, cb, :],
                               t1c[:, cb, :], sem_name="dve",
                               waits=(h3, cmb4[7][cb]))
                    hys = fw.dma(SY, y_out[b, dc * P:(dc + 1) * P, to:to + 512],
                                 yc[:, cb, :], f"ys{cb}", waits=(hy,))
                    cmb4[1][cb] = hu2
                    cmb4[2][cb] = ht1
                    cmb4[3][cb] = hy
                    cmb4[4][cb] = hsi
                    cmb4[5][cb] = ht1
                    cmb4[6][cb] = hy
                    cmb4[7][cb] = hys
    ctx8.close()

    return finish()


def kernel(**inputs):
    in_maps = _host_prep(inputs)
    nc = build_program(sim_strict=True)
    res = run_bass_kernel_spmd(nc, in_maps, core_ids=list(range(NCORE)))
    out = np.concatenate(
        [r["y"].transpose(0, 2, 1) for r in res.results], axis=0)
    return np.ascontiguousarray(out)


def _run_debug(inputs, dbg=(), limit=99, cores=(0,), serialize=False, p2_stop=False):
    in_maps = _host_prep(inputs)
    nc = build_program(dbg=dbg, limit=limit, serialize=serialize, p2_stop=p2_stop)
    res = run_bass_kernel_spmd(nc, [in_maps[c] for c in cores],
                               core_ids=list(cores))
    return res.results


# version 04 (final)
# speedup vs baseline: 1.2069x; 1.1385x over previous
import numpy as np
import ml_dtypes
import concourse.bass as bass
import concourse.mybir as mybir
from concourse.bass_utils import run_bass_kernel_spmd
from concourse.bass_types import AP
from contextlib import ExitStack

F32 = mybir.dt.float32
BF16 = mybir.dt.bfloat16
AF = mybir.ActivationFunctionType
ALU = mybir.AluOpType
AX = mybir.AxisListType

P = 128
S = 2048
D = 1024
NCH = D // P
NB = 2
NCORE = 8
PP = 2 * S - 1
H0 = 16
HD = 64

NEG = -1.0e30


class FW:
    def __init__(self, nc):
        self.nc = nc
        self.sems = {}
        self.counts = {}
        self.hwm = {}
        self.serialize = False
        self.tagmap = {}
        self.pool_i = {"s": 0, "g": 0}
        self.engines = []
        self.strict = False

    def sem(self, name):
        if name not in self.sems:
            self.sems[name] = self.nc.alloc_semaphore(name=name)
            self.counts[name] = 0
        return self.sems[name]

    def h(self, name):
        return (name, self.counts[name])

    def wait(self, eng, *handles):
        if self.serialize:
            handles = tuple(handles) + self.all_handles()
        for hh in handles:
            if hh is None:
                continue
            name, cnt = hh
            if cnt <= 0:
                continue
            key = (eng.engine, name)
            if self.hwm.get(key, 0) < cnt:
                eng.wait_ge(self.sems[name], cnt)
                self.hwm[key] = cnt

    def inc(self, inst, name, amount=1):
        inst.then_inc(self.sems[name], amount)
        self.counts[name] += amount
        return (name, self.counts[name])

    def incn(self, eng, name):
        return self.inc(eng.nop(nofuse=True), name)

    def dma(self, eng, out, in_, tag, waits=(), global_tag=False, **kw):
        if global_tag:
            s = "GT_" + tag + ("g" if eng.engine == mybir.EngineType.Pool
                               else "s")
            self.sem(s)
        else:
            kind = "g" if eng.engine == mybir.EngineType.Pool else "s"
            s = self._tag_sem(tag, kind)
        self.wait(eng, *waits)
        inst = eng.dma_start(out=out, in_=in_, **kw)
        return self.inc(inst, s, 16)

    def dmat(self, eng, out, in_, tag, waits=()):
        s = self._tag_sem(tag, "s")
        self.wait(eng, *waits)
        inst = eng.dma_start_transpose(out, in_)
        return self.inc(inst, s, 16)

    _ENG_SEM = {}

    def op(self, eng, fn, *args, sem_name=None, waits=(), **kw):
        if self.serialize and sem_name is None:
            sem_name = "dve" if eng.engine == mybir.EngineType.DVE else None
        if self.strict:
            en = {mybir.EngineType.DVE: "dve",
                  mybir.EngineType.Activation: "act"}.get(eng.engine)
            if en is not None:
                self.wait(eng, (en, self.counts[en]))
                if sem_name is None:
                    sem_name = en
        self.wait(eng, *waits)
        inst = fn(*args, **kw)
        if sem_name is not None:
            return self.inc(inst, sem_name, 1)
        return None

    def all_handles(self, include_global=False):
        return tuple(self.h(n) for n in self.sems
                     if include_global or not n.startswith("GT_"))

    def phase_barrier(self, engines=None):
        hs = self.all_handles()
        for eng in (engines or self.engines):
            self.wait(eng, *hs)

    def new_phase(self):
        self.phase_barrier()
        self.tagmap = {}
        self.pool_i = {"s": 0, "g": 0}

    def _tag_sem(self, tag, kind="s"):
        key = kind + tag
        if key not in self.tagmap:
            name = f"{kind}{self.pool_i[kind]}"
            self.pool_i[kind] += 1
            self.sem(name)
            self.tagmap[key] = name
        return self.tagmap[key]

    def ht(self, tag, kind="s"):
        return self.h(self._tag_sem(tag, kind))


def _host_consts():
    pos = np.arange(S - 1, -S, -1, dtype=np.float64)
    inv_freq = 1.0 / (10000.0 ** (np.arange(0, D, 2, dtype=np.float64) / D))
    ang = pos[:, None] * inv_freq[None, :]
    pe = np.concatenate([np.sin(ang), np.cos(ang)], axis=-1)
    pe_rev = pe[::-1]
    peT = np.zeros((D, PP + 1), dtype=np.float32)
    peT[:, :PP] = pe_rev.T.astype(np.float32)

    dec = np.exp(-np.arange(S, dtype=np.float64) / S).astype(np.float32)
    dec_pc = dec.reshape(S // P, P).T.copy()

    r = np.arange(P)
    trineg = np.where(r[None, :] <= r[:, None], 0.0, NEG).astype(np.float32)
    eyef = np.eye(P, dtype=np.float32)
    return peT, dec_pc, trineg, eyef


def _pc(v):
    return np.ascontiguousarray(v.reshape(-1, P).T)


def _host_prep(inputs):
    g = lambda k: np.asarray(inputs[k], np.float32)
    peT, dec_pc, trineg, eyef = _host_consts()
    com = {
        "WqT": np.ascontiguousarray(g("L0_attn_W_q").T),
        "WkT": np.ascontiguousarray(g("L0_attn_W_k").T),
        "WvT": np.ascontiguousarray(g("L0_attn_W_v").T),
        "WoT": np.ascontiguousarray(g("L0_attn_W_o").T),
        "bq_row": g("L0_attn_b_q")[None, :],
        "bk_row": g("L0_attn_b_k")[None, :],
        "bv_row": g("L0_attn_b_v")[None, :],
        "bo_pc": _pc(g("L0_attn_b_o")),
        "fwq": g("L0_ffn_wq"), "fwk": g("L0_ffn_wk"), "fwv": g("L0_ffn_wv"),
        "fwp": g("L0_ffn_wp"), "fwo": g("L0_ffn_wo"),
        "ln1g_pc": _pc(g("L0_ffn_ln1_g")), "ln1b_pc": _pc(g("L0_ffn_ln1_b")),
        "ln2g_pc": _pc(g("L0_ffn_ln2_g")), "ln2b_pc": _pc(g("L0_ffn_ln2_b")),
        "sel_w": g("L0_ffn_sel_w"),
        "keys": g("L0_ffn_keys"), "values": g("L0_ffn_values"),
        "a1wq": g("L1_attn_wq"), "a1wk": g("L1_attn_wk"), "a1wv": g("L1_attn_wv"),
        "a1bq_row": g("L1_attn_bq")[None, :],
        "a1bk_row": g("L1_attn_bk")[None, :],
        "a1bv_row": g("L1_attn_bv")[None, :],
        "a1wo": g("L1_attn_wo"),
        "a1bo_pc": _pc(g("L1_attn_bo")),
        "gam_pc": _pc(g("L1_ffn_gamma")), "bet_pc": _pc(g("L1_ffn_beta")),
        "WgT": np.ascontiguousarray(g("L1_ffn_Wg").T),
        "W1": g("L1_ffn_W1"),
        "b1_pc": np.ascontiguousarray(
            g("L1_ffn_b1").reshape(8, 32, P).transpose(0, 2, 1)),
        "W2": g("L1_ffn_W2"),
        "b2pc": np.ascontiguousarray(
            g("L1_ffn_b2").reshape(8, 8, P).transpose(2, 0, 1)),
        "peT": peT, "dec_pc": dec_pc, "trineg": trineg, "eyef": eyef,
    }
    x = np.asarray(inputs["L0_attn_x"], np.float32)
    in_maps = []
    for c in range(NCORE):
        xb = x[2 * c:2 * c + 2]
        m = dict(com)
        m["x"] = np.ascontiguousarray(xb.transpose(0, 2, 1))
        in_maps.append(m)
    return in_maps


def build_program(dbg=(), limit=99, serialize=False, p2_stop=False, sim_strict=False):
    nc = bass.Bass("TRN2", target_bir_lowering=False, debug=False)
    fw = FW(nc)
    fw.serialize = serialize
    fw.strict = sim_strict
    for s_ in ["pe", "act", "dve"]:
        fw.sem(s_)

    def din(name, shape):
        return nc.dram_tensor(name, shape, F32, kind="ExternalInput").ap()

    x_in = din("x", [NB, D, S])
    WqT, WkT, WvT, WoT = (din(n, [D, D]) for n in ["WqT", "WkT", "WvT", "WoT"])
    bq_row, bk_row, bv_row = (din(n, [1, D]) for n in ["bq_row", "bk_row", "bv_row"])
    bo_pc = din("bo_pc", [P, NCH])
    fwq, fwk, fwv, fwp, fwo = (din(n, [D, D]) for n in
                               ["fwq", "fwk", "fwv", "fwp", "fwo"])
    ln1g_pc, ln1b_pc = din("ln1g_pc", [P, NCH]), din("ln1b_pc", [P, NCH])
    ln2g_pc, ln2b_pc = din("ln2g_pc", [P, NCH]), din("ln2b_pc", [P, NCH])
    sel_w = din("sel_w", [D, 32])
    keys = din("keys", [32, D, 128])
    values = din("values", [32, 128, D])
    a1wq, a1wk, a1wv = din("a1wq", [D, 64]), din("a1wk", [D, 64]), din("a1wv", [D, 64])
    a1bq_row, a1bk_row, a1bv_row = (din(n, [1, 64]) for n in
                                    ["a1bq_row", "a1bk_row", "a1bv_row"])
    a1wo = din("a1wo", [64, D])
    a1bo_pc = din("a1bo_pc", [P, NCH])
    gam_pc, bet_pc = din("gam_pc", [P, NCH]), din("bet_pc", [P, NCH])
    WgT = din("WgT", [D, 8])
    W1 = din("W1", [8, D, 4096])
    b1_pc = din("b1_pc", [8, P, 32])
    W2 = din("W2", [8, 4096, D])
    b2pc = din("b2pc", [P, 8, NCH])
    peT = din("peT", [D, PP + 1])
    dec_pc = din("dec_pc", [P, S // P])
    trineg_in = din("trineg", [P, P])
    eyef_in = din("eyef", [P, P])

    y_out = nc.dram_tensor("y", [NB, D, S], F32, kind="ExternalOutput").ap()

    scr = {}

    def dscr(name, shape, dtype):
        kind = "ExternalOutput" if name in dbg else "Internal"
        t = nc.dram_tensor(name, shape, dtype, kind=kind)
        scr[name] = t.ap()
        return scr[name]

    xn_d = dscr("xn_d", [NB, D, S], BF16)
    ccos_d = dscr("ccos_d", [NB, D, S], BF16)
    lnr_d = dscr("lnr_d", [NB, 4, 3, 512], F32)
    p2_d = dscr("p2_d", [D, PP + 1], BF16)
    qf_d = dscr("qf_d", [NB, S, D], BF16)
    s_d = dscr("s_d", [NB, D, S], BF16)
    x1_d = dscr("x1_d", [NB, D, S], BF16)
    qa_d = dscr("qa_d", [NB, D, S], BF16)
    ka_d = dscr("ka_d", [NB, D, S], BF16)
    v_d = dscr("v_d", [NB, NCH, S, P], BF16)
    bd_d = dscr("bd_d", [4, P, 2176], BF16)
    den_d = dscr("den_d", [16, P], F32)
    src2_d = dscr("src2_d", [NB, D, S], BF16)
    x2_d = dscr("x2_d", [NB, D, S], BF16)
    gates_d = dscr("gates_d", [NB, 32, S], F32)
    xnew_d = dscr("xnew_d", [NB, D, S], F32)
    xnewb_d = dscr("xnewb_d", [NB, D, S], BF16)
    xn1_d = dscr("xn1_d", [NB, D, S], BF16)
    s1_d = dscr("s1_d", [NB, D, S], BF16)
    xnorm_d = dscr("xnorm_d", [NB, D, S], BF16)
    g8_d = dscr("g8_d", [NB, 8, S], F32)
    w1p_d = dscr("w1p_d", [8, 8, P, 4096], BF16)
    w2p_d = dscr("w2p_d", [8, 8, P, 4 * D], BF16)
    keyp_d = dscr("keyp_d", [32, P, NCH * P], BF16)
    valp_d = dscr("valp_d", [32, P, D], BF16)

    ctx = ExitStack()
    e_ = ctx.enter_context

    PS = [e_(nc.psum_tensor(f"ps{i}", [P, 512], F32)) for i in range(8)]
    ps_free = [None] * 8

    ones_bf = e_(nc.sbuf_tensor("ones_bf", [1, P], BF16))
    onesc_bf = e_(nc.sbuf_tensor("onesc_bf", [P, 1], BF16))
    ones_f = e_(nc.sbuf_tensor("ones_f", [1, P], F32))
    eps5 = e_(nc.sbuf_tensor("eps5", [P, 1], F32))
    eps6 = e_(nc.sbuf_tensor("eps6", [P, 1], F32))
    zcol = e_(nc.sbuf_tensor("zcol", [P, 1], F32))
    trineg_s = e_(nc.sbuf_tensor("trineg_s", [P, P], F32))
    eyef_s = e_(nc.sbuf_tensor("eyef_s", [P, P], F32))
    eyeb_s = e_(nc.sbuf_tensor("eyeb_s", [P, P], BF16))
    dec_s = e_(nc.sbuf_tensor("dec_s", [P, S // P], F32))
    pcs = {}
    for nm, apin in [("bo_pc", bo_pc), ("ln1g_pc", ln1g_pc), ("ln1b_pc", ln1b_pc),
                     ("ln2g_pc", ln2g_pc), ("ln2b_pc", ln2b_pc),
                     ("a1bo_pc", a1bo_pc), ("gam_pc", gam_pc), ("bet_pc", bet_pc)]:
        pcs[nm] = e_(nc.sbuf_tensor(nm + "_s", [P, NCH], F32))
    maxp2 = e_(nc.sbuf_tensor("maxp2", [1, NCH], F32))

    V = nc.vector
    A = nc.scalar
    T = nc.tensor
    G = nc.gpsimd
    SY = nc.sync
    fw.engines = [T, A, V, SY, G]

    fw.op(V, V.memset, ones_bf[:], 1.0, sem_name="dve")
    fw.op(V, V.memset, onesc_bf[:], 1.0, sem_name="dve")
    fw.op(V, V.memset, ones_f[:], 1.0, sem_name="dve")
    fw.op(V, V.memset, eps5[:], 1e-5, sem_name="dve")
    fw.op(V, V.memset, eps6[:], 1e-6, sem_name="dve")
    fw.op(V, V.memset, zcol[:], 0.0, sem_name="dve")
    fw.dma(SY, trineg_s[:], trineg_in, "cst")
    fw.dma(SY, eyef_s[:], eyef_in, "cst")
    fw.dma(G, eyeb_s[:], eyef_in, "cstp")
    fw.dma(SY, dec_s[:], dec_pc, "cst")
    for nm, apin in [("bo_pc", bo_pc), ("ln1g_pc", ln1g_pc), ("ln1b_pc", ln1b_pc),
                     ("ln2g_pc", ln2g_pc), ("ln2b_pc", ln2b_pc),
                     ("a1bo_pc", a1bo_pc), ("gam_pc", gam_pc), ("bet_pc", bet_pc)]:
        fw.dma(SY, pcs[nm][:], apin, "cst")
    consts_ld = (fw.ht("cst"), fw.ht("cstp", "g"), fw.h("dve"))

    with (nc.sbuf_tensor("peb", [P, NCH, PP + 1], BF16) as peb,
          nc.sbuf_tensor("wpb", [P, NCH, D], BF16) as wpb,
          nc.sbuf_tensor("p2t", [P, PP + 1], BF16) as p2t,
          nc.sbuf_tensor("p2sq", [P, PP + 1], BF16) as p2sq,
          nc.sbuf_tensor("prow", [1, 512], F32) as prow,
          nc.sbuf_tensor("pmax", [1, 1], F32) as pmax):
        p2t_free = None
        p2sq_free = None
        hpe = fw.dma(G, peb[:], peT.rearrange("(c p) q -> p c q", p=P), "peb")
        hwp = fw.dma(G, wpb[:], fwp.rearrange("(c p) m -> p c m", p=P), "wpb")
        for e_w in range(8):
            for fg_ in range(8):
                fw.dma(G, w1p_d[e_w, fg_].rearrange("p (c f) -> p c f", c=NCH),
                       W1[e_w][:, fg_ * 512:(fg_ + 1) * 512].rearrange(
                           "(c p) f -> p c f", p=P), "wpc", global_tag=True)
                fw.dma(G, w2p_d[e_w, fg_].rearrange("p (c d) -> p c d", c=4),
                       W2[e_w, fg_ * 512:(fg_ + 1) * 512, :].rearrange(
                           "(c p) d -> p c d", p=P), "wpc", global_tag=True)
        for e_w in range(32):
            fw.dma(G, keyp_d[e_w].rearrange("p (c f) -> p c f", c=NCH),
                   keys[e_w].rearrange("(c p) f -> p c f", p=P), "kpc",
                   global_tag=True)
            fw.dma(G, valp_d[e_w], values[e_w], "kpc", global_tag=True)
        wpc_done = fw.h("GT_wpcg")
        kpc_done = fw.h("GT_kpcg")
        NSL = (PP + 1) // 512
        for mc in range(NCH):
            for sl in range(NSL):
                psi = sl % 2
                fw.wait(T, hpe, hwp, ps_free[psi], *consts_ld)
                for dc in range(NCH):
                    inst = T.matmul(PS[psi][:], wpb[:, dc, mc * P:(mc + 1) * P],
                                    peb[:, dc, sl * 512:(sl + 1) * 512],
                                    start=(dc == 0), stop=(dc == NCH - 1))
                hmm = fw.inc(inst, "pe")
                hact = fw.op(A, A.activation, p2t[:, sl * 512:(sl + 1) * 512],
                             PS[psi][:], AF.Identity, sem_name="act",
                             waits=(hmm, p2t_free))
                ps_free[psi] = hact
            hst = fw.dma(SY, p2_d[mc * P:(mc + 1) * P, :], p2t[:],
                         "p2s", waits=(fw.h("act"),))
            hsq = fw.op(A, A.activation, p2sq[:], p2t[:], AF.Square,
                        sem_name="act", waits=(fw.h("act"), p2sq_free))
            p2t_free = hst
            fw.op(V, V.memset, pmax[:], 0.0, sem_name="dve")
            for sl in range(NSL):
                psi = 2 + sl % 2
                fw.wait(T, hsq, ps_free[psi], *consts_ld)
                inst = T.matmul(PS[psi][0:1, :], onesc_bf[:],
                                p2sq[:, sl * 512:(sl + 1) * 512],
                                start=True, stop=True)
                hmm = fw.inc(inst, "pe")
                p2sq_free = hmm
                fw.op(V, V.reduce_max, prow[:, 0:1], PS[psi][0:1, :], axis=AX.X,
                      sem_name="dve", waits=(hmm,))
                hdv = fw.op(V, V.tensor_max, pmax[:], pmax[:], prow[:, 0:1],
                            sem_name="dve")
                ps_free[psi] = hdv
            fw.op(V, V.tensor_copy, maxp2[:, mc:mc + 1], pmax[:],
                  sem_name="dve")
        p0_done = (fw.h("dve"), fw.h("act"), fw.h("pe"))

    def finish():
        for hh in fw.all_handles(include_global=True):
            fw.wait(SY, hh)
        ctx.close()
        return nc

    if limit < 1:
        return finish()

    ln_ctr = [0]

    def emit_ln(src_tile_dma, dst_d, gain=None, bias=None, src_f32=False):
        ln_ctr[0] += 1
        fw.new_phase()
        tg = f"ln{ln_ctr[0]}"
        with (nc.sbuf_tensor(tg + "_x", [P, NCH, 512], F32 if src_f32 else BF16) as xt,
              nc.sbuf_tensor(tg + "_xb", [P, NCH, 512], BF16) as xb,
              nc.sbuf_tensor(tg + "_sq", [P, NCH, 512], BF16) as sqt,
              nc.sbuf_tensor(tg + "_o", [P, NCH, 512], BF16) as ot,
              nc.sbuf_tensor(tg + "_rb", [1, 512], BF16) as rbr,
              nc.sbuf_tensor(tg + "_cb", [1, 512], BF16) as cbr,
              nc.sbuf_tensor(tg + "_m", [1, 512], F32) as mr,
              nc.sbuf_tensor(tg + "_q", [1, 512], F32) as qr,
              nc.sbuf_tensor(tg + "_rr", [1, 512], F32) as rr,
              nc.sbuf_tensor(tg + "_cr", [1, 512], F32) as cr):
            xt_free = [None] * NCH
            ot_free = [None] * NCH
            sq_free = [None] * NCH
            xb_free = [None] * NCH
            row_free = None
            ps_s, ps_q, ps_r, ps_c = PS[0], PS[1], PS[2], PS[3]
            for b in range(NB):
                for tt in range(S // 512):
                    t0 = tt * 512
                    hld = []
                    for dc in range(NCH):
                        hld.append(src_tile_dma(b, dc, t0, xt[:, dc, :],
                                                (xt_free[dc],)))
                    for dc in range(NCH):
                        if src_f32:
                            fw.op(A, A.activation, xb[:, dc, :], xt[:, dc, :],
                                  AF.Identity, sem_name="act",
                                  waits=(hld[dc], xb_free[dc]))
                            src_bf = xb[:, dc, :]
                            hsrc = fw.h("act")
                        else:
                            src_bf = xt[:, dc, :]
                            hsrc = hld[dc]
                        hsq = fw.op(A, A.activation, sqt[:, dc, :], src_bf,
                                    AF.Square, sem_name="act",
                                    waits=(hsrc, sq_free[dc]))
                        fw.wait(T, ps_free[0], ps_free[1], hsrc, hsq)
                        T.matmul(ps_s[0:1, :], onesc_bf[:], src_bf,
                                 start=(dc == 0), stop=(dc == NCH - 1))
                        inst2 = T.matmul(ps_q[0:1, :], onesc_bf[:], sqt[:, dc, :],
                                         start=(dc == 0), stop=(dc == NCH - 1))
                    hmm = fw.inc(inst2, "pe")
                    for dc in range(NCH):
                        sq_free[dc] = hmm
                        xb_free[dc] = hmm
                    fw.op(V, V.tensor_scalar, mr[:], ps_s[0:1, :], 1.0 / D, None,
                          ALU.mult, sem_name="dve", waits=(hmm,))
                    fw.op(V, V.tensor_scalar, qr[:], ps_q[0:1, :], 1.0 / D, None,
                          ALU.mult, sem_name="dve")
                    ps_free[0] = ps_free[1] = fw.h("dve")
                    fw.op(V, V.tensor_mul, rr[:], mr[:], mr[:], sem_name="dve")
                    fw.op(V, V.tensor_sub, qr[:], qr[:], rr[:], sem_name="dve")
                    fw.op(A, A.activation, qr[:], qr[:], AF.Ln, bias=eps5[0:1, :],
                          sem_name="act", waits=(fw.h("dve"),))
                    fw.op(A, A.activation, rr[:], qr[:], AF.Exp, scale=-0.5,
                          sem_name="act")
                    fw.wait(V, fw.h("act"))
                    fw.op(V, V.tensor_mul, cr[:], mr[:], rr[:], sem_name="dve")
                    fw.op(V, V.tensor_scalar, cr[:], cr[:], -1.0, None, ALU.mult,
                          sem_name="dve")
                    fw.op(A, A.activation, rbr[:], rr[:], AF.Identity,
                          sem_name="act", waits=(row_free, fw.h("dve")))
                    fw.op(A, A.activation, cbr[:], cr[:], AF.Identity,
                          sem_name="act")
                    hrows = fw.h("act")
                    fw.wait(T, hrows, ps_free[2], ps_free[3])
                    T.matmul(ps_r[:], ones_bf[:], rbr[:], start=True, stop=True)
                    inst = T.matmul(ps_c[:], ones_bf[:], cbr[:],
                                    start=True, stop=True)
                    hbc = fw.inc(inst, "pe")
                    row_free = hbc
                    for dc in range(NCH):
                        fw.op(V, V.tensor_mul, xt[:, dc, :], xt[:, dc, :], ps_r[:],
                              sem_name="dve", waits=(hbc, hld[dc]))
                        if gain is None:
                            hn = fw.op(V, V.tensor_add, ot[:, dc, :], xt[:, dc, :],
                                       ps_c[:], sem_name="dve",
                                       waits=(ot_free[dc],))
                        else:
                            fw.op(V, V.tensor_add, xt[:, dc, :], xt[:, dc, :],
                                  ps_c[:], sem_name="dve")
                            hn = fw.op(V, V.tensor_scalar, ot[:, dc, :],
                                       xt[:, dc, :], gain[:, dc:dc + 1],
                                       bias[:, dc:dc + 1], ALU.mult, ALU.add,
                                       sem_name="dve", waits=(ot_free[dc],))
                        xt_free[dc] = hn
                        ot_free[dc] = fw.dma(
                            SY, dst_d[b, dc * P:(dc + 1) * P, t0:t0 + 512],
                            ot[:, dc, :], f"lno{dc}", waits=(hn,))
                    ps_free[2] = ps_free[3] = fw.h("dve")

    fw.new_phase()

    def x_tile(b, dc, t0, tile_ap, waits):
        return fw.dma(SY, tile_ap, x_in[b, dc * P:(dc + 1) * P, t0:t0 + 512],
                      f"lnx{dc}", waits=waits)

    emit_ln(x_tile, xn_d, src_f32=True)

    if limit < 2:
        return finish()

    class Rot:
        def __init__(self, idxs):
            self.idxs = list(idxs)
            self.i = 0

        def get(self):
            idx = self.idxs[self.i % len(self.idxs)]
            self.i += 1
            fw.wait(T, ps_free[idx])
            return idx

    fw.new_phase()
    ctx2 = ExitStack()
    e2 = ctx2.enter_context
    if True:
        wqTb = e2(nc.sbuf_tensor("wqTb", [P, NCH, D], BF16))
        wkTb = e2(nc.sbuf_tensor("wkTb", [P, NCH, D], BF16))
        wvTb = e2(nc.sbuf_tensor("wvTb", [P, NCH, D], BF16))
        woTb = e2(nc.sbuf_tensor("woTb", [P, NCH, D], BF16))
        bqr = e2(nc.sbuf_tensor("bqr", [1, D], BF16))
        bkr = e2(nc.sbuf_tensor("bkr", [1, D], BF16))
        bvr = e2(nc.sbuf_tensor("bvr", [1, D], BF16))
        xnf = e2(nc.sbuf_tensor("xnf", [P, NCH, S], BF16))
        kfr = e2(nc.sbuf_tensor("kfr", [P, 16, D], BF16))
        vfr = e2(nc.sbuf_tensor("vfr", [P, 16, D], BF16))
        kvall = e2(nc.sbuf_tensor("kvall", [P, 8 * 65], BF16))
        qtile = e2(nc.sbuf_tensor("qtile", [P, 2, D], BF16))
        qf2 = e2(nc.sbuf_tensor("qf2", [P, S], BF16))
        nume = e2(nc.sbuf_tensor("nume", [64, S], BF16))
        denr = e2(nc.sbuf_tensor("denr", [1, S], F32))
        rdenb = e2(nc.sbuf_tensor("rdenb", [1, S], BF16))
        cmul = e2(nc.sbuf_tensor("cmul", [64, 512], BF16))
        cct = e2(nc.sbuf_tensor("cct", [P, NCH, 512], BF16))
        stile = e2(nc.sbuf_tensor("stile", [P, 2, 512], BF16))
        for wb_, wT in [(wqTb, WqT), (wkTb, WkT), (wvTb, WvT), (woTb, WoT)]:
            fw.dma(G, wb_[:], wT.rearrange("(c p) m -> p c m", p=P), "w2l")
        for rb_, rr_ in [(bqr, bq_row), (bkr, bk_row), (bvr, bv_row)]:
            fw.dma(G, rb_[:], rr_, "w2l")
        hwld = fw.ht("w2l", "g")
        rot6 = Rot(range(6))
        qt_free = [None, None]
        kv_free = None
        qf2_free = None
        nume_free = None
        denr_free = None
        rdenb_free = None
        cmul_free = None
        cct_free = [None] * NCH
        st_free = [None, None]
        for b in range(NB):
            hxn = fw.dma(SY, xnf[:], xn_d[b].rearrange("(c p) t -> p c t", p=P),
                         "xnf", waits=(fw.h("pe"), fw.h("act")))
            for tc in range(16):
                for proj in range(3):
                    wb_ = (wqTb, wkTb, wvTb)[proj]
                    br_ = (bqr, bkr, bvr)[proj]
                    for ms in range(2):
                        psi = rot6.get()
                        fw.wait(T, hwld, hxn)
                        BIAS_MM = True
                        if BIAS_MM:
                            T.matmul(PS[psi][:], ones_bf[:],
                                     br_[0:1, ms * 512:(ms + 1) * 512],
                                     start=True, stop=False)
                        for dc in range(NCH):
                            inst = T.matmul(PS[psi][:],
                                            xnf[:, dc, tc * P:(tc + 1) * P],
                                            wb_[:, dc, ms * 512:(ms + 1) * 512],
                                            start=(dc == 0 and not BIAS_MM),
                                            stop=(dc == NCH - 1))
                        hmm = fw.inc(inst, "pe")
                        if proj == 0:
                            dst = qtile[:, tc % 2, ms * 512:(ms + 1) * 512]
                            wfree = qt_free[tc % 2]
                        elif proj == 1:
                            dst = kfr[:, tc, ms * 512:(ms + 1) * 512]
                            wfree = None
                        else:
                            dst = vfr[:, tc, ms * 512:(ms + 1) * 512]
                            wfree = None
                        if proj < 2:
                            hev = fw.op(A, A.activation, dst, PS[psi][:], AF.Relu,
                                        scale=dec_s[:, tc:tc + 1], sem_name="act",
                                        waits=(hmm, wfree))
                        else:
                            hev = fw.op(A, A.activation, dst, PS[psi][:],
                                        AF.Identity, sem_name="act",
                                        waits=(hmm, wfree))
                        ps_free[psi] = hev
                qt_free[tc % 2] = fw.dma(
                    SY, qf_d[b, tc * P:(tc + 1) * P, :], qtile[:, tc % 2, :],
                    f"qt{tc % 2}", waits=(fw.h("act"),))
            if p2_stop:
                continue
            p2a_done = fw.h("act")
            for h in range(16):
                psi = rot6.get()
                fw.wait(T, kv_free, p2a_done)
                for tc in range(16):
                    T.matmul(PS[psi][0:64, 0:64],
                             kfr[:, tc, h * 64:h * 64 + 64],
                             vfr[:, tc, h * 64:h * 64 + 64],
                             start=(tc == 0), stop=(tc == 15))
                for tc in range(16):
                    inst = T.matmul(PS[psi][0:64, 64:65],
                                    kfr[:, tc, h * 64:h * 64 + 64],
                                    onesc_bf[:], start=(tc == 0), stop=(tc == 15),
                                    skip_group_check=True)
                hmm = fw.inc(inst, "pe")
                rb0 = (h % 2) * 64
                hev = fw.op(A, A.activation,
                            kvall[rb0:rb0 + 64, (h // 2) * 65:(h // 2 + 1) * 65],
                            PS[psi][0:64, 0:65], AF.Identity, sem_name="act",
                            waits=(hmm,))
                ps_free[psi] = hev
            kv_done = fw.h("act")
            for hp in range(NCH):
                hqf = fw.dmat(SY, qf2[:], qf_d[b][:, hp * P:(hp + 1) * P], "qf2",
                              waits=(qf2_free, qt_free[0], qt_free[1]))
                for head in range(2):
                    hq = head * 64
                    hh = hp * 2 + head
                    pss = [rot6.get() for _ in range(2)]
                    for sl in range(4):
                        psi = pss[sl % 2]
                        if sl >= 2:
                            fw.wait(T, ps_free[psi])
                        fw.wait(T, kv_done, hqf)
                        inst = T.matmul(PS[psi][0:65, :],
                                        kvall[hq:hq + 64, hp * 65:(hp + 1) * 65],
                                        qf2[hq:hq + 64, sl * 512:(sl + 1) * 512],
                                        start=True, stop=True)
                        hmm = fw.inc(inst, "pe")
                        fw.op(A, A.activation, denr[0:1, sl * 512:(sl + 1) * 512],
                              PS[psi][64:65, :], AF.Identity, bias=eps6[0:1, :],
                              sem_name="act", waits=(hmm, denr_free))
                        hev = fw.op(A, A.activation,
                                    nume[:, sl * 512:(sl + 1) * 512],
                                    PS[psi][0:64, :], AF.Identity,
                                    sem_name="act", waits=(nume_free,))
                        ps_free[psi] = hev
                    hrec = fw.op(V, V.reciprocal, denr[:], denr[:],
                                 sem_name="dve", waits=(fw.h("act"),))
                    denr_free = hrec
                    hrd = fw.op(A, A.activation, rdenb[:], denr[:], AF.Identity,
                                sem_name="act", waits=(hrec, rdenb_free))
                    for sl in range(4):
                        psi = rot6.get()
                        fw.wait(T, hrd)
                        inst = T.matmul(PS[psi][0:64, :], ones_bf[0:1, 0:64],
                                        rdenb[0:1, sl * 512:(sl + 1) * 512],
                                        start=True, stop=True)
                        hmm = fw.inc(inst, "pe")
                        rdenb_free = hmm
                        hmul = fw.op(V, V.tensor_mul, cmul[:],
                                     nume[:, sl * 512:(sl + 1) * 512],
                                     PS[psi][0:64, :], sem_name="dve",
                                     waits=(hmm, cmul_free))
                        ps_free[psi] = hmul
                        nume_free = hmul
                        cmul_free = fw.dma(
                            SY, scr["ccos_d"][b, hp * P + hq:hp * P + hq + 64,
                                              sl * 512:(sl + 1) * 512],
                            cmul[:], "cm", waits=(hmul,))
                qf2_free = fw.h("pe")
            cc_done = fw.ht("cm")
            for ts in range(4):
                hcc = []
                for hc in range(NCH):
                    hcc.append(fw.dma(
                        SY, cct[:, hc, :],
                        scr["ccos_d"][b, hc * P:(hc + 1) * P,
                                      ts * 512:(ts + 1) * 512],
                        f"cct{hc}", waits=(cc_done, cct_free[hc])))
                for dc in range(NCH):
                    psi = rot6.get()
                    fw.wait(T, *hcc)
                    for hc in range(NCH):
                        inst = T.matmul(PS[psi][:], woTb[:, hc, dc * P:(dc + 1) * P],
                                        cct[:, hc, :], start=(hc == 0),
                                        stop=(hc == NCH - 1))
                    hmm = fw.inc(inst, "pe")
                    sb = dc % 2
                    hev = fw.op(A, A.activation, stile[:, sb, :], PS[psi][:],
                                AF.Identity, bias=pcs["bo_pc"][:, dc:dc + 1],
                                sem_name="act", waits=(hmm, st_free[sb]))
                    ps_free[psi] = hev
                    st_free[sb] = fw.dma(
                        SY, s_d[b, dc * P:(dc + 1) * P, ts * 512:(ts + 1) * 512],
                        stile[:, sb, :], f"sst{sb}", waits=(hev,))
                for hc in range(NCH):
                    cct_free[hc] = fw.h("pe")

    ctx2.close()
    fw.new_phase()

    if limit < 3:
        return finish()

    def xn_tile(b, dc, t0, tile_ap, waits):
        return fw.dma(SY, tile_ap, xn_d[b, dc * P:(dc + 1) * P, t0:t0 + 512],
                      f"lnx{dc}", waits=waits)

    fw.new_phase()
    emit_ln(xn_tile, x1_d, gain=pcs["ln1g_pc"], bias=pcs["ln1b_pc"])

    fw.new_phase()
    ctx3 = ExitStack()
    e3 = ctx3.enter_context
    if True:
        wqb3 = e3(nc.sbuf_tensor("wqb3", [P, NCH, D], BF16))
        wkb3 = e3(nc.sbuf_tensor("wkb3", [P, NCH, D], BF16))
        wvb3 = e3(nc.sbuf_tensor("wvb3", [P, NCH, D], BF16))
        x1f = e3(nc.sbuf_tensor("x1f", [P, NCH, S], BF16))
        qmt = e3(nc.sbuf_tensor("qmt", [P, 2, S], BF16))
        vt3 = e3(nc.sbuf_tensor("vt3", [P, 2, D], BF16))
        for wb_, win in [(wqb3, fwq), (wkb3, fwk), (wvb3, fwv)]:
            fw.dma(G, wb_[:], win.rearrange("(c p) m -> p c m", p=P), "w3l")
        hwld3 = fw.ht("w3l", "g")
        rot = Rot(range(6))
        qm_free = [None, None]
        vt_free = [None, None]
        for b in range(NB):
            hx1 = fw.dma(SY, x1f[:], x1_d[b].rearrange("(c p) t -> p c t", p=P),
                         "x1f", waits=(fw.h("pe"),))
            for wb_, dst_d in [(wqb3, qa_d), (wkb3, ka_d)]:
                for mc in range(NCH):
                    for ts in range(4):
                        psi = rot.get()
                        fw.wait(T, hwld3, hx1)
                        for dc in range(NCH):
                            inst = T.matmul(PS[psi][:],
                                            wb_[:, dc, mc * P:(mc + 1) * P],
                                            x1f[:, dc, ts * 512:(ts + 1) * 512],
                                            start=(dc == 0), stop=(dc == NCH - 1))
                        hmm = fw.inc(inst, "pe")
                        hev = fw.op(A, A.activation,
                                    qmt[:, mc % 2, ts * 512:(ts + 1) * 512],
                                    PS[psi][:], AF.Identity, sem_name="act",
                                    waits=(hmm, qm_free[mc % 2]))
                        ps_free[psi] = hev
                    qm_free[mc % 2] = fw.dma(
                        SY, dst_d[b, mc * P:(mc + 1) * P, :], qmt[:, mc % 2, :],
                        f"qm{mc % 2}", waits=(fw.h("act"),))
            for tc in range(16):
                for hp in range(NCH):
                    psi = rot.get()
                    fw.wait(T, hwld3, hx1)
                    for dc in range(NCH):
                        inst = T.matmul(PS[psi][0:P, 0:P],
                                        x1f[:, dc, tc * P:(tc + 1) * P],
                                        wvb3[:, dc, hp * P:(hp + 1) * P],
                                        start=(dc == 0), stop=(dc == NCH - 1))
                    hmm = fw.inc(inst, "pe")
                    hev = fw.op(A, A.activation, vt3[:, tc % 2, hp * P:(hp + 1) * P],
                                PS[psi][0:P, 0:P], AF.Identity, sem_name="act",
                                waits=(hmm, vt_free[tc % 2]))
                    ps_free[psi] = hev
                for hp in range(NCH):
                    fw.dma(SY, v_d[b, hp, tc * P:(tc + 1) * P, :],
                           vt3[:, tc % 2, hp * P:(hp + 1) * P], f"vt{tc % 2}",
                           waits=(fw.h("act"),))
                vt_free[tc % 2] = fw.ht(f"vt{tc % 2}")
    ctx3.close()

    fw.new_phase()
    ctx4 = ExitStack()
    e4 = ctx4.enter_context
    if True:
        q2 = e4(nc.sbuf_tensor("q2", [P, S], BF16))
        k2 = e4(nc.sbuf_tensor("k2", [P, S], BF16))
        v2 = e4(nc.sbuf_tensor("v2", [P, 16, P], BF16))
        p2sl = e4(nc.sbuf_tensor("p2sl", [P, PP + 1], BF16))
        sqb = e4(nc.sbuf_tensor("sqb", [P, S], BF16))
        S_f = e4(nc.sbuf_tensor("S_f", [P, S], F32))
        ful = e4(nc.sbuf_tensor("ful", [P, 2176], BF16))
        Ptile = e4(nc.sbuf_tensor("Ptile", [P, S], BF16))
        denall = e4(nc.sbuf_tensor("denall", [P, 16], F32))
        den16 = e4(nc.sbuf_tensor("den16", [16, P], F32))
        rdenr = e4(nc.sbuf_tensor("rdenr", [1, S], F32))
        rdenb2 = e4(nc.sbuf_tensor("rdenb2", [1, S], BF16))
        negBc = e4(nc.sbuf_tensor("negBc", [P, 1], F32))
        bnds = e4(nc.sbuf_tensor("bnds", [1, 8], F32))
        nrow = e4(nc.sbuf_tensor("nrow", [1, S], F32))
        PTf = e4(nc.sbuf_tensor("PTf", [P, 16, S], BF16))
        zsl = e4(nc.sbuf_tensor("zsl", [64, 512], BF16))
        zmul = e4(nc.sbuf_tensor("zmul", [64, 512], BF16))
        bd_t = scr["bd_d"].tensor
        den_t = scr["den_d"].tensor
        rot = Rot(range(4))
        ld_free = None
        denall_free = None
        den16_free = None
        rdenr_free = None
        rdenb2_free = None
        nrow_free = None
        bnds_free = None
        ful_free = None
        ring_free = [None] * 4
        Ptile_free = None
        zsl_free = None
        zmul_free = None
        for b in range(NB):
            for hp in range(NCH):
                hq2 = fw.dma(SY, q2[:], qa_d[b, hp * P:(hp + 1) * P, :], "q2",
                             waits=(ld_free,))
                hk2 = fw.dma(SY, k2[:], ka_d[b, hp * P:(hp + 1) * P, :], "k2")
                hv2 = fw.dma(SY, v2[:],
                             v_d[b, hp].rearrange("(c p) f -> p c f", p=P), "v2")
                hp2 = fw.dma(SY, p2sl[:], p2_d[hp * P:(hp + 1) * P, :], "p2l")
                for src_, col in [(q2, 0), (k2, 1)]:
                    hsq = fw.op(A, A.activation, sqb[:], src_[:], AF.Square,
                                sem_name="act",
                                waits=(hq2 if col == 0 else hk2, fw.h("pe")))
                    for sl in range(4):
                        psi = rot.get()
                        fw.wait(T, hsq)
                        inst = T.matmul(PS[psi][0:1, 0:512], onesc_bf[:],
                                        sqb[:, sl * 512:(sl + 1) * 512],
                                        start=True, stop=True)
                        hmm = fw.inc(inst, "pe")
                        hev = fw.op(A, A.activation,
                                    nrow[0:1, sl * 512:(sl + 1) * 512],
                                    PS[psi][0:1, 0:512], AF.Identity,
                                    sem_name="act", waits=(hmm, nrow_free))
                        ps_free[psi] = hev
                    nrow_free = fw.op(V, V.reduce_max, bnds[0:1, col:col + 1],
                                      nrow[:], axis=AX.X, sem_name="dve",
                                      waits=(fw.h("act"), bnds_free))
                fw.op(A, A.activation, bnds[0:1, 3:4], bnds[0:1, 0:1], AF.Sqrt,
                      sem_name="act", waits=(fw.h("dve"),))
                fw.op(A, A.activation, bnds[0:1, 4:5], bnds[0:1, 1:2], AF.Sqrt,
                      sem_name="act")
                fw.op(A, A.activation, bnds[0:1, 5:6], maxp2[0:1, hp:hp + 1],
                      AF.Sqrt, sem_name="act")
                fw.op(V, V.tensor_add, bnds[0:1, 6:7], bnds[0:1, 4:5],
                      bnds[0:1, 5:6], sem_name="dve", waits=(fw.h("act"),))
                fw.op(V, V.tensor_mul, bnds[0:1, 6:7], bnds[0:1, 6:7],
                      bnds[0:1, 3:4], sem_name="dve")
                hB = fw.op(V, V.tensor_scalar, bnds[0:1, 7:8], bnds[0:1, 6:7],
                           -0.125, None, ALU.mult, sem_name="dve")
                psi = rot.get()
                fw.wait(T, hB)
                inst = T.matmul(PS[psi][0:P, 0:1], ones_f[:], bnds[0:1, 7:8],
                                start=True, stop=True)
                hmm = fw.inc(inst, "pe")
                bnds_free = hmm
                hnB = fw.op(A, A.activation, negBc[:], PS[psi][0:P, 0:1],
                            AF.Identity, sem_name="act", waits=(hmm,))
                ps_free[psi] = hnB
                for head in range(2):
                    hq = head * 64
                    for it in range(16):
                        base = (S - P) - it * P
                        off = 0
                        for w_ in (512, 512, 512, 512, 128):
                            psi = rot.get()
                            fw.wait(T, hq2, hp2)
                            inst = T.matmul(PS[psi][0:P, 0:w_],
                                            q2[hq:hq + 64, it * P:(it + 1) * P],
                                            p2sl[hq:hq + 64,
                                                 base + off:base + off + w_],
                                            start=True, stop=True)
                            hmm = fw.inc(inst, "pe")
                            hev = fw.op(A, A.activation, ful[:, off:off + w_],
                                        PS[psi][0:P, 0:w_], AF.Identity,
                                        sem_name="act", waits=(hmm, ful_free))
                            ps_free[psi] = hev
                            off += w_
                        rg = it % 4
                        hfs = fw.dma(SY, scr["bd_d"][rg], ful[:], "bds",
                                     waits=(fw.h("act"), ring_free[rg]))
                        ful_free = hfs
                        for sl in range(4):
                            psi = rot.get()
                            fw.wait(T, hk2)
                            inst = T.matmul(PS[psi][0:P, 0:512],
                                            q2[hq:hq + 64, it * P:(it + 1) * P],
                                            k2[hq:hq + 64, sl * 512:(sl + 1) * 512],
                                            start=True, stop=True)
                            hmm = fw.inc(inst, "pe")
                            hev = fw.op(A, A.activation,
                                        S_f[:, sl * 512:(sl + 1) * 512],
                                        PS[psi][0:P, 0:512], AF.Identity,
                                        sem_name="act", waits=(hmm,))
                            ps_free[psi] = hev
                        shift_ap = AP(bd_t, rg * P * 2176 + 127,
                                      [[2175, P], [1, S]])
                        hacc = fw.dma(G, S_f[:], shift_ap, "acc",
                                      waits=(hfs, fw.h("act")),
                                      accum_op=ALU.add)
                        ring_free[rg] = hacc
                        hexp = fw.op(A, A.activation, Ptile[:], S_f[:], AF.Exp,
                                     bias=negBc[:], scale=0.125,
                                     accum_out=denall[:, it:it + 1],
                                     sem_name="act",
                                     waits=(hacc, hnB, Ptile_free, denall_free))
                        for g8 in range(2):
                            psi = rot.get()
                            fw.wait(T, hexp, ps_free[psi])
                            pb16 = PS[psi][:, :].bitcast(BF16)
                            for k in range(8):
                                jc = g8 * 8 + k
                                inst = T.transpose(
                                    pb16[0:P, k * P:(k + 1) * P],
                                    Ptile[:, jc * P:(jc + 1) * P], eyeb_s[:])
                                hmm = fw.inc(inst, "pe")
                            hcp = fw.op(V, V.tensor_copy,
                                        PTf[:, g8 * 8:(g8 + 1) * 8,
                                            it * P:(it + 1) * P],
                                        pb16[0:P, 0:8 * P], sem_name="dve",
                                        waits=(hmm,))
                            ps_free[psi] = hcp
                        Ptile_free = hmm
                    hptf = fw.h("dve")
                    psi = rot.get()
                    fw.wait(T, fw.h("act"))
                    inst = T.transpose(PS[psi][0:16, 0:P], denall[:], eyef_s[:])
                    hmm = fw.inc(inst, "pe")
                    denall_free = hmm
                    hd16 = fw.op(A, A.activation, den16[:], PS[psi][0:16, 0:P],
                                 AF.Identity, sem_name="act",
                                 waits=(hmm, den16_free))
                    ps_free[psi] = hd16
                    den16_free = fw.dma(SY, den_d[:], den16[:], "dns",
                                        waits=(hd16,))
                    flat_ap = AP(den_t, 0, [[S, 1], [1, S]])
                    hdf = fw.dma(SY, rdenr[:], flat_ap, "dnl",
                                 waits=(rdenr_free, den16_free))
                    hrc = fw.op(V, V.reciprocal, rdenr[:], rdenr[:],
                                sem_name="dve", waits=(hdf,))
                    hrd = fw.op(A, A.activation, rdenb2[:], rdenr[:], AF.Identity,
                                sem_name="act", waits=(hrc, rdenb2_free))
                    rdenr_free = hrd
                    for jc in range(16):
                        fw.wait(T, hptf, hv2)
                        if jc == 0:
                            fw.wait(T, ps_free[4], ps_free[5], ps_free[6],
                                    ps_free[7])
                        for isl in range(4):
                            inst = T.matmul(PS[4 + isl][0:64, 0:512],
                                            v2[:, jc, hq:hq + 64],
                                            PTf[:, jc, isl * 512:(isl + 1) * 512],
                                            start=(jc == 0), stop=(jc == 15))
                        fw.inc(inst, "pe")
                    hV = fw.h("pe")
                    for isl in range(4):
                        psi = rot.get()
                        fw.wait(T, hrd)
                        inst = T.matmul(PS[psi][0:64, 0:512], ones_bf[0:1, 0:64],
                                        rdenb2[0:1, isl * 512:(isl + 1) * 512],
                                        start=True, stop=True)
                        hmm = fw.inc(inst, "pe")
                        rdenb2_free = hmm
                        hz = fw.op(A, A.activation, zsl[:],
                                   PS[4 + isl][0:64, 0:512], AF.Identity,
                                   sem_name="act", waits=(hV, zsl_free))
                        ps_free[4 + isl] = hz
                        hmul = fw.op(V, V.tensor_mul, zmul[:], zsl[:],
                                     PS[psi][0:64, 0:512], sem_name="dve",
                                     waits=(hmm, hz, zmul_free))
                        ps_free[psi] = hmul
                        zsl_free = hmul
                        zmul_free = fw.dma(
                            SY, scr["ccos_d"][b, hp * P + hq:hp * P + hq + 64,
                                              isl * 512:(isl + 1) * 512],
                            zmul[:], "zst", waits=(hmul,))
                ld_free = fw.h("pe")
    ctx4.close()

    fw.new_phase()
    ctx5 = ExitStack()
    e5 = ctx5.enter_context
    if True:
        wob = e5(nc.sbuf_tensor("wob", [P, NCH, D], BF16))
        cc3 = e5(nc.sbuf_tensor("cc3", [P, NCH, 512], BF16))
        xnt3 = e5(nc.sbuf_tensor("xnt3", [P, 2, 512], BF16))
        srct = e5(nc.sbuf_tensor("srct", [P, 2, 512], BF16))
        hwo = fw.dma(G, wob[:], fwo.rearrange("(c p) m -> p c m", p=P), "wol")
        rot = Rot(range(6))
        cc3_free = [None] * NCH
        xnt_free = [None, None]
        srct_free = [None, None]
        for b in range(NB):
            for ts in range(4):
                hcc = []
                for hc in range(NCH):
                    hcc.append(fw.dma(
                        SY, cc3[:, hc, :],
                        scr["ccos_d"][b, hc * P:(hc + 1) * P,
                                      ts * 512:(ts + 1) * 512],
                        f"cc{hc}", waits=(cc3_free[hc],)))
                for dc in range(NCH):
                    psi = rot.get()
                    fw.wait(T, hwo, *hcc)
                    for hc in range(NCH):
                        inst = T.matmul(PS[psi][:], wob[:, hc, dc * P:(dc + 1) * P],
                                        cc3[:, hc, :], start=(hc == 0),
                                        stop=(hc == NCH - 1))
                    hmm = fw.inc(inst, "pe")
                    sb = dc % 2
                    hxnt = fw.dma(SY, xnt3[:, sb, :],
                                  xn_d[b, dc * P:(dc + 1) * P,
                                       ts * 512:(ts + 1) * 512],
                                  f"xt{sb}", waits=(xnt_free[sb],))
                    hadd = fw.op(V, V.tensor_add, srct[:, sb, :], xnt3[:, sb, :],
                                 PS[psi][:], sem_name="dve",
                                 waits=(hmm, hxnt, srct_free[sb]))
                    ps_free[psi] = hadd
                    xnt_free[sb] = hadd
                    srct_free[sb] = fw.dma(
                        SY, src2_d[b, dc * P:(dc + 1) * P, ts * 512:(ts + 1) * 512],
                        srct[:, sb, :], f"sr{sb}", waits=(hadd,))
                for hc in range(NCH):
                    cc3_free[hc] = fw.h("pe")
    ctx5.close()

    def src2_tile(b, dc, t0, tile_ap, waits):
        return fw.dma(SY, tile_ap, src2_d[b, dc * P:(dc + 1) * P, t0:t0 + 512],
                      f"lnx{dc}", waits=waits)

    fw.new_phase()
    emit_ln(src2_tile, x2_d, gain=pcs["ln2g_pc"], bias=pcs["ln2b_pc"])

    fw.new_phase()
    ctx6 = ExitStack()
    e6 = ctx6.enter_context
    if True:
        x2f = e6(nc.sbuf_tensor("x2f", [P, NCH, S], BF16))
        selb = e6(nc.sbuf_tensor("selb", [P, NCH, 32], BF16))
        selt = e6(nc.sbuf_tensor("selt", [P, 32], F32))
        curt = e6(nc.sbuf_tensor("curt", [P, 32], F32))
        mc1 = e6(nc.sbuf_tensor("mc1", [P, 1], F32))
        gatesV = e6(nc.sbuf_tensor("gatesV", [P, 32], F32))
        gatesA = e6(nc.sbuf_tensor("gatesA", [P, 32], F32))
        gall = e6(nc.sbuf_tensor("gall", [32, S], F32))
        grow = e6(nc.sbuf_tensor("grow", [1, 32 * S], BF16))
        keyb = [e6(nc.sbuf_tensor(f"keyb{i}", [P, NCH, P], BF16)) for i in range(2)]
        valb = [e6(nc.sbuf_tensor(f"valb{i}", [P, D], BF16)) for i in range(2)]
        hrb = e6(nc.sbuf_tensor("hrb", [P, 2, 256], BF16))
        hgb = e6(nc.sbuf_tensor("hgb", [P, 2, 256], BF16))
        hgl = e6(nc.sbuf_tensor("hgl", [P, 2, 256], BF16))
        s2t = e6(nc.sbuf_tensor("s2t", [P, 2, 256], BF16))
        sst = e6(nc.sbuf_tensor("sst", [P, 2, 256], BF16))
        xft = e6(nc.sbuf_tensor("xft", [P, 2, 256], F32))
        ut = e6(nc.sbuf_tensor("ut", [P, 2, 256], F32))
        gt = e6(nc.sbuf_tensor("gt", [P, 2, 256], BF16))
        t1t = e6(nc.sbuf_tensor("t1t", [P, 2, 256], F32))
        xot = e6(nc.sbuf_tensor("xot", [P, 2, 256], F32))
        hsel = fw.dma(G, selb[:], sel_w.rearrange("(c p) e -> p c e", p=P), "sel")
        rot = Rot(range(3))
        selt_free = None
        gall_free = None
        key_free = [None, None]
        hload = [None, None]
        out_free = [None] * NCH
        cmb_free = [[None, None] for _ in range(7)]
        for b in range(NB):
            hx2 = fw.dma(SY, x2f[:], x2_d[b].rearrange("(c p) t -> p c t", p=P),
                         "x2f", waits=(fw.h("pe"),))
            for tc in range(16):
                psi = rot.get()
                fw.wait(T, hsel, hx2)
                for dc in range(NCH):
                    inst = T.matmul(PS[psi][0:P, 0:32],
                                    x2f[:, dc, tc * P:(tc + 1) * P],
                                    selb[:, dc, :], start=(dc == 0),
                                    stop=(dc == NCH - 1))
                hmm = fw.inc(inst, "pe")
                fw.op(A, A.activation, selt[:], PS[psi][0:P, 0:32], AF.Sigmoid,
                      sem_name="act", waits=(hmm, selt_free))
                fw.op(V, V.tensor_copy, curt[:], selt[:], sem_name="dve",
                      waits=(fw.h("act"),))
                for _ in range(3):
                    fw.op(V, V.reduce_max, mc1[:], curt[:], axis=AX.X,
                          sem_name="dve")
                    fw.op(V, V.scalar_tensor_tensor, curt[:], curt[:], mc1[:],
                          curt[:], ALU.is_lt, ALU.mult, sem_name="dve")
                fw.op(V, V.reduce_max, mc1[:], curt[:], axis=AX.X, sem_name="dve")
                fw.op(V, V.scalar_tensor_tensor, gatesV[:], selt[:], mc1[:],
                      selt[:], ALU.is_ge, ALU.mult, sem_name="dve")
                selt_free = fw.h("dve")
                hga = fw.op(A, A.activation, gatesA[:], gatesV[:], AF.Identity,
                            sem_name="act", waits=(selt_free,))
                ps_free[psi] = hga
                psi2 = rot.get()
                fw.wait(T, hga)
                inst = T.transpose(PS[psi2][0:32, 0:P], gatesA[:], eyef_s[:])
                hmm2 = fw.inc(inst, "pe")
                hgl_ = fw.op(A, A.activation, gall[:, tc * P:(tc + 1) * P],
                             PS[psi2][0:32, 0:P], AF.Identity, sem_name="act",
                             waits=(hmm2, gall_free))
                ps_free[psi2] = hgl_
            hgst = fw.dma(SY, gates_d[b], gall[:], "gst", waits=(fw.h("act"),))
            gall_free = hgst
            for e in range(32):
                fw.dma(G, grow[0:1, e * S:(e + 1) * S], gates_d[b, e:e + 1, :],
                       "grw", waits=(hgst, fw.h("pe")))
            hgrow = fw.ht("grw", "g")
            for tt in range(NB * 4):
                to = tt * 256
                for e in range(32):
                    r = e % 2
                    if e == 0 and tt == 0 and b == 0:
                        hk0 = fw.dma(SY, keyb[0][:].rearrange(
                            "p c f -> p (c f)"), keyp_d[0], "ky0",
                            waits=(kpc_done,))
                        hload[0] = (hk0, fw.dma(SY, valb[0][:], valp_d[0],
                                                "vl0"))
                    if not (e == 31 and tt == NB * 4 - 1 and b == NB - 1):
                        nxt = (e + 1) % 32
                        rn = nxt % 2
                        fw.wait(SY, key_free[rn])
                        hkn = fw.dma(SY, keyb[rn][:].rearrange(
                            "p c f -> p (c f)"), keyp_d[nxt], f"ky{rn}",
                            waits=(kpc_done,))
                        hload[rn] = (hkn,
                                     fw.dma(SY, valb[rn][:], valp_d[nxt],
                                            f"vl{rn}"))
                    fw.wait(T, hgrow, ps_free[3])
                    inst = T.matmul(PS[3][0:P, 0:256], ones_bf[:],
                                    grow[0:1, e * S + to:e * S + to + 256],
                                    start=True, stop=True)
                    hgb_ = fw.inc(inst, "pe")
                    psi_h = e % 2
                    fw.wait(T, ps_free[psi_h], hx2, *hload[r])
                    for dc in range(NCH):
                        inst = T.matmul(PS[psi_h][0:P, 0:256],
                                        keyb[r][:, dc, :],
                                        x2f[:, dc, to:to + 256],
                                        start=(dc == 0), stop=(dc == NCH - 1))
                    hmm = fw.inc(inst, "pe")
                    hr_ = fw.op(A, A.activation, hrb[:, e % 2, :],
                                PS[psi_h][0:P, 0:256], AF.Relu, sem_name="act",
                                waits=(hmm,))
                    ps_free[psi_h] = hr_
                    hg_ = fw.op(V, V.tensor_mul, hgb[:, e % 2, :],
                                hrb[:, e % 2, :], PS[3][0:P, 0:256],
                                sem_name="dve", waits=(hgb_, hr_))
                    ps_free[3] = hg_
                    hl_ = fw.op(A, A.activation, hgl[:, e % 2, :],
                                hgb[:, e % 2, :], AF.Identity, sem_name="act",
                                waits=(hg_,))
                    fw.wait(T, hl_)
                    if e == 0:
                        fw.wait(T, *[out_free[i] for i in range(NCH)])
                    for dc in range(NCH):
                        pt = 4 + dc % 4
                        rg0 = (dc // 4) * 256
                        inst = T.matmul(PS[pt][0:P, rg0:rg0 + 256],
                                        valb[r][:, dc * P:(dc + 1) * P],
                                        hgl[:, e % 2, :],
                                        start=(e == 0 and dc < 4),
                                        stop=(e == 31 and dc >= 4),
                                        skip_group_check=True)
                    key_free[r] = fw.inc(inst, "pe")
                hMoE = fw.h("pe")
                for dc in range(NCH):
                    pt = 4 + dc % 4
                    rg0 = (dc // 4) * 256
                    cb = dc % 2
                    h1 = fw.dma(SY, s2t[:, cb, :],
                                src2_d[b, dc * P:(dc + 1) * P, to:to + 256],
                                f"cb1{cb}", waits=(cmb_free[0][cb],))
                    h2 = fw.dma(SY, sst[:, cb, :],
                                s_d[b, dc * P:(dc + 1) * P, to:to + 256],
                                f"cb2{cb}", waits=(cmb_free[1][cb],))
                    h3 = fw.dma(SY, xft[:, cb, :],
                                x_in[b, dc * P:(dc + 1) * P, to:to + 256],
                                f"cb3{cb}", waits=(cmb_free[2][cb],))
                    hu = fw.op(V, V.tensor_add, ut[:, cb, :], s2t[:, cb, :],
                               PS[pt][0:P, rg0:rg0 + 256], sem_name="dve",
                               waits=(hMoE, h1, cmb_free[3][cb]))
                    out_free[dc] = hu
                    hgsl = fw.op(A, A.activation, gt[:, cb, :], ut[:, cb, :],
                                 AF.Silu, sem_name="act",
                                 waits=(hu, cmb_free[4][cb]))
                    ht1 = fw.op(V, V.tensor_mul, t1t[:, cb, :], sst[:, cb, :],
                                gt[:, cb, :], sem_name="dve",
                                waits=(hgsl, h2))
                    hxo = fw.op(V, V.tensor_add, xot[:, cb, :], xft[:, cb, :],
                                t1t[:, cb, :], sem_name="dve",
                                waits=(h3, cmb_free[5][cb], cmb_free[6][cb]))
                    hst1 = fw.dma(SY, xnew_d[b, dc * P:(dc + 1) * P, to:to + 256],
                                  xot[:, cb, :], f"xs{cb}", waits=(hxo,))
                    hst2 = fw.dma(G, xnewb_d[b, dc * P:(dc + 1) * P, to:to + 256],
                                  xot[:, cb, :], f"xp{cb}", waits=(hxo,))
                    cmb_free[0][cb] = hu
                    cmb_free[1][cb] = ht1
                    cmb_free[2][cb] = hxo
                    cmb_free[3][cb] = hgsl
                    cmb_free[4][cb] = ht1
                    cmb_free[5][cb] = hst2
                    cmb_free[6][cb] = hst1
    ctx6.close()


    if limit < 4:
        return finish()

    def xnb_tile(b, dc, t0, tile_ap, waits):
        return fw.dma(SY, tile_ap, xnewb_d[b, dc * P:(dc + 1) * P, t0:t0 + 512],
                      f"lnx{dc}", waits=waits)

    emit_ln(xnb_tile, xn1_d)

    fw.new_phase()
    ctx7 = ExitStack()
    e7 = ctx7.enter_context
    if True:
        aqb = [e7(nc.sbuf_tensor(f"aqb{i}", [P, NCH, 64], BF16)) for i in range(3)]
        brow4 = e7(nc.sbuf_tensor("brow4", [1, 3 * 64], BF16))
        awob = e7(nc.sbuf_tensor("awob", [64, D], BF16))
        xn1f = e7(nc.sbuf_tensor("xn1f", [P, NCH, S], BF16))
        qkvf = e7(nc.sbuf_tensor("qkvf", [P, 192], F32))
        v4 = e7(nc.sbuf_tensor("v4", [P, 16, 64], BF16))
        qfm4 = e7(nc.sbuf_tensor("qfm4", [64, S], BF16))
        kfm4 = e7(nc.sbuf_tensor("kfm4", [64, S], BF16))
        S4 = e7(nc.sbuf_tensor("S4", [P, S], F32))
        m4r = e7(nc.sbuf_tensor("m4r", [P, 1], F32))
        nm4 = e7(nc.sbuf_tensor("nm4", [P, 1], F32))
        Pt4 = e7(nc.sbuf_tensor("Pt4", [P, S], BF16))
        dal4 = e7(nc.sbuf_tensor("dal4", [P, 16], F32))
        d164 = e7(nc.sbuf_tensor("d164", [16, P], F32))
        rdr4 = e7(nc.sbuf_tensor("rdr4", [1, S], F32))
        rdb4 = e7(nc.sbuf_tensor("rdb4", [1, S], BF16))
        PTf4 = e7(nc.sbuf_tensor("PTf4", [P, 16, S], BF16))
        zsl4 = e7(nc.sbuf_tensor("zsl4", [64, 512], BF16))
        zmul4 = e7(nc.sbuf_tensor("zmul4", [64, 512], BF16))
        zln4 = e7(nc.sbuf_tensor("zln4", [64, 512], BF16))
        s1t = e7(nc.sbuf_tensor("s1t", [P, 2, 512], BF16))
        for i, (win, brs) in enumerate([(a1wq, a1bq_row), (a1wk, a1bk_row),
                                        (a1wv, a1bv_row)]):
            fw.dma(G, aqb[i][:], win.rearrange("(c p) m -> p c m", p=P), "a4l")
            fw.dma(G, brow4[0:1, i * 64:(i + 1) * 64], brs, "a4l")
        fw.dma(G, awob[:], a1wo, "a4l")
        ha4 = fw.ht("a4l", "g")
        rot = Rot(range(4))
        den_t4 = scr["den_d"].tensor
        Pt4_free = None
        dal4_free = None
        d164_free = None
        rdr4_free = None
        rdb4_free = None
        zs4_free = None
        zm4_free = None
        zl4_free = None
        s1t_free = [None, None]
        xn1f_free = None
        fw.op(V, V.memset, PTf4[:], 0.0, sem_name="dve")
        for b in range(NB):
            hx4 = fw.dma(SY, xn1f[:], xn1_d[b].rearrange("(c p) t -> p c t", p=P),
                         "xn1f", waits=(xn1f_free,))
            for tc in range(16):
                psi = rot.get()
                fw.wait(T, hx4, ha4)
                for proj in range(3):
                    T.matmul(PS[psi][0:P, proj * 64:(proj + 1) * 64], ones_bf[:],
                             brow4[0:1, proj * 64:(proj + 1) * 64],
                             start=True, stop=False, skip_group_check=True)
                    for dc in range(NCH):
                        inst = T.matmul(PS[psi][0:P, proj * 64:(proj + 1) * 64],
                                        xn1f[:, dc, tc * P:(tc + 1) * P],
                                        aqb[proj][:, dc, :], start=False,
                                        stop=(dc == NCH - 1),
                                        skip_group_check=True)
                hmm = fw.inc(inst, "pe")
                hqk = fw.op(A, A.activation, qkvf[:], PS[psi][0:P, 0:192],
                            AF.Identity, sem_name="act", waits=(hmm, fw.h("pe")))
                hv4 = fw.op(A, A.activation, v4[:, tc, :], PS[psi][0:P, 128:192],
                            AF.Identity, sem_name="act")
                ps_free[psi] = hv4
                for qk in range(2):
                    psi2 = rot.get()
                    fw.wait(T, hqk)
                    inst = T.transpose(PS[psi2][0:64, 0:P],
                                       qkvf[:, qk * 64:(qk + 1) * 64], eyef_s[:])
                    hmm2 = fw.inc(inst, "pe")
                    dst = (qfm4 if qk == 0 else kfm4)[:, tc * P:(tc + 1) * P]
                    hev = fw.op(A, A.activation, dst, PS[psi2][0:64, 0:P],
                                AF.Identity, sem_name="act", waits=(hmm2,))
                    ps_free[psi2] = hev
            qkv_done = fw.h("act")
            S4_free = None
            nm_free = None
            for it in range(16):
                nsl = it // 4 + 1
                cols = nsl * 512
                for sl in range(nsl):
                    psi = rot.get()
                    fw.wait(T, qkv_done)
                    inst = T.matmul(PS[psi][0:P, 0:512],
                                    qfm4[:, it * P:(it + 1) * P],
                                    kfm4[:, sl * 512:(sl + 1) * 512],
                                    start=True, stop=True)
                    hmm = fw.inc(inst, "pe")
                    hev = fw.op(A, A.activation, S4[:, sl * 512:(sl + 1) * 512],
                                PS[psi][0:P, 0:512], AF.Identity,
                                sem_name="act", waits=(hmm, S4_free))
                    ps_free[psi] = hev
                doff = it * P - (nsl - 1) * 512
                hmask = fw.op(V, V.tensor_add, S4[:, (nsl - 1) * 512 + doff:
                                                 (nsl - 1) * 512 + doff + P],
                              S4[:, (nsl - 1) * 512 + doff:
                                 (nsl - 1) * 512 + doff + P],
                              trineg_s[:], sem_name="dve", waits=(fw.h("act"),))
                if doff + P < 512:
                    hmask = fw.op(V, V.memset,
                                  S4[:, (nsl - 1) * 512 + doff + P:cols], NEG,
                                  sem_name="dve")
                hmx = fw.op(V, V.reduce_max, m4r[:], S4[:, 0:cols], axis=AX.X,
                            sem_name="dve")
                hnm = fw.op(V, V.tensor_scalar, nm4[:], m4r[:], -0.125, None,
                            ALU.mult, sem_name="dve", waits=(nm_free,))
                S4_free = hnm
                hexp = fw.op(A, A.activation, Pt4[:, 0:cols], S4[:, 0:cols],
                             AF.Exp, bias=nm4[:], scale=0.125,
                             accum_out=dal4[:, it:it + 1], sem_name="act",
                             waits=(hnm, Pt4_free, dal4_free))
                nm_free = hexp
                for g8 in range((it + 8) // 8):
                    nk = min(it + 1 - g8 * 8, 8)
                    psi = rot.get()
                    fw.wait(T, hexp, ps_free[psi])
                    pb16 = PS[psi][:, :].bitcast(BF16)
                    for k in range(nk):
                        jc = g8 * 8 + k
                        inst = T.transpose(pb16[0:P, k * P:(k + 1) * P],
                                           Pt4[:, jc * P:(jc + 1) * P],
                                           eyeb_s[:])
                        hmm = fw.inc(inst, "pe")
                    hcp = fw.op(V, V.tensor_copy,
                                PTf4[:, g8 * 8:g8 * 8 + nk, it * P:(it + 1) * P],
                                pb16[0:P, 0:nk * P], sem_name="dve",
                                waits=(hmm,))
                    ps_free[psi] = hcp
                Pt4_free = hmm
            hptf4 = fw.h("dve")
            psi = rot.get()
            fw.wait(T, fw.h("act"))
            inst = T.transpose(PS[psi][0:16, 0:P], dal4[:], eyef_s[:])
            hmm = fw.inc(inst, "pe")
            dal4_free = hmm
            hd16 = fw.op(A, A.activation, d164[:], PS[psi][0:16, 0:P],
                         AF.Identity, sem_name="act", waits=(hmm, d164_free))
            ps_free[psi] = hd16
            d164_free = fw.dma(SY, den_d[:], d164[:], "dn4", waits=(hd16,))
            hdf = fw.dma(SY, rdr4[:], AP(den_t4, 0, [[S, 1], [1, S]]), "dl4",
                         waits=(rdr4_free, d164_free))
            hrc = fw.op(V, V.reciprocal, rdr4[:], rdr4[:], sem_name="dve",
                        waits=(hdf,))
            hrd = fw.op(A, A.activation, rdb4[:], rdr4[:], AF.Identity,
                        sem_name="act", waits=(hrc, rdb4_free))
            rdr4_free = hrd
            for jc in range(16):
                fw.wait(T, hptf4, qkv_done)
                if jc == 0:
                    fw.wait(T, ps_free[4], ps_free[5], ps_free[6], ps_free[7])
                for isl in range(4):
                    inst = T.matmul(PS[4 + isl][0:64, 0:512], v4[:, jc, :],
                                    PTf4[:, jc, isl * 512:(isl + 1) * 512],
                                    start=(jc == 0), stop=(jc == 15))
                fw.inc(inst, "pe")
            hV = fw.h("pe")
            for isl in range(4):
                psi = rot.get()
                fw.wait(T, hrd)
                inst = T.matmul(PS[psi][0:64, 0:512], ones_bf[0:1, 0:64],
                                rdb4[0:1, isl * 512:(isl + 1) * 512],
                                start=True, stop=True)
                hmm = fw.inc(inst, "pe")
                rdb4_free = hmm
                hz = fw.op(A, A.activation, zsl4[:], PS[4 + isl][0:64, 0:512],
                           AF.Identity, sem_name="act", waits=(hV, zs4_free))
                ps_free[4 + isl] = hz
                hmul = fw.op(V, V.tensor_mul, zmul4[:], zsl4[:],
                             PS[psi][0:64, 0:512], sem_name="dve",
                             waits=(hmm, hz, zm4_free))
                ps_free[psi] = hmul
                zs4_free = hmul
                hzl = fw.op(A, A.activation, zln4[:], zmul4[:], AF.Identity,
                            sem_name="act", waits=(hmul, zl4_free))
                zm4_free = hzl
                for dc in range(NCH):
                    psi3 = rot.get()
                    fw.wait(T, hzl, ha4)
                    inst = T.matmul(PS[psi3][0:P, 0:512],
                                    awob[:, dc * P:(dc + 1) * P], zln4[:],
                                    start=True, stop=True)
                    hmm3 = fw.inc(inst, "pe")
                    if dc == NCH - 1:
                        zl4_free = hmm3
                    sb = dc % 2
                    hse = fw.op(A, A.activation, s1t[:, sb, :], PS[psi3][0:P, 0:512],
                                AF.Identity, bias=pcs["a1bo_pc"][:, dc:dc + 1],
                                sem_name="act", waits=(hmm3, s1t_free[sb]))
                    ps_free[psi3] = hse
                    s1t_free[sb] = fw.dma(
                        SY, s1_d[b, dc * P:(dc + 1) * P,
                                 isl * 512:(isl + 1) * 512],
                        s1t[:, sb, :], f"s1s{sb}", waits=(hse,))
            xn1f_free = fw.h("pe")
    ctx7.close()

    def xn1_tile(b, dc, t0, tile_ap, waits):
        return fw.dma(SY, tile_ap, xn1_d[b, dc * P:(dc + 1) * P, t0:t0 + 512],
                      f"lnx{dc}", waits=waits)

    emit_ln(xn1_tile, xnorm_d, gain=pcs["gam_pc"], bias=pcs["bet_pc"])

    fw.new_phase()
    ctx8 = ExitStack()
    e8 = ctx8.enter_context
    if True:
        wgb = e8(nc.sbuf_tensor("wgb", [P, NCH, 8], BF16))
        xg = [e8(nc.sbuf_tensor(f"xg{i}", [P, NCH, P], BF16)) for i in range(2)]
        eprob = e8(nc.sbuf_tensor("eprob", [P, 8], F32))
        curg = e8(nc.sbuf_tensor("curg", [P, 8], F32))
        mg1 = e8(nc.sbuf_tensor("mg1", [P, 1], F32))
        mg2 = e8(nc.sbuf_tensor("mg2", [P, 1], F32))
        sum8 = e8(nc.sbuf_tensor("sum8", [P, 1], F32))
        d2g = e8(nc.sbuf_tensor("d2g", [P, 1], F32))
        gt8 = e8(nc.sbuf_tensor("gt8", [P, 8], F32))
        gt8A = e8(nc.sbuf_tensor("gt8A", [P, 8], F32))
        g8all = e8(nc.sbuf_tensor("g8all", [8, S], F32))
        g8b = e8(nc.sbuf_tensor("g8b", [8, S], BF16))
        g8row = [e8(nc.sbuf_tensor(f"g8row{i}", [1, S], BF16)) for i in range(2)]
        b2b = e8(nc.sbuf_tensor("b2b", [P, 8, NCH], F32))
        w1r4 = [e8(nc.sbuf_tensor(f"w1r4{i}", [P, NCH, 512], BF16))
                for i in range(2)]
        w2r4 = [e8(nc.sbuf_tensor(f"w2r4{i}", [P, 4, D], BF16)) for i in range(2)]
        b1b4 = [e8(nc.sbuf_tensor(f"b1b4{i}", [P, 32], F32)) for i in range(2)]
        hall4 = e8(nc.sbuf_tensor("hall4", [P, 32, 512], BF16))
        xnt4 = [e8(nc.sbuf_tensor(f"xnt4{i}", [P, NCH, 512], BF16))
                for i in range(2)]
        gbs4 = e8(nc.sbuf_tensor("gbs4", [P, 512], BF16))
        accs4 = e8(nc.sbuf_tensor("accs4", [P, NCH, S], F32))
        uc = e8(nc.sbuf_tensor("uc", [P, 2, 512], F32))
        gc = e8(nc.sbuf_tensor("gc", [P, 2, 512], BF16))
        xn1c = e8(nc.sbuf_tensor("xn1c", [P, 2, 512], BF16))
        s1c = e8(nc.sbuf_tensor("s1c", [P, 2, 512], BF16))
        xnc = e8(nc.sbuf_tensor("xnc", [P, 2, 512], F32))
        t1c = e8(nc.sbuf_tensor("t1c", [P, 2, 512], F32))
        yc = e8(nc.sbuf_tensor("yc", [P, 2, 512], F32))
        hwg = fw.dma(G, wgb[:], WgT.rearrange("(c p) e -> p c e", p=P), "wg4")
        hb2 = fw.dma(SY, b2b[:], b2pc, "b24")
        rot = Rot(range(4))
        ep_free = None
        gt_free = None
        gt8A_free = None
        hex8_prev = None
        w1g_free = [None, None]
        hwc2 = [None, None]
        hgst = None
        g8a_free = None
        xg_free = None
        w1_free = None
        w2_free = [None, None]
        b1_free = [None, None]
        xnt_free = [None, None]
        gbs_free = None
        hall_free = None
        hw1 = None
        hb1 = None
        cmb4 = [[None, None] for _ in range(8)]
        xg_free = [None, None]
        for b in range(NB):
            for tc in range(16):
                gb_ = tc % 2
                hxg = fw.dma(SY, xg[gb_][:],
                             xnorm_d[b][:, tc * P:(tc + 1) * P].rearrange(
                                 "(c p) t -> p c t", p=P),
                             f"xg4{gb_}", waits=(xg_free[gb_],))
                psi = rot.get()
                fw.wait(T, hwg, hxg)
                for dc in range(NCH):
                    inst = T.matmul(PS[psi][0:P, 0:8],
                                    xg[gb_][:, dc, :],
                                    wgb[:, dc, :], start=(dc == 0),
                                    stop=(dc == NCH - 1))
                hmm = fw.inc(inst, "pe")
                xg_free[gb_] = hmm
                fw.wait(V, hmm, ep_free, hex8_prev)
                fw.op(V, V.reduce_max, mg1[:], PS[psi][0:P, 0:8], axis=AX.X,
                      sem_name="dve")
                fw.op(V, V.tensor_scalar, mg2[:], mg1[:], -1.0, None, ALU.mult,
                      sem_name="dve")
                hex8 = fw.op(A, A.activation, eprob[:], PS[psi][0:P, 0:8],
                             AF.Exp, bias=mg2[:], accum_out=sum8[:],
                             sem_name="act", waits=(fw.h("dve"),))
                ps_free[psi] = hex8
                hex8_prev = hex8
                fw.op(V, V.reduce_max, mg1[:], eprob[:], axis=AX.X,
                      sem_name="dve", waits=(hex8,))
                fw.op(V, V.scalar_tensor_tensor, curg[:], eprob[:], mg1[:],
                      eprob[:], ALU.is_lt, ALU.mult, sem_name="dve")
                fw.op(V, V.reduce_max, mg2[:], curg[:], axis=AX.X,
                      sem_name="dve")
                fw.op(V, V.tensor_add, d2g[:], mg1[:], mg2[:], sem_name="dve")
                fw.op(V, V.scalar_tensor_tensor, d2g[:], sum8[:], 1e-9,
                      d2g[:], ALU.mult, ALU.add, sem_name="dve")
                fw.op(V, V.reciprocal, d2g[:], d2g[:], sem_name="dve")
                fw.op(V, V.scalar_tensor_tensor, gt8[:], eprob[:], mg2[:],
                      eprob[:], ALU.is_ge, ALU.mult, sem_name="dve",
                      waits=(gt_free,))
                ep_free = fw.op(V, V.tensor_scalar, gt8[:], gt8[:], d2g[:],
                                None, ALU.mult, sem_name="dve")
                hga = fw.op(A, A.activation, gt8A[:], gt8[:], AF.Identity,
                            sem_name="act", waits=(ep_free, gt8A_free))
                psi2 = rot.get()
                fw.wait(T, hga)
                inst = T.transpose(PS[psi2][0:8, 0:P], gt8A[:], eyef_s[:])
                hmm2 = fw.inc(inst, "pe")
                gt_free = hmm2
                gt8A_free = hmm2
                hgl = fw.op(A, A.activation, g8all[:, tc * P:(tc + 1) * P],
                            PS[psi2][0:8, 0:P], AF.Identity, sem_name="act",
                            waits=(hmm2, g8a_free))
                ps_free[psi2] = hgl
            hg8b = fw.op(A, A.activation, g8b[:], g8all[:], AF.Identity,
                         sem_name="act")
            hgst = fw.dma(SY, g8_d[b], g8all[:], "g84", waits=(fw.h("act"),))
            g8a_free = hgst

            hacz = fw.op(V, V.memset, accs4[:], 0.0, sem_name="dve",
                         waits=(fw.h("pe"),))
            hgr = [None, None]
            hb1r = [None, None]
            for e in range(8):
                if e == 0:
                    hb1r[0] = fw.dma(SY, b1b4[0][:], b1_pc[0], "b14a",
                                     waits=(fw.h("act"),))
                    hgr[0] = fw.dma(G, g8row[0][:], g8_d[b, 0:1, :], "gr40",
                                    waits=(hgst, fw.h("pe")))
                if e + 1 < 8:
                    hb1r[(e + 1) % 2] = fw.dma(
                        SY, b1b4[(e + 1) % 2][:], b1_pc[e + 1],
                        f"b14{(e + 1) % 2}", waits=(fw.h("act"),))
                    hgr[(e + 1) % 2] = fw.dma(
                        G, g8row[(e + 1) % 2][:], g8_d[b, e + 1:e + 2, :],
                        f"gr4{(e + 1) % 2}", waits=(hgst, fw.h("pe")))
                for tt in range(4):
                    to = tt * 512
                    xb_ = tt % 2
                    hxt = []
                    for dc in range(NCH):
                        hxt.append(fw.dma(
                            SY, xnt4[xb_][:, dc, :],
                            xnorm_d[b, dc * P:(dc + 1) * P, to:to + 512],
                            f"xq{xb_}_{dc}", waits=(xnt_free[xb_],)))
                    psi = rot.get()
                    fw.wait(T, hgr[e % 2])
                    inst = T.matmul(PS[psi][0:P, 0:512], ones_bf[:],
                                    g8row[e % 2][0:1, to:to + 512],
                                    start=True, stop=True)
                    hmm = fw.inc(inst, "pe")
                    hgbs = fw.op(A, A.activation, gbs4[:], PS[psi][0:P, 0:512],
                                 AF.Identity, sem_name="act",
                                 waits=(hmm, gbs_free))
                    ps_free[psi] = hgbs
                    hw1g = [None, None]
                    for fc in range(32):
                        psi_h = fc % 2
                        if fc % 4 == 0:
                            fg = fc // 4
                            wr1 = fg % 2
                            fw.wait(SY, w1g_free[wr1])
                            hw1g[wr1] = fw.dma(
                                SY, w1r4[wr1][:].rearrange("p c f -> p (c f)"),
                                w1p_d[e, fg], f"w14{wr1}", waits=(wpc_done,))
                        fw.wait(T, ps_free[psi_h], hw1g[(fc // 4) % 2], *hxt)
                        for dc in range(NCH):
                            inst = T.matmul(PS[psi_h][0:P, 0:512],
                                            w1r4[(fc // 4) % 2][:, dc,
                                                (fc % 4) * P:(fc % 4 + 1) * P],
                                            xnt4[xb_][:, dc, :],
                                            start=(dc == 0), stop=(dc == NCH - 1))
                        hmm = fw.inc(inst, "pe")
                        if fc % 4 == 3:
                            w1g_free[(fc // 4) % 2] = hmm
                        hge = fw.op(A, A.activation, hall4[:, fc, :],
                                    PS[psi_h][0:P, 0:512], AF.Gelu_apprx_tanh,
                                    bias=b1b4[e % 2][:, fc:fc + 1],
                                    sem_name="act",
                                    waits=(hmm, hb1r[e % 2], hall_free))
                        ps_free[psi_h] = hge
                    hW1done = fw.h("pe")
                    xnt_free[xb_] = hW1done
                    for fc in range(32):
                        if fc % 4 == 0:
                            fg = fc // 4
                            wr = fg % 2
                            fw.wait(SY, w2_free[wr])
                            hwc2[wr] = fw.dma(
                                SY, w2r4[wr][:].rearrange("p c d -> p (c d)"),
                                w2p_d[e, fg], f"w24{wr}", waits=(wpc_done,))
                        wr = (fc // 4) % 2
                        fw.wait(T, hwc2[wr], fw.h("act"))
                        if fc == 0:
                            fw.wait(T, *[ps_free[i] for i in range(8)])
                        for dc in range(NCH):
                            inst = T.matmul(PS[dc][0:P, 0:512],
                                            w2r4[wr][:, fc % 4,
                                                dc * P:(dc + 1) * P],
                                            hall4[:, fc, :],
                                            start=(fc == 0), stop=(fc == 31))
                        hmmw = fw.inc(inst, "pe")
                        if fc % 4 == 3:
                            w2_free[wr] = hmmw
                    hall_free = fw.h("pe")
                    for dc in range(NCH):
                        hu = fw.op(V, V.scalar_tensor_tensor, uc[:, dc % 2, :],
                                   PS[dc][0:P, 0:512], b2b[:, e, dc:dc + 1],
                                   gbs4[:], ALU.add, ALU.mult, sem_name="dve",
                                   waits=(fw.h("pe"), hgbs, hacz, hb2,
                                          cmb4[0][dc % 2]))
                        hac = fw.op(V, V.tensor_add,
                                    accs4[:, dc, to:to + 512],
                                    accs4[:, dc, to:to + 512], uc[:, dc % 2, :],
                                    sem_name="dve")
                        cmb4[0][dc % 2] = hac
                        ps_free[dc] = hac
                    gbs_free = fw.h("dve")
            moe_done = fw.h("dve")
            w1_free = fw.h("pe")
            for dc in range(NCH):
                for ts in range(4):
                    to = ts * 512
                    cb = ts % 2
                    h1 = fw.dma(SY, xn1c[:, cb, :],
                                xn1_d[b, dc * P:(dc + 1) * P, to:to + 512],
                                f"f1{cb}", waits=(cmb4[1][cb],))
                    h2 = fw.dma(SY, s1c[:, cb, :],
                                s1_d[b, dc * P:(dc + 1) * P, to:to + 512],
                                f"f2{cb}", waits=(cmb4[2][cb],))
                    h3 = fw.dma(SY, xnc[:, cb, :],
                                xnew_d[b, dc * P:(dc + 1) * P, to:to + 512],
                                f"f3{cb}", waits=(cmb4[3][cb],))
                    hu2 = fw.op(V, V.tensor_add, uc[:, cb, :],
                                xn1c[:, cb, :], accs4[:, dc, to:to + 512],
                                sem_name="dve", waits=(moe_done, h1,
                                                       cmb4[4][cb]))
                    hsi = fw.op(A, A.activation, gc[:, cb, :], uc[:, cb, :],
                                AF.Silu, sem_name="act",
                                waits=(hu2, cmb4[5][cb]))
                    ht1 = fw.op(V, V.tensor_mul, t1c[:, cb, :], s1c[:, cb, :],
                                gc[:, cb, :], sem_name="dve",
                                waits=(hsi, h2, cmb4[6][cb]))
                    hy = fw.op(V, V.tensor_add, yc[:, cb, :], xnc[:, cb, :],
                               t1c[:, cb, :], sem_name="dve",
                               waits=(h3, cmb4[7][cb]))
                    hys = fw.dma(SY, y_out[b, dc * P:(dc + 1) * P, to:to + 512],
                                 yc[:, cb, :], f"ys{cb}", waits=(hy,))
                    cmb4[1][cb] = hu2
                    cmb4[2][cb] = ht1
                    cmb4[3][cb] = hy
                    cmb4[4][cb] = hsi
                    cmb4[5][cb] = ht1
                    cmb4[6][cb] = hy
                    cmb4[7][cb] = hys
    ctx8.close()

    return finish()


def kernel(**inputs):
    in_maps = _host_prep(inputs)
    nc = build_program(sim_strict=True)
    res = run_bass_kernel_spmd(nc, in_maps, core_ids=list(range(NCORE)))
    out = np.concatenate(
        [r["y"].transpose(0, 2, 1) for r in res.results], axis=0)
    return np.ascontiguousarray(out)


def _run_debug(inputs, dbg=(), limit=99, cores=(0,), serialize=False, p2_stop=False):
    in_maps = _host_prep(inputs)
    nc = build_program(dbg=dbg, limit=limit, serialize=serialize, p2_stop=p2_stop)
    res = run_bass_kernel_spmd(nc, [in_maps[c] for c in cores],
                               core_ids=list(cores))
    return res.results


# version 05
# speedup vs baseline: 1.2245x; 1.0146x over previous
import numpy as np
import ml_dtypes
import concourse.bass as bass
import concourse.mybir as mybir
from concourse.bass_utils import run_bass_kernel_spmd
from concourse.bass_types import AP
from contextlib import ExitStack

F32 = mybir.dt.float32
BF16 = mybir.dt.bfloat16
AF = mybir.ActivationFunctionType
ALU = mybir.AluOpType
AX = mybir.AxisListType

P = 128
S = 2048
D = 1024
NCH = D // P
NB = 2
NCORE = 8
PP = 2 * S - 1
H0 = 16
HD = 64

NEG = -1.0e30


class FW:
    def __init__(self, nc):
        self.nc = nc
        self.sems = {}
        self.counts = {}
        self.hwm = {}
        self.serialize = False
        self.tagmap = {}
        self.pool_i = {"s": 0, "g": 0}
        self.engines = []
        self.strict = False

    def sem(self, name):
        if name not in self.sems:
            self.sems[name] = self.nc.alloc_semaphore(name=name)
            self.counts[name] = 0
        return self.sems[name]

    def h(self, name):
        return (name, self.counts[name])

    def wait(self, eng, *handles):
        if self.serialize:
            handles = tuple(handles) + self.all_handles()
        for hh in handles:
            if hh is None:
                continue
            name, cnt = hh
            if cnt <= 0:
                continue
            key = (eng.engine, name)
            if self.hwm.get(key, 0) < cnt:
                eng.wait_ge(self.sems[name], cnt)
                self.hwm[key] = cnt

    def inc(self, inst, name, amount=1):
        inst.then_inc(self.sems[name], amount)
        self.counts[name] += amount
        return (name, self.counts[name])

    def incn(self, eng, name):
        return self.inc(eng.nop(nofuse=True), name)

    def dma(self, eng, out, in_, tag, waits=(), global_tag=False, **kw):
        if global_tag:
            s = "GT_" + tag + ("g" if eng.engine == mybir.EngineType.Pool
                               else "s")
            self.sem(s)
        else:
            kind = "g" if eng.engine == mybir.EngineType.Pool else "s"
            s = self._tag_sem(tag, kind)
        self.wait(eng, *waits)
        inst = eng.dma_start(out=out, in_=in_, **kw)
        return self.inc(inst, s, 16)

    def dmat(self, eng, out, in_, tag, waits=()):
        s = self._tag_sem(tag, "s")
        self.wait(eng, *waits)
        inst = eng.dma_start_transpose(out, in_)
        return self.inc(inst, s, 16)

    _ENG_SEM = {}

    def op(self, eng, fn, *args, sem_name=None, waits=(), **kw):
        if self.serialize and sem_name is None:
            sem_name = "dve" if eng.engine == mybir.EngineType.DVE else None
        if self.strict:
            en = {mybir.EngineType.DVE: "dve",
                  mybir.EngineType.Activation: "act"}.get(eng.engine)
            if en is not None:
                self.wait(eng, (en, self.counts[en]))
                if sem_name is None:
                    sem_name = en
        self.wait(eng, *waits)
        inst = fn(*args, **kw)
        if sem_name is not None:
            return self.inc(inst, sem_name, 1)
        return None

    def all_handles(self, include_global=False):
        return tuple(self.h(n) for n in self.sems
                     if include_global or not n.startswith("GT_"))

    def phase_barrier(self, engines=None):
        hs = self.all_handles()
        for eng in (engines or self.engines):
            self.wait(eng, *hs)

    def new_phase(self):
        self.phase_barrier()
        self.tagmap = {}
        self.pool_i = {"s": 0, "g": 0}

    def _tag_sem(self, tag, kind="s"):
        key = kind + tag
        if key not in self.tagmap:
            name = f"{kind}{self.pool_i[kind]}"
            self.pool_i[kind] += 1
            self.sem(name)
            self.tagmap[key] = name
        return self.tagmap[key]

    def ht(self, tag, kind="s"):
        return self.h(self._tag_sem(tag, kind))


def _host_consts():
    pos = np.arange(S - 1, -S, -1, dtype=np.float64)
    inv_freq = 1.0 / (10000.0 ** (np.arange(0, D, 2, dtype=np.float64) / D))
    ang = pos[:, None] * inv_freq[None, :]
    pe = np.concatenate([np.sin(ang), np.cos(ang)], axis=-1)
    pe_rev = pe[::-1]
    peT = np.zeros((D, PP + 1), dtype=np.float32)
    peT[:, :PP] = pe_rev.T.astype(np.float32)

    dec = np.exp(-np.arange(S, dtype=np.float64) / S).astype(np.float32)
    dec_pc = dec.reshape(S // P, P).T.copy()

    r = np.arange(P)
    trineg = np.where(r[None, :] <= r[:, None], 0.0, NEG).astype(np.float32)
    eyef = np.eye(P, dtype=np.float32)
    return peT, dec_pc, trineg, eyef


def _pc(v):
    return np.ascontiguousarray(v.reshape(-1, P).T)


def _host_prep(inputs):
    g = lambda k: np.asarray(inputs[k], np.float32)
    peT, dec_pc, trineg, eyef = _host_consts()
    com = {
        "WqT": np.ascontiguousarray(g("L0_attn_W_q").T),
        "WkT": np.ascontiguousarray(g("L0_attn_W_k").T),
        "WvT": np.ascontiguousarray(g("L0_attn_W_v").T),
        "WoT": np.ascontiguousarray(g("L0_attn_W_o").T),
        "bq_row": g("L0_attn_b_q")[None, :],
        "bk_row": g("L0_attn_b_k")[None, :],
        "bv_row": g("L0_attn_b_v")[None, :],
        "bo_pc": _pc(g("L0_attn_b_o")),
        "fwq": g("L0_ffn_wq"), "fwk": g("L0_ffn_wk"), "fwv": g("L0_ffn_wv"),
        "fwp": g("L0_ffn_wp"), "fwo": g("L0_ffn_wo"),
        "ln1g_pc": _pc(g("L0_ffn_ln1_g")), "ln1b_pc": _pc(g("L0_ffn_ln1_b")),
        "ln2g_pc": _pc(g("L0_ffn_ln2_g")), "ln2b_pc": _pc(g("L0_ffn_ln2_b")),
        "sel_w": g("L0_ffn_sel_w"),
        "keys": g("L0_ffn_keys"), "values": g("L0_ffn_values"),
        "a1wq": g("L1_attn_wq"), "a1wk": g("L1_attn_wk"), "a1wv": g("L1_attn_wv"),
        "a1bq_row": g("L1_attn_bq")[None, :],
        "a1bk_row": g("L1_attn_bk")[None, :],
        "a1bv_row": g("L1_attn_bv")[None, :],
        "a1wo": g("L1_attn_wo"),
        "a1bo_pc": _pc(g("L1_attn_bo")),
        "gam_pc": _pc(g("L1_ffn_gamma")), "bet_pc": _pc(g("L1_ffn_beta")),
        "WgT": np.ascontiguousarray(g("L1_ffn_Wg").T),
        "W1": g("L1_ffn_W1"),
        "b1_pc": np.ascontiguousarray(
            g("L1_ffn_b1").reshape(8, 32, P).transpose(0, 2, 1)),
        "W2": g("L1_ffn_W2"),
        "b2pc": np.ascontiguousarray(
            g("L1_ffn_b2").reshape(8, 8, P).transpose(2, 0, 1)),
        "peT": peT, "dec_pc": dec_pc, "trineg": trineg, "eyef": eyef,
    }
    x = np.asarray(inputs["L0_attn_x"], np.float32)
    in_maps = []
    for c in range(NCORE):
        xb = x[2 * c:2 * c + 2]
        m = dict(com)
        m["x"] = np.ascontiguousarray(xb.transpose(0, 2, 1))
        in_maps.append(m)
    return in_maps


def build_program(dbg=(), limit=99, serialize=False, p2_stop=False, sim_strict=False):
    nc = bass.Bass("TRN2", target_bir_lowering=False, debug=False)
    fw = FW(nc)
    fw.serialize = serialize
    fw.strict = sim_strict
    for s_ in ["pe", "act", "dve"]:
        fw.sem(s_)

    def din(name, shape):
        return nc.dram_tensor(name, shape, F32, kind="ExternalInput").ap()

    x_in = din("x", [NB, D, S])
    WqT, WkT, WvT, WoT = (din(n, [D, D]) for n in ["WqT", "WkT", "WvT", "WoT"])
    bq_row, bk_row, bv_row = (din(n, [1, D]) for n in ["bq_row", "bk_row", "bv_row"])
    bo_pc = din("bo_pc", [P, NCH])
    fwq, fwk, fwv, fwp, fwo = (din(n, [D, D]) for n in
                               ["fwq", "fwk", "fwv", "fwp", "fwo"])
    ln1g_pc, ln1b_pc = din("ln1g_pc", [P, NCH]), din("ln1b_pc", [P, NCH])
    ln2g_pc, ln2b_pc = din("ln2g_pc", [P, NCH]), din("ln2b_pc", [P, NCH])
    sel_w = din("sel_w", [D, 32])
    keys = din("keys", [32, D, 128])
    values = din("values", [32, 128, D])
    a1wq, a1wk, a1wv = din("a1wq", [D, 64]), din("a1wk", [D, 64]), din("a1wv", [D, 64])
    a1bq_row, a1bk_row, a1bv_row = (din(n, [1, 64]) for n in
                                    ["a1bq_row", "a1bk_row", "a1bv_row"])
    a1wo = din("a1wo", [64, D])
    a1bo_pc = din("a1bo_pc", [P, NCH])
    gam_pc, bet_pc = din("gam_pc", [P, NCH]), din("bet_pc", [P, NCH])
    WgT = din("WgT", [D, 8])
    W1 = din("W1", [8, D, 4096])
    b1_pc = din("b1_pc", [8, P, 32])
    W2 = din("W2", [8, 4096, D])
    b2pc = din("b2pc", [P, 8, NCH])
    peT = din("peT", [D, PP + 1])
    dec_pc = din("dec_pc", [P, S // P])
    trineg_in = din("trineg", [P, P])
    eyef_in = din("eyef", [P, P])

    y_out = nc.dram_tensor("y", [NB, D, S], F32, kind="ExternalOutput").ap()

    scr = {}

    def dscr(name, shape, dtype):
        kind = "ExternalOutput" if name in dbg else "Internal"
        t = nc.dram_tensor(name, shape, dtype, kind=kind)
        scr[name] = t.ap()
        return scr[name]

    xn_d = dscr("xn_d", [NB, D, S], BF16)
    ccos_d = dscr("ccos_d", [NB, D, S], BF16)
    lnr_d = dscr("lnr_d", [NB, 4, 3, 512], F32)
    p2_d = dscr("p2_d", [D, PP + 1], BF16)
    qf_d = dscr("qf_d", [NB, S, D], BF16)
    s_d = dscr("s_d", [NB, D, S], BF16)
    x1_d = dscr("x1_d", [NB, D, S], BF16)
    qa_d = dscr("qa_d", [NB, D, S], BF16)
    ka_d = dscr("ka_d", [NB, D, S], BF16)
    v_d = dscr("v_d", [NB, NCH, S, P], BF16)
    bd_d = dscr("bd_d", [4, P, 2176], BF16)
    den_d = dscr("den_d", [16, P], F32)
    src2_d = dscr("src2_d", [NB, D, S], BF16)
    x2_d = dscr("x2_d", [NB, D, S], BF16)
    gates_d = dscr("gates_d", [NB, 32, S], F32)
    xnew_d = dscr("xnew_d", [NB, D, S], F32)
    xnewb_d = dscr("xnewb_d", [NB, D, S], BF16)
    xn1_d = dscr("xn1_d", [NB, D, S], BF16)
    s1_d = dscr("s1_d", [NB, D, S], BF16)
    xnorm_d = dscr("xnorm_d", [NB, D, S], BF16)
    g8_d = dscr("g8_d", [NB, 8, S], F32)
    w1p_d = dscr("w1p_d", [8, 8, P, 4096], BF16)
    w2p_d = dscr("w2p_d", [8, 8, P, 4 * D], BF16)
    keyp_d = dscr("keyp_d", [32, P, NCH * P], BF16)
    valp_d = dscr("valp_d", [32, P, D], BF16)

    ctx = ExitStack()
    e_ = ctx.enter_context

    PS = [e_(nc.psum_tensor(f"ps{i}", [P, 512], F32)) for i in range(8)]
    ps_free = [None] * 8

    ones_bf = e_(nc.sbuf_tensor("ones_bf", [1, P], BF16))
    onesc_bf = e_(nc.sbuf_tensor("onesc_bf", [P, 1], BF16))
    ones_f = e_(nc.sbuf_tensor("ones_f", [1, P], F32))
    eps5 = e_(nc.sbuf_tensor("eps5", [P, 1], F32))
    eps6 = e_(nc.sbuf_tensor("eps6", [P, 1], F32))
    zcol = e_(nc.sbuf_tensor("zcol", [P, 1], F32))
    trineg_s = e_(nc.sbuf_tensor("trineg_s", [P, P], F32))
    eyef_s = e_(nc.sbuf_tensor("eyef_s", [P, P], F32))
    eyeb_s = e_(nc.sbuf_tensor("eyeb_s", [P, P], BF16))
    dec_s = e_(nc.sbuf_tensor("dec_s", [P, S // P], F32))
    pcs = {}
    for nm, apin in [("bo_pc", bo_pc), ("ln1g_pc", ln1g_pc), ("ln1b_pc", ln1b_pc),
                     ("ln2g_pc", ln2g_pc), ("ln2b_pc", ln2b_pc),
                     ("a1bo_pc", a1bo_pc), ("gam_pc", gam_pc), ("bet_pc", bet_pc)]:
        pcs[nm] = e_(nc.sbuf_tensor(nm + "_s", [P, NCH], F32))
    maxp2 = e_(nc.sbuf_tensor("maxp2", [1, NCH], F32))

    V = nc.vector
    A = nc.scalar
    T = nc.tensor
    G = nc.gpsimd
    SY = nc.sync
    fw.engines = [T, A, V, SY, G]

    fw.op(V, V.memset, ones_bf[:], 1.0, sem_name="dve")
    fw.op(V, V.memset, onesc_bf[:], 1.0, sem_name="dve")
    fw.op(V, V.memset, ones_f[:], 1.0, sem_name="dve")
    fw.op(V, V.memset, eps5[:], 1e-5, sem_name="dve")
    fw.op(V, V.memset, eps6[:], 1e-6, sem_name="dve")
    fw.op(V, V.memset, zcol[:], 0.0, sem_name="dve")
    fw.dma(SY, trineg_s[:], trineg_in, "cst")
    fw.dma(SY, eyef_s[:], eyef_in, "cst")
    fw.dma(G, eyeb_s[:], eyef_in, "cstp")
    fw.dma(SY, dec_s[:], dec_pc, "cst")
    for nm, apin in [("bo_pc", bo_pc), ("ln1g_pc", ln1g_pc), ("ln1b_pc", ln1b_pc),
                     ("ln2g_pc", ln2g_pc), ("ln2b_pc", ln2b_pc),
                     ("a1bo_pc", a1bo_pc), ("gam_pc", gam_pc), ("bet_pc", bet_pc)]:
        fw.dma(SY, pcs[nm][:], apin, "cst")
    consts_ld = (fw.ht("cst"), fw.ht("cstp", "g"), fw.h("dve"))

    with (nc.sbuf_tensor("peb", [P, NCH, PP + 1], BF16) as peb,
          nc.sbuf_tensor("wpb", [P, NCH, D], BF16) as wpb,
          nc.sbuf_tensor("p2t", [P, PP + 1], BF16) as p2t,
          nc.sbuf_tensor("p2sq", [P, PP + 1], BF16) as p2sq,
          nc.sbuf_tensor("prow", [1, 512], F32) as prow,
          nc.sbuf_tensor("pmax", [1, 1], F32) as pmax):
        p2t_free = None
        p2sq_free = None
        hpe = fw.dma(G, peb[:], peT.rearrange("(c p) q -> p c q", p=P), "peb")
        hwp = fw.dma(G, wpb[:], fwp.rearrange("(c p) m -> p c m", p=P), "wpb")
        for e_w in range(8):
            for fg_ in range(8):
                fw.dma(G, w1p_d[e_w, fg_].rearrange("p (c f) -> p c f", c=NCH),
                       W1[e_w][:, fg_ * 512:(fg_ + 1) * 512].rearrange(
                           "(c p) f -> p c f", p=P), "wpc", global_tag=True)
                fw.dma(G, w2p_d[e_w, fg_].rearrange("p (c d) -> p c d", c=4),
                       W2[e_w, fg_ * 512:(fg_ + 1) * 512, :].rearrange(
                           "(c p) d -> p c d", p=P), "wpc", global_tag=True)
        for e_w in range(32):
            fw.dma(G, keyp_d[e_w].rearrange("p (c f) -> p c f", c=NCH),
                   keys[e_w].rearrange("(c p) f -> p c f", p=P), "kpc",
                   global_tag=True)
            fw.dma(G, valp_d[e_w], values[e_w], "kpc", global_tag=True)
        wpc_done = fw.h("GT_wpcg")
        kpc_done = fw.h("GT_kpcg")
        NSL = (PP + 1) // 512
        for mc in range(NCH):
            for sl in range(NSL):
                psi = sl % 2
                fw.wait(T, hpe, hwp, ps_free[psi], *consts_ld)
                for dc in range(NCH):
                    inst = T.matmul(PS[psi][:], wpb[:, dc, mc * P:(mc + 1) * P],
                                    peb[:, dc, sl * 512:(sl + 1) * 512],
                                    start=(dc == 0), stop=(dc == NCH - 1))
                hmm = fw.inc(inst, "pe")
                hact = fw.op(A, A.activation, p2t[:, sl * 512:(sl + 1) * 512],
                             PS[psi][:], AF.Identity, sem_name="act",
                             waits=(hmm, p2t_free))
                ps_free[psi] = hact
            hst = fw.dma(SY, p2_d[mc * P:(mc + 1) * P, :], p2t[:],
                         "p2s", waits=(fw.h("act"),))
            hsq = fw.op(A, A.activation, p2sq[:], p2t[:], AF.Square,
                        sem_name="act", waits=(fw.h("act"), p2sq_free))
            p2t_free = hst
            fw.op(V, V.memset, pmax[:], 0.0, sem_name="dve")
            for sl in range(NSL):
                psi = 2 + sl % 2
                fw.wait(T, hsq, ps_free[psi], *consts_ld)
                inst = T.matmul(PS[psi][0:1, :], onesc_bf[:],
                                p2sq[:, sl * 512:(sl + 1) * 512],
                                start=True, stop=True)
                hmm = fw.inc(inst, "pe")
                p2sq_free = hmm
                fw.op(V, V.reduce_max, prow[:, 0:1], PS[psi][0:1, :], axis=AX.X,
                      sem_name="dve", waits=(hmm,))
                hdv = fw.op(V, V.tensor_max, pmax[:], pmax[:], prow[:, 0:1],
                            sem_name="dve")
                ps_free[psi] = hdv
            fw.op(V, V.tensor_copy, maxp2[:, mc:mc + 1], pmax[:],
                  sem_name="dve")
        p0_done = (fw.h("dve"), fw.h("act"), fw.h("pe"))

    def finish():
        for hh in fw.all_handles(include_global=True):
            fw.wait(SY, hh)
        ctx.close()
        return nc

    if limit < 1:
        return finish()

    ln_ctr = [0]

    def emit_ln(src_tile_dma, dst_d, gain=None, bias=None, src_f32=False):
        ln_ctr[0] += 1
        fw.new_phase()
        tg = f"ln{ln_ctr[0]}"
        with (nc.sbuf_tensor(tg + "_x", [P, NCH, 512], F32 if src_f32 else BF16) as xt,
              nc.sbuf_tensor(tg + "_xb", [P, NCH, 512], BF16) as xb,
              nc.sbuf_tensor(tg + "_sq", [P, NCH, 512], BF16) as sqt,
              nc.sbuf_tensor(tg + "_o", [P, NCH, 512], BF16) as ot,
              nc.sbuf_tensor(tg + "_rb", [1, 512], BF16) as rbr,
              nc.sbuf_tensor(tg + "_cb", [1, 512], BF16) as cbr,
              nc.sbuf_tensor(tg + "_m", [1, 512], F32) as mr,
              nc.sbuf_tensor(tg + "_q", [1, 512], F32) as qr,
              nc.sbuf_tensor(tg + "_rr", [1, 512], F32) as rr,
              nc.sbuf_tensor(tg + "_cr", [1, 512], F32) as cr):
            xt_free = [None] * NCH
            ot_free = [None] * NCH
            sq_free = [None] * NCH
            xb_free = [None] * NCH
            row_free = None
            ps_s, ps_q, ps_r, ps_c = PS[0], PS[1], PS[2], PS[3]
            for b in range(NB):
                for tt in range(S // 512):
                    t0 = tt * 512
                    hld = []
                    for dc in range(NCH):
                        hld.append(src_tile_dma(b, dc, t0, xt[:, dc, :],
                                                (xt_free[dc],)))
                    for dc in range(NCH):
                        if src_f32:
                            fw.op(A, A.activation, xb[:, dc, :], xt[:, dc, :],
                                  AF.Identity, sem_name="act",
                                  waits=(hld[dc], xb_free[dc]))
                            src_bf = xb[:, dc, :]
                            hsrc = fw.h("act")
                        else:
                            src_bf = xt[:, dc, :]
                            hsrc = hld[dc]
                        hsq = fw.op(A, A.activation, sqt[:, dc, :], src_bf,
                                    AF.Square, sem_name="act",
                                    waits=(hsrc, sq_free[dc]))
                        fw.wait(T, ps_free[0], ps_free[1], hsrc, hsq)
                        T.matmul(ps_s[0:1, :], onesc_bf[:], src_bf,
                                 start=(dc == 0), stop=(dc == NCH - 1))
                        inst2 = T.matmul(ps_q[0:1, :], onesc_bf[:], sqt[:, dc, :],
                                         start=(dc == 0), stop=(dc == NCH - 1))
                    hmm = fw.inc(inst2, "pe")
                    for dc in range(NCH):
                        sq_free[dc] = hmm
                        xb_free[dc] = hmm
                    fw.op(V, V.tensor_scalar, mr[:], ps_s[0:1, :], 1.0 / D, None,
                          ALU.mult, sem_name="dve", waits=(hmm,))
                    fw.op(V, V.tensor_scalar, qr[:], ps_q[0:1, :], 1.0 / D, None,
                          ALU.mult, sem_name="dve")
                    ps_free[0] = ps_free[1] = fw.h("dve")
                    fw.op(V, V.tensor_mul, rr[:], mr[:], mr[:], sem_name="dve")
                    fw.op(V, V.tensor_sub, qr[:], qr[:], rr[:], sem_name="dve")
                    fw.op(A, A.activation, qr[:], qr[:], AF.Ln, bias=eps5[0:1, :],
                          sem_name="act", waits=(fw.h("dve"),))
                    fw.op(A, A.activation, rr[:], qr[:], AF.Exp, scale=-0.5,
                          sem_name="act")
                    fw.wait(V, fw.h("act"))
                    fw.op(V, V.tensor_mul, cr[:], mr[:], rr[:], sem_name="dve")
                    fw.op(V, V.tensor_scalar, cr[:], cr[:], -1.0, None, ALU.mult,
                          sem_name="dve")
                    fw.op(A, A.activation, rbr[:], rr[:], AF.Identity,
                          sem_name="act", waits=(row_free, fw.h("dve")))
                    fw.op(A, A.activation, cbr[:], cr[:], AF.Identity,
                          sem_name="act")
                    hrows = fw.h("act")
                    fw.wait(T, hrows, ps_free[2], ps_free[3])
                    T.matmul(ps_r[:], ones_bf[:], rbr[:], start=True, stop=True)
                    inst = T.matmul(ps_c[:], ones_bf[:], cbr[:],
                                    start=True, stop=True)
                    hbc = fw.inc(inst, "pe")
                    row_free = hbc
                    for dc in range(NCH):
                        fw.op(V, V.tensor_mul, xt[:, dc, :], xt[:, dc, :], ps_r[:],
                              sem_name="dve", waits=(hbc, hld[dc]))
                        if gain is None:
                            hn = fw.op(V, V.tensor_add, ot[:, dc, :], xt[:, dc, :],
                                       ps_c[:], sem_name="dve",
                                       waits=(ot_free[dc],))
                        else:
                            fw.op(V, V.tensor_add, xt[:, dc, :], xt[:, dc, :],
                                  ps_c[:], sem_name="dve")
                            hn = fw.op(V, V.tensor_scalar, ot[:, dc, :],
                                       xt[:, dc, :], gain[:, dc:dc + 1],
                                       bias[:, dc:dc + 1], ALU.mult, ALU.add,
                                       sem_name="dve", waits=(ot_free[dc],))
                        xt_free[dc] = hn
                        ot_free[dc] = fw.dma(
                            SY, dst_d[b, dc * P:(dc + 1) * P, t0:t0 + 512],
                            ot[:, dc, :], f"lno{dc}", waits=(hn,))
                    ps_free[2] = ps_free[3] = fw.h("dve")

    fw.new_phase()

    def x_tile(b, dc, t0, tile_ap, waits):
        return fw.dma(SY, tile_ap, x_in[b, dc * P:(dc + 1) * P, t0:t0 + 512],
                      f"lnx{dc}", waits=waits)

    emit_ln(x_tile, xn_d, src_f32=True)

    if limit < 2:
        return finish()

    class Rot:
        def __init__(self, idxs):
            self.idxs = list(idxs)
            self.i = 0

        def get(self):
            idx = self.idxs[self.i % len(self.idxs)]
            self.i += 1
            fw.wait(T, ps_free[idx])
            return idx

    fw.new_phase()
    ctx2 = ExitStack()
    e2 = ctx2.enter_context
    if True:
        wqTb = e2(nc.sbuf_tensor("wqTb", [P, NCH, D], BF16))
        wkTb = e2(nc.sbuf_tensor("wkTb", [P, NCH, D], BF16))
        wvTb = e2(nc.sbuf_tensor("wvTb", [P, NCH, D], BF16))
        woTb = e2(nc.sbuf_tensor("woTb", [P, NCH, D], BF16))
        bqr = e2(nc.sbuf_tensor("bqr", [1, D], BF16))
        bkr = e2(nc.sbuf_tensor("bkr", [1, D], BF16))
        bvr = e2(nc.sbuf_tensor("bvr", [1, D], BF16))
        xnf = e2(nc.sbuf_tensor("xnf", [P, NCH, S], BF16))
        kfr = e2(nc.sbuf_tensor("kfr", [P, 16, D], BF16))
        vfr = e2(nc.sbuf_tensor("vfr", [P, 16, D], BF16))
        kvall = e2(nc.sbuf_tensor("kvall", [P, 8 * 65], BF16))
        qtile = e2(nc.sbuf_tensor("qtile", [P, 2, D], BF16))
        qf2 = e2(nc.sbuf_tensor("qf2", [P, S], BF16))
        nume = e2(nc.sbuf_tensor("nume", [64, S], BF16))
        denr = e2(nc.sbuf_tensor("denr", [1, S], F32))
        rdenb = e2(nc.sbuf_tensor("rdenb", [1, S], BF16))
        cmul = e2(nc.sbuf_tensor("cmul", [64, 512], BF16))
        cct = e2(nc.sbuf_tensor("cct", [P, NCH, 512], BF16))
        stile = e2(nc.sbuf_tensor("stile", [P, 2, 512], BF16))
        for wb_, wT in [(wqTb, WqT), (wkTb, WkT), (wvTb, WvT), (woTb, WoT)]:
            fw.dma(G, wb_[:], wT.rearrange("(c p) m -> p c m", p=P), "w2l")
        for rb_, rr_ in [(bqr, bq_row), (bkr, bk_row), (bvr, bv_row)]:
            fw.dma(G, rb_[:], rr_, "w2l")
        hwld = fw.ht("w2l", "g")
        rot6 = Rot(range(6))
        qt_free = [None, None]
        kv_free = None
        qf2_free = None
        nume_free = None
        denr_free = None
        rdenb_free = None
        cmul_free = None
        cct_free = [None] * NCH
        st_free = [None, None]
        for b in range(NB):
            hxn = fw.dma(SY, xnf[:], xn_d[b].rearrange("(c p) t -> p c t", p=P),
                         "xnf", waits=(fw.h("pe"), fw.h("act")))
            for tc in range(16):
                for proj in range(3):
                    wb_ = (wqTb, wkTb, wvTb)[proj]
                    br_ = (bqr, bkr, bvr)[proj]
                    for ms in range(2):
                        psi = rot6.get()
                        fw.wait(T, hwld, hxn)
                        BIAS_MM = True
                        if BIAS_MM:
                            T.matmul(PS[psi][:], ones_bf[:],
                                     br_[0:1, ms * 512:(ms + 1) * 512],
                                     start=True, stop=False)
                        for dc in range(NCH):
                            inst = T.matmul(PS[psi][:],
                                            xnf[:, dc, tc * P:(tc + 1) * P],
                                            wb_[:, dc, ms * 512:(ms + 1) * 512],
                                            start=(dc == 0 and not BIAS_MM),
                                            stop=(dc == NCH - 1))
                        hmm = fw.inc(inst, "pe")
                        if proj == 0:
                            dst = qtile[:, tc % 2, ms * 512:(ms + 1) * 512]
                            wfree = qt_free[tc % 2]
                        elif proj == 1:
                            dst = kfr[:, tc, ms * 512:(ms + 1) * 512]
                            wfree = None
                        else:
                            dst = vfr[:, tc, ms * 512:(ms + 1) * 512]
                            wfree = None
                        if proj < 2:
                            hev = fw.op(A, A.activation, dst, PS[psi][:], AF.Relu,
                                        scale=dec_s[:, tc:tc + 1], sem_name="act",
                                        waits=(hmm, wfree))
                        else:
                            hev = fw.op(A, A.activation, dst, PS[psi][:],
                                        AF.Identity, sem_name="act",
                                        waits=(hmm, wfree))
                        ps_free[psi] = hev
                qt_free[tc % 2] = fw.dma(
                    SY, qf_d[b, tc * P:(tc + 1) * P, :], qtile[:, tc % 2, :],
                    f"qt{tc % 2}", waits=(fw.h("act"),))
            if p2_stop:
                continue
            p2a_done = fw.h("act")
            for h in range(16):
                psi = rot6.get()
                fw.wait(T, kv_free, p2a_done)
                for tc in range(16):
                    T.matmul(PS[psi][0:64, 0:64],
                             kfr[:, tc, h * 64:h * 64 + 64],
                             vfr[:, tc, h * 64:h * 64 + 64],
                             start=(tc == 0), stop=(tc == 15))
                for tc in range(16):
                    inst = T.matmul(PS[psi][0:64, 64:65],
                                    kfr[:, tc, h * 64:h * 64 + 64],
                                    onesc_bf[:], start=(tc == 0), stop=(tc == 15),
                                    skip_group_check=True)
                hmm = fw.inc(inst, "pe")
                rb0 = (h % 2) * 64
                hev = fw.op(A, A.activation,
                            kvall[rb0:rb0 + 64, (h // 2) * 65:(h // 2 + 1) * 65],
                            PS[psi][0:64, 0:65], AF.Identity, sem_name="act",
                            waits=(hmm,))
                ps_free[psi] = hev
            kv_done = fw.h("act")
            for hp in range(NCH):
                hqf = fw.dmat(SY, qf2[:], qf_d[b][:, hp * P:(hp + 1) * P], "qf2",
                              waits=(qf2_free, qt_free[0], qt_free[1]))
                for head in range(2):
                    hq = head * 64
                    hh = hp * 2 + head
                    pss = [rot6.get() for _ in range(2)]
                    for sl in range(4):
                        psi = pss[sl % 2]
                        if sl >= 2:
                            fw.wait(T, ps_free[psi])
                        fw.wait(T, kv_done, hqf)
                        inst = T.matmul(PS[psi][0:65, :],
                                        kvall[hq:hq + 64, hp * 65:(hp + 1) * 65],
                                        qf2[hq:hq + 64, sl * 512:(sl + 1) * 512],
                                        start=True, stop=True)
                        hmm = fw.inc(inst, "pe")
                        fw.op(A, A.activation, denr[0:1, sl * 512:(sl + 1) * 512],
                              PS[psi][64:65, :], AF.Identity, bias=eps6[0:1, :],
                              sem_name="act", waits=(hmm, denr_free))
                        hev = fw.op(A, A.activation,
                                    nume[:, sl * 512:(sl + 1) * 512],
                                    PS[psi][0:64, :], AF.Identity,
                                    sem_name="act", waits=(nume_free,))
                        ps_free[psi] = hev
                    hrec = fw.op(V, V.reciprocal, denr[:], denr[:],
                                 sem_name="dve", waits=(fw.h("act"),))
                    denr_free = hrec
                    hrd = fw.op(A, A.activation, rdenb[:], denr[:], AF.Identity,
                                sem_name="act", waits=(hrec, rdenb_free))
                    for sl in range(4):
                        psi = rot6.get()
                        fw.wait(T, hrd)
                        inst = T.matmul(PS[psi][0:64, :], ones_bf[0:1, 0:64],
                                        rdenb[0:1, sl * 512:(sl + 1) * 512],
                                        start=True, stop=True)
                        hmm = fw.inc(inst, "pe")
                        rdenb_free = hmm
                        hmul = fw.op(V, V.tensor_mul, cmul[:],
                                     nume[:, sl * 512:(sl + 1) * 512],
                                     PS[psi][0:64, :], sem_name="dve",
                                     waits=(hmm, cmul_free))
                        ps_free[psi] = hmul
                        nume_free = hmul
                        cmul_free = fw.dma(
                            SY, scr["ccos_d"][b, hp * P + hq:hp * P + hq + 64,
                                              sl * 512:(sl + 1) * 512],
                            cmul[:], "cm", waits=(hmul,))
                qf2_free = fw.h("pe")
            cc_done = fw.ht("cm")
            for ts in range(4):
                hcc = []
                for hc in range(NCH):
                    hcc.append(fw.dma(
                        SY, cct[:, hc, :],
                        scr["ccos_d"][b, hc * P:(hc + 1) * P,
                                      ts * 512:(ts + 1) * 512],
                        f"cct{hc}", waits=(cc_done, cct_free[hc])))
                for dc in range(NCH):
                    psi = rot6.get()
                    fw.wait(T, *hcc)
                    for hc in range(NCH):
                        inst = T.matmul(PS[psi][:], woTb[:, hc, dc * P:(dc + 1) * P],
                                        cct[:, hc, :], start=(hc == 0),
                                        stop=(hc == NCH - 1))
                    hmm = fw.inc(inst, "pe")
                    sb = dc % 2
                    hev = fw.op(A, A.activation, stile[:, sb, :], PS[psi][:],
                                AF.Identity, bias=pcs["bo_pc"][:, dc:dc + 1],
                                sem_name="act", waits=(hmm, st_free[sb]))
                    ps_free[psi] = hev
                    st_free[sb] = fw.dma(
                        SY, s_d[b, dc * P:(dc + 1) * P, ts * 512:(ts + 1) * 512],
                        stile[:, sb, :], f"sst{sb}", waits=(hev,))
                for hc in range(NCH):
                    cct_free[hc] = fw.h("pe")

    ctx2.close()
    fw.new_phase()

    if limit < 3:
        return finish()

    def xn_tile(b, dc, t0, tile_ap, waits):
        return fw.dma(SY, tile_ap, xn_d[b, dc * P:(dc + 1) * P, t0:t0 + 512],
                      f"lnx{dc}", waits=waits)

    fw.new_phase()
    emit_ln(xn_tile, x1_d, gain=pcs["ln1g_pc"], bias=pcs["ln1b_pc"])

    fw.new_phase()
    ctx3 = ExitStack()
    e3 = ctx3.enter_context
    if True:
        wqb3 = e3(nc.sbuf_tensor("wqb3", [P, NCH, D], BF16))
        wkb3 = e3(nc.sbuf_tensor("wkb3", [P, NCH, D], BF16))
        wvb3 = e3(nc.sbuf_tensor("wvb3", [P, NCH, D], BF16))
        x1f = e3(nc.sbuf_tensor("x1f", [P, NCH, S], BF16))
        qmt = e3(nc.sbuf_tensor("qmt", [P, 2, S], BF16))
        vt3 = e3(nc.sbuf_tensor("vt3", [P, 2, D], BF16))
        for wb_, win in [(wqb3, fwq), (wkb3, fwk), (wvb3, fwv)]:
            fw.dma(G, wb_[:], win.rearrange("(c p) m -> p c m", p=P), "w3l")
        hwld3 = fw.ht("w3l", "g")
        rot = Rot(range(6))
        qm_free = [None, None]
        vt_free = [None, None]
        for b in range(NB):
            hx1 = fw.dma(SY, x1f[:], x1_d[b].rearrange("(c p) t -> p c t", p=P),
                         "x1f", waits=(fw.h("pe"),))
            for wb_, dst_d in [(wqb3, qa_d), (wkb3, ka_d)]:
                for mc in range(NCH):
                    for ts in range(4):
                        psi = rot.get()
                        fw.wait(T, hwld3, hx1)
                        for dc in range(NCH):
                            inst = T.matmul(PS[psi][:],
                                            wb_[:, dc, mc * P:(mc + 1) * P],
                                            x1f[:, dc, ts * 512:(ts + 1) * 512],
                                            start=(dc == 0), stop=(dc == NCH - 1))
                        hmm = fw.inc(inst, "pe")
                        hev = fw.op(A, A.activation,
                                    qmt[:, mc % 2, ts * 512:(ts + 1) * 512],
                                    PS[psi][:], AF.Identity, sem_name="act",
                                    waits=(hmm, qm_free[mc % 2]))
                        ps_free[psi] = hev
                    qm_free[mc % 2] = fw.dma(
                        SY, dst_d[b, mc * P:(mc + 1) * P, :], qmt[:, mc % 2, :],
                        f"qm{mc % 2}", waits=(fw.h("act"),))
            for tc in range(16):
                for hp in range(NCH):
                    psi = rot.get()
                    fw.wait(T, hwld3, hx1)
                    for dc in range(NCH):
                        inst = T.matmul(PS[psi][0:P, 0:P],
                                        x1f[:, dc, tc * P:(tc + 1) * P],
                                        wvb3[:, dc, hp * P:(hp + 1) * P],
                                        start=(dc == 0), stop=(dc == NCH - 1))
                    hmm = fw.inc(inst, "pe")
                    hev = fw.op(A, A.activation, vt3[:, tc % 2, hp * P:(hp + 1) * P],
                                PS[psi][0:P, 0:P], AF.Identity, sem_name="act",
                                waits=(hmm, vt_free[tc % 2]))
                    ps_free[psi] = hev
                for hp in range(NCH):
                    fw.dma(SY, v_d[b, hp, tc * P:(tc + 1) * P, :],
                           vt3[:, tc % 2, hp * P:(hp + 1) * P], f"vt{tc % 2}",
                           waits=(fw.h("act"),))
                vt_free[tc % 2] = fw.ht(f"vt{tc % 2}")
    ctx3.close()

    fw.new_phase()
    ctx4 = ExitStack()
    e4 = ctx4.enter_context
    if True:
        q2 = e4(nc.sbuf_tensor("q2", [P, S], BF16))
        k2 = e4(nc.sbuf_tensor("k2", [P, S], BF16))
        v2 = e4(nc.sbuf_tensor("v2", [P, 16, P], BF16))
        p2sl = e4(nc.sbuf_tensor("p2sl", [P, PP + 1], BF16))
        sqb = e4(nc.sbuf_tensor("sqb", [P, S], BF16))
        S_f = e4(nc.sbuf_tensor("S_f", [P, S], F32))
        bdsh = [e4(nc.sbuf_tensor(f"bdsh{i}", [P, S], BF16)) for i in range(2)]
        ful = e4(nc.sbuf_tensor("ful", [P, 2176], BF16))
        Ptile = e4(nc.sbuf_tensor("Ptile", [P, S], BF16))
        denall = e4(nc.sbuf_tensor("denall", [P, 16], F32))
        den16 = e4(nc.sbuf_tensor("den16", [16, P], F32))
        rdenr = e4(nc.sbuf_tensor("rdenr", [1, S], F32))
        rdenb2 = e4(nc.sbuf_tensor("rdenb2", [1, S], BF16))
        negBc = e4(nc.sbuf_tensor("negBc", [P, 1], F32))
        bnds = e4(nc.sbuf_tensor("bnds", [1, 8], F32))
        nrow = e4(nc.sbuf_tensor("nrow", [1, S], F32))
        PTf = e4(nc.sbuf_tensor("PTf", [P, 16, S], BF16))
        zsl = e4(nc.sbuf_tensor("zsl", [64, 512], BF16))
        zmul = e4(nc.sbuf_tensor("zmul", [64, 512], BF16))
        bd_t = scr["bd_d"].tensor
        den_t = scr["den_d"].tensor
        rot = Rot(range(4))
        ld_free = None
        denall_free = None
        den16_free = None
        rdenr_free = None
        rdenb2_free = None
        nrow_free = None
        bnds_free = None
        ful_free = None
        ring_free = [None] * 4
        Ptile_free = None
        zsl_free = None
        zmul_free = None
        bdsh_free = [None, None]
        S_f_free = None
        for b in range(NB):
            for hp in range(NCH):
                hq2 = fw.dma(SY, q2[:], qa_d[b, hp * P:(hp + 1) * P, :], "q2",
                             waits=(ld_free,))
                hk2 = fw.dma(SY, k2[:], ka_d[b, hp * P:(hp + 1) * P, :], "k2")
                hv2 = fw.dma(SY, v2[:],
                             v_d[b, hp].rearrange("(c p) f -> p c f", p=P), "v2")
                hp2 = fw.dma(SY, p2sl[:], p2_d[hp * P:(hp + 1) * P, :], "p2l")
                for src_, col in [(q2, 0), (k2, 1)]:
                    hsq = fw.op(A, A.activation, sqb[:], src_[:], AF.Square,
                                sem_name="act",
                                waits=(hq2 if col == 0 else hk2, fw.h("pe")))
                    for sl in range(4):
                        psi = rot.get()
                        fw.wait(T, hsq)
                        inst = T.matmul(PS[psi][0:1, 0:512], onesc_bf[:],
                                        sqb[:, sl * 512:(sl + 1) * 512],
                                        start=True, stop=True)
                        hmm = fw.inc(inst, "pe")
                        hev = fw.op(A, A.activation,
                                    nrow[0:1, sl * 512:(sl + 1) * 512],
                                    PS[psi][0:1, 0:512], AF.Identity,
                                    sem_name="act", waits=(hmm, nrow_free))
                        ps_free[psi] = hev
                    nrow_free = fw.op(V, V.reduce_max, bnds[0:1, col:col + 1],
                                      nrow[:], axis=AX.X, sem_name="dve",
                                      waits=(fw.h("act"), bnds_free))
                fw.op(A, A.activation, bnds[0:1, 3:4], bnds[0:1, 0:1], AF.Sqrt,
                      sem_name="act", waits=(fw.h("dve"),))
                fw.op(A, A.activation, bnds[0:1, 4:5], bnds[0:1, 1:2], AF.Sqrt,
                      sem_name="act")
                fw.op(A, A.activation, bnds[0:1, 5:6], maxp2[0:1, hp:hp + 1],
                      AF.Sqrt, sem_name="act")
                fw.op(V, V.tensor_add, bnds[0:1, 6:7], bnds[0:1, 4:5],
                      bnds[0:1, 5:6], sem_name="dve", waits=(fw.h("act"),))
                fw.op(V, V.tensor_mul, bnds[0:1, 6:7], bnds[0:1, 6:7],
                      bnds[0:1, 3:4], sem_name="dve")
                hB = fw.op(V, V.tensor_scalar, bnds[0:1, 7:8], bnds[0:1, 6:7],
                           -0.125, None, ALU.mult, sem_name="dve")
                psi = rot.get()
                fw.wait(T, hB)
                inst = T.matmul(PS[psi][0:P, 0:1], ones_f[:], bnds[0:1, 7:8],
                                start=True, stop=True)
                hmm = fw.inc(inst, "pe")
                bnds_free = hmm
                hnB = fw.op(A, A.activation, negBc[:], PS[psi][0:P, 0:1],
                            AF.Identity, sem_name="act", waits=(hmm,))
                ps_free[psi] = hnB
                for head in range(2):
                    hq = head * 64
                    for it in range(16):
                        base = (S - P) - it * P
                        off = 0
                        for w_ in (512, 512, 512, 512, 128):
                            psi = rot.get()
                            fw.wait(T, hq2, hp2)
                            inst = T.matmul(PS[psi][0:P, 0:w_],
                                            q2[hq:hq + 64, it * P:(it + 1) * P],
                                            p2sl[hq:hq + 64,
                                                 base + off:base + off + w_],
                                            start=True, stop=True)
                            hmm = fw.inc(inst, "pe")
                            hev = fw.op(A, A.activation, ful[:, off:off + w_],
                                        PS[psi][0:P, 0:w_], AF.Identity,
                                        sem_name="act", waits=(hmm, ful_free))
                            ps_free[psi] = hev
                            off += w_
                        rg = it % 4
                        hfs = fw.dma(SY, scr["bd_d"][rg], ful[:], "bds",
                                     waits=(fw.h("act"), ring_free[rg]))
                        ful_free = hfs
                        sb_ = it % 2
                        shift_ap = AP(bd_t, rg * P * 2176 + 127,
                                      [[2175, P], [1, S]])
                        hsh = fw.dma(SY, bdsh[sb_][:], shift_ap, f"sh{sb_}",
                                     waits=(hfs, bdsh_free[sb_]))
                        ring_free[rg] = hsh
                        for sl in range(4):
                            psi = rot.get()
                            fw.wait(T, hk2)
                            inst = T.matmul(PS[psi][0:P, 0:512],
                                            q2[hq:hq + 64, it * P:(it + 1) * P],
                                            k2[hq:hq + 64, sl * 512:(sl + 1) * 512],
                                            start=True, stop=True)
                            hmm = fw.inc(inst, "pe")
                            hev = fw.op(V, V.tensor_add,
                                        S_f[:, sl * 512:(sl + 1) * 512],
                                        bdsh[sb_][:, sl * 512:(sl + 1) * 512],
                                        PS[psi][0:P, 0:512],
                                        sem_name="dve",
                                        waits=(hmm, hsh, S_f_free))
                            ps_free[psi] = hev
                        hadds = fw.h("dve")
                        bdsh_free[sb_] = hadds
                        hexp = fw.op(A, A.activation, Ptile[:], S_f[:], AF.Exp,
                                     bias=negBc[:], scale=0.125,
                                     accum_out=denall[:, it:it + 1],
                                     sem_name="act",
                                     waits=(hadds, hnB, Ptile_free, denall_free))
                        S_f_free = hexp
                        for g8 in range(2):
                            psi = rot.get()
                            fw.wait(T, hexp, ps_free[psi])
                            pb16 = PS[psi][:, :].bitcast(BF16)
                            for k in range(8):
                                jc = g8 * 8 + k
                                inst = T.transpose(
                                    pb16[0:P, k * P:(k + 1) * P],
                                    Ptile[:, jc * P:(jc + 1) * P], eyeb_s[:])
                                hmm = fw.inc(inst, "pe")
                            hcp = fw.op(V, V.tensor_copy,
                                        PTf[:, g8 * 8:(g8 + 1) * 8,
                                            it * P:(it + 1) * P],
                                        pb16[0:P, 0:8 * P], sem_name="dve",
                                        waits=(hmm,))
                            ps_free[psi] = hcp
                        Ptile_free = hmm
                    hptf = fw.h("dve")
                    psi = rot.get()
                    fw.wait(T, fw.h("act"))
                    inst = T.transpose(PS[psi][0:16, 0:P], denall[:], eyef_s[:])
                    hmm = fw.inc(inst, "pe")
                    denall_free = hmm
                    hd16 = fw.op(A, A.activation, den16[:], PS[psi][0:16, 0:P],
                                 AF.Identity, sem_name="act",
                                 waits=(hmm, den16_free))
                    ps_free[psi] = hd16
                    den16_free = fw.dma(SY, den_d[:], den16[:], "dns",
                                        waits=(hd16,))
                    flat_ap = AP(den_t, 0, [[S, 1], [1, S]])
                    hdf = fw.dma(SY, rdenr[:], flat_ap, "dnl",
                                 waits=(rdenr_free, den16_free))
                    hrc = fw.op(V, V.reciprocal, rdenr[:], rdenr[:],
                                sem_name="dve", waits=(hdf,))
                    hrd = fw.op(A, A.activation, rdenb2[:], rdenr[:], AF.Identity,
                                sem_name="act", waits=(hrc, rdenb2_free))
                    rdenr_free = hrd
                    for jc in range(16):
                        fw.wait(T, hptf, hv2)
                        if jc == 0:
                            fw.wait(T, ps_free[4], ps_free[5], ps_free[6],
                                    ps_free[7])
                        for isl in range(4):
                            inst = T.matmul(PS[4 + isl][0:64, 0:512],
                                            v2[:, jc, hq:hq + 64],
                                            PTf[:, jc, isl * 512:(isl + 1) * 512],
                                            start=(jc == 0), stop=(jc == 15))
                        fw.inc(inst, "pe")
                    hV = fw.h("pe")
                    for isl in range(4):
                        psi = rot.get()
                        fw.wait(T, hrd)
                        inst = T.matmul(PS[psi][0:64, 0:512], ones_bf[0:1, 0:64],
                                        rdenb2[0:1, isl * 512:(isl + 1) * 512],
                                        start=True, stop=True)
                        hmm = fw.inc(inst, "pe")
                        rdenb2_free = hmm
                        hz = fw.op(A, A.activation, zsl[:],
                                   PS[4 + isl][0:64, 0:512], AF.Identity,
                                   sem_name="act", waits=(hV, zsl_free))
                        ps_free[4 + isl] = hz
                        hmul = fw.op(V, V.tensor_mul, zmul[:], zsl[:],
                                     PS[psi][0:64, 0:512], sem_name="dve",
                                     waits=(hmm, hz, zmul_free))
                        ps_free[psi] = hmul
                        zsl_free = hmul
                        zmul_free = fw.dma(
                            SY, scr["ccos_d"][b, hp * P + hq:hp * P + hq + 64,
                                              isl * 512:(isl + 1) * 512],
                            zmul[:], "zst", waits=(hmul,))
                ld_free = fw.h("pe")
    ctx4.close()

    fw.new_phase()
    ctx5 = ExitStack()
    e5 = ctx5.enter_context
    if True:
        wob = e5(nc.sbuf_tensor("wob", [P, NCH, D], BF16))
        cc3 = e5(nc.sbuf_tensor("cc3", [P, NCH, 512], BF16))
        xnt3 = e5(nc.sbuf_tensor("xnt3", [P, 2, 512], BF16))
        srct = e5(nc.sbuf_tensor("srct", [P, 2, 512], BF16))
        hwo = fw.dma(G, wob[:], fwo.rearrange("(c p) m -> p c m", p=P), "wol")
        rot = Rot(range(6))
        cc3_free = [None] * NCH
        xnt_free = [None, None]
        srct_free = [None, None]
        for b in range(NB):
            for ts in range(4):
                hcc = []
                for hc in range(NCH):
                    hcc.append(fw.dma(
                        SY, cc3[:, hc, :],
                        scr["ccos_d"][b, hc * P:(hc + 1) * P,
                                      ts * 512:(ts + 1) * 512],
                        f"cc{hc}", waits=(cc3_free[hc],)))
                for dc in range(NCH):
                    psi = rot.get()
                    fw.wait(T, hwo, *hcc)
                    for hc in range(NCH):
                        inst = T.matmul(PS[psi][:], wob[:, hc, dc * P:(dc + 1) * P],
                                        cc3[:, hc, :], start=(hc == 0),
                                        stop=(hc == NCH - 1))
                    hmm = fw.inc(inst, "pe")
                    sb = dc % 2
                    hxnt = fw.dma(SY, xnt3[:, sb, :],
                                  xn_d[b, dc * P:(dc + 1) * P,
                                       ts * 512:(ts + 1) * 512],
                                  f"xt{sb}", waits=(xnt_free[sb],))
                    hadd = fw.op(V, V.tensor_add, srct[:, sb, :], xnt3[:, sb, :],
                                 PS[psi][:], sem_name="dve",
                                 waits=(hmm, hxnt, srct_free[sb]))
                    ps_free[psi] = hadd
                    xnt_free[sb] = hadd
                    srct_free[sb] = fw.dma(
                        SY, src2_d[b, dc * P:(dc + 1) * P, ts * 512:(ts + 1) * 512],
                        srct[:, sb, :], f"sr{sb}", waits=(hadd,))
                for hc in range(NCH):
                    cc3_free[hc] = fw.h("pe")
    ctx5.close()

    def src2_tile(b, dc, t0, tile_ap, waits):
        return fw.dma(SY, tile_ap, src2_d[b, dc * P:(dc + 1) * P, t0:t0 + 512],
                      f"lnx{dc}", waits=waits)

    fw.new_phase()
    emit_ln(src2_tile, x2_d, gain=pcs["ln2g_pc"], bias=pcs["ln2b_pc"])

    fw.new_phase()
    ctx6 = ExitStack()
    e6 = ctx6.enter_context
    if True:
        x2f = e6(nc.sbuf_tensor("x2f", [P, NCH, S], BF16))
        selb = e6(nc.sbuf_tensor("selb", [P, NCH, 32], BF16))
        selt = e6(nc.sbuf_tensor("selt", [P, 32], F32))
        curt = e6(nc.sbuf_tensor("curt", [P, 32], F32))
        mc1 = e6(nc.sbuf_tensor("mc1", [P, 1], F32))
        gatesV = e6(nc.sbuf_tensor("gatesV", [P, 32], F32))
        gatesA = e6(nc.sbuf_tensor("gatesA", [P, 32], F32))
        gall = e6(nc.sbuf_tensor("gall", [32, S], F32))
        grow = e6(nc.sbuf_tensor("grow", [1, 32 * S], BF16))
        keyb = [e6(nc.sbuf_tensor(f"keyb{i}", [P, NCH, P], BF16)) for i in range(2)]
        valb = [e6(nc.sbuf_tensor(f"valb{i}", [P, D], BF16)) for i in range(2)]
        hrb = e6(nc.sbuf_tensor("hrb", [P, 2, 256], BF16))
        hgb = e6(nc.sbuf_tensor("hgb", [P, 2, 256], BF16))
        hgl = e6(nc.sbuf_tensor("hgl", [P, 2, 256], BF16))
        s2t = e6(nc.sbuf_tensor("s2t", [P, 2, 256], BF16))
        sst = e6(nc.sbuf_tensor("sst", [P, 2, 256], BF16))
        xft = e6(nc.sbuf_tensor("xft", [P, 2, 256], F32))
        ut = e6(nc.sbuf_tensor("ut", [P, 2, 256], F32))
        gt = e6(nc.sbuf_tensor("gt", [P, 2, 256], BF16))
        t1t = e6(nc.sbuf_tensor("t1t", [P, 2, 256], F32))
        xot = e6(nc.sbuf_tensor("xot", [P, 2, 256], F32))
        hsel = fw.dma(G, selb[:], sel_w.rearrange("(c p) e -> p c e", p=P), "sel")
        rot = Rot(range(3))
        selt_free = None
        gall_free = None
        key_free = [None, None]
        hload = [None, None]
        out_free = [None] * NCH
        cmb_free = [[None, None] for _ in range(7)]
        for b in range(NB):
            hx2 = fw.dma(SY, x2f[:], x2_d[b].rearrange("(c p) t -> p c t", p=P),
                         "x2f", waits=(fw.h("pe"),))
            for tc in range(16):
                psi = rot.get()
                fw.wait(T, hsel, hx2)
                for dc in range(NCH):
                    inst = T.matmul(PS[psi][0:P, 0:32],
                                    x2f[:, dc, tc * P:(tc + 1) * P],
                                    selb[:, dc, :], start=(dc == 0),
                                    stop=(dc == NCH - 1))
                hmm = fw.inc(inst, "pe")
                fw.op(A, A.activation, selt[:], PS[psi][0:P, 0:32], AF.Sigmoid,
                      sem_name="act", waits=(hmm, selt_free))
                fw.op(V, V.tensor_copy, curt[:], selt[:], sem_name="dve",
                      waits=(fw.h("act"),))
                for _ in range(3):
                    fw.op(V, V.reduce_max, mc1[:], curt[:], axis=AX.X,
                          sem_name="dve")
                    fw.op(V, V.scalar_tensor_tensor, curt[:], curt[:], mc1[:],
                          curt[:], ALU.is_lt, ALU.mult, sem_name="dve")
                fw.op(V, V.reduce_max, mc1[:], curt[:], axis=AX.X, sem_name="dve")
                fw.op(V, V.scalar_tensor_tensor, gatesV[:], selt[:], mc1[:],
                      selt[:], ALU.is_ge, ALU.mult, sem_name="dve")
                selt_free = fw.h("dve")
                hga = fw.op(A, A.activation, gatesA[:], gatesV[:], AF.Identity,
                            sem_name="act", waits=(selt_free,))
                ps_free[psi] = hga
                psi2 = rot.get()
                fw.wait(T, hga)
                inst = T.transpose(PS[psi2][0:32, 0:P], gatesA[:], eyef_s[:])
                hmm2 = fw.inc(inst, "pe")
                hgl_ = fw.op(A, A.activation, gall[:, tc * P:(tc + 1) * P],
                             PS[psi2][0:32, 0:P], AF.Identity, sem_name="act",
                             waits=(hmm2, gall_free))
                ps_free[psi2] = hgl_
            hgst = fw.dma(SY, gates_d[b], gall[:], "gst", waits=(fw.h("act"),))
            gall_free = hgst
            for e in range(32):
                fw.dma(G, grow[0:1, e * S:(e + 1) * S], gates_d[b, e:e + 1, :],
                       "grw", waits=(hgst, fw.h("pe")))
            hgrow = fw.ht("grw", "g")
            for tt in range(NB * 4):
                to = tt * 256
                for e in range(32):
                    r = e % 2
                    if e == 0 and tt == 0 and b == 0:
                        hk0 = fw.dma(SY, keyb[0][:].rearrange(
                            "p c f -> p (c f)"), keyp_d[0], "ky0",
                            waits=(kpc_done,))
                        hload[0] = (hk0, fw.dma(SY, valb[0][:], valp_d[0],
                                                "vl0"))
                    if not (e == 31 and tt == NB * 4 - 1 and b == NB - 1):
                        nxt = (e + 1) % 32
                        rn = nxt % 2
                        fw.wait(SY, key_free[rn])
                        hkn = fw.dma(SY, keyb[rn][:].rearrange(
                            "p c f -> p (c f)"), keyp_d[nxt], f"ky{rn}",
                            waits=(kpc_done,))
                        hload[rn] = (hkn,
                                     fw.dma(SY, valb[rn][:], valp_d[nxt],
                                            f"vl{rn}"))
                    fw.wait(T, hgrow, ps_free[3])
                    inst = T.matmul(PS[3][0:P, 0:256], ones_bf[:],
                                    grow[0:1, e * S + to:e * S + to + 256],
                                    start=True, stop=True)
                    hgb_ = fw.inc(inst, "pe")
                    psi_h = e % 2
                    fw.wait(T, ps_free[psi_h], hx2, *hload[r])
                    for dc in range(NCH):
                        inst = T.matmul(PS[psi_h][0:P, 0:256],
                                        keyb[r][:, dc, :],
                                        x2f[:, dc, to:to + 256],
                                        start=(dc == 0), stop=(dc == NCH - 1))
                    hmm = fw.inc(inst, "pe")
                    hr_ = fw.op(A, A.activation, hrb[:, e % 2, :],
                                PS[psi_h][0:P, 0:256], AF.Relu, sem_name="act",
                                waits=(hmm,))
                    ps_free[psi_h] = hr_
                    hg_ = fw.op(V, V.tensor_mul, hgb[:, e % 2, :],
                                hrb[:, e % 2, :], PS[3][0:P, 0:256],
                                sem_name="dve", waits=(hgb_, hr_))
                    ps_free[3] = hg_
                    hl_ = fw.op(A, A.activation, hgl[:, e % 2, :],
                                hgb[:, e % 2, :], AF.Identity, sem_name="act",
                                waits=(hg_,))
                    fw.wait(T, hl_)
                    if e == 0:
                        fw.wait(T, *[out_free[i] for i in range(NCH)])
                    for dc in range(NCH):
                        pt = 4 + dc % 4
                        rg0 = (dc // 4) * 256
                        inst = T.matmul(PS[pt][0:P, rg0:rg0 + 256],
                                        valb[r][:, dc * P:(dc + 1) * P],
                                        hgl[:, e % 2, :],
                                        start=(e == 0 and dc < 4),
                                        stop=(e == 31 and dc >= 4),
                                        skip_group_check=True)
                    key_free[r] = fw.inc(inst, "pe")
                hMoE = fw.h("pe")
                for dc in range(NCH):
                    pt = 4 + dc % 4
                    rg0 = (dc // 4) * 256
                    cb = dc % 2
                    h1 = fw.dma(SY, s2t[:, cb, :],
                                src2_d[b, dc * P:(dc + 1) * P, to:to + 256],
                                f"cb1{cb}", waits=(cmb_free[0][cb],))
                    h2 = fw.dma(SY, sst[:, cb, :],
                                s_d[b, dc * P:(dc + 1) * P, to:to + 256],
                                f"cb2{cb}", waits=(cmb_free[1][cb],))
                    h3 = fw.dma(SY, xft[:, cb, :],
                                x_in[b, dc * P:(dc + 1) * P, to:to + 256],
                                f"cb3{cb}", waits=(cmb_free[2][cb],))
                    hu = fw.op(V, V.tensor_add, ut[:, cb, :], s2t[:, cb, :],
                               PS[pt][0:P, rg0:rg0 + 256], sem_name="dve",
                               waits=(hMoE, h1, cmb_free[3][cb]))
                    out_free[dc] = hu
                    hgsl = fw.op(A, A.activation, gt[:, cb, :], ut[:, cb, :],
                                 AF.Silu, sem_name="act",
                                 waits=(hu, cmb_free[4][cb]))
                    ht1 = fw.op(V, V.tensor_mul, t1t[:, cb, :], sst[:, cb, :],
                                gt[:, cb, :], sem_name="dve",
                                waits=(hgsl, h2))
                    hxo = fw.op(V, V.tensor_add, xot[:, cb, :], xft[:, cb, :],
                                t1t[:, cb, :], sem_name="dve",
                                waits=(h3, cmb_free[5][cb], cmb_free[6][cb]))
                    hst1 = fw.dma(SY, xnew_d[b, dc * P:(dc + 1) * P, to:to + 256],
                                  xot[:, cb, :], f"xs{cb}", waits=(hxo,))
                    hst2 = fw.dma(G, xnewb_d[b, dc * P:(dc + 1) * P, to:to + 256],
                                  xot[:, cb, :], f"xp{cb}", waits=(hxo,))
                    cmb_free[0][cb] = hu
                    cmb_free[1][cb] = ht1
                    cmb_free[2][cb] = hxo
                    cmb_free[3][cb] = hgsl
                    cmb_free[4][cb] = ht1
                    cmb_free[5][cb] = hst2
                    cmb_free[6][cb] = hst1
    ctx6.close()


    if limit < 4:
        return finish()

    def xnb_tile(b, dc, t0, tile_ap, waits):
        return fw.dma(SY, tile_ap, xnewb_d[b, dc * P:(dc + 1) * P, t0:t0 + 512],
                      f"lnx{dc}", waits=waits)

    emit_ln(xnb_tile, xn1_d)

    fw.new_phase()
    ctx7 = ExitStack()
    e7 = ctx7.enter_context
    if True:
        aqb = [e7(nc.sbuf_tensor(f"aqb{i}", [P, NCH, 64], BF16)) for i in range(3)]
        brow4 = e7(nc.sbuf_tensor("brow4", [1, 3 * 64], BF16))
        awob = e7(nc.sbuf_tensor("awob", [64, D], BF16))
        xn1f = e7(nc.sbuf_tensor("xn1f", [P, NCH, S], BF16))
        qkvf = e7(nc.sbuf_tensor("qkvf", [P, 192], F32))
        v4 = e7(nc.sbuf_tensor("v4", [P, 16, 64], BF16))
        qfm4 = e7(nc.sbuf_tensor("qfm4", [64, S], BF16))
        kfm4 = e7(nc.sbuf_tensor("kfm4", [64, S], BF16))
        S4 = e7(nc.sbuf_tensor("S4", [P, S], F32))
        m4r = e7(nc.sbuf_tensor("m4r", [P, 1], F32))
        nm4 = e7(nc.sbuf_tensor("nm4", [P, 1], F32))
        Pt4 = e7(nc.sbuf_tensor("Pt4", [P, S], BF16))
        dal4 = e7(nc.sbuf_tensor("dal4", [P, 16], F32))
        d164 = e7(nc.sbuf_tensor("d164", [16, P], F32))
        rdr4 = e7(nc.sbuf_tensor("rdr4", [1, S], F32))
        rdb4 = e7(nc.sbuf_tensor("rdb4", [1, S], BF16))
        PTf4 = e7(nc.sbuf_tensor("PTf4", [P, 16, S], BF16))
        zsl4 = e7(nc.sbuf_tensor("zsl4", [64, 512], BF16))
        zmul4 = e7(nc.sbuf_tensor("zmul4", [64, 512], BF16))
        zln4 = e7(nc.sbuf_tensor("zln4", [64, 512], BF16))
        s1t = e7(nc.sbuf_tensor("s1t", [P, 2, 512], BF16))
        for i, (win, brs) in enumerate([(a1wq, a1bq_row), (a1wk, a1bk_row),
                                        (a1wv, a1bv_row)]):
            fw.dma(G, aqb[i][:], win.rearrange("(c p) m -> p c m", p=P), "a4l")
            fw.dma(G, brow4[0:1, i * 64:(i + 1) * 64], brs, "a4l")
        fw.dma(G, awob[:], a1wo, "a4l")
        ha4 = fw.ht("a4l", "g")
        rot = Rot(range(4))
        den_t4 = scr["den_d"].tensor
        Pt4_free = None
        dal4_free = None
        d164_free = None
        rdr4_free = None
        rdb4_free = None
        zs4_free = None
        zm4_free = None
        zl4_free = None
        s1t_free = [None, None]
        xn1f_free = None
        fw.op(V, V.memset, PTf4[:], 0.0, sem_name="dve")
        for b in range(NB):
            hx4 = fw.dma(SY, xn1f[:], xn1_d[b].rearrange("(c p) t -> p c t", p=P),
                         "xn1f", waits=(xn1f_free,))
            for tc in range(16):
                psi = rot.get()
                fw.wait(T, hx4, ha4)
                for proj in range(3):
                    T.matmul(PS[psi][0:P, proj * 64:(proj + 1) * 64], ones_bf[:],
                             brow4[0:1, proj * 64:(proj + 1) * 64],
                             start=True, stop=False, skip_group_check=True)
                    for dc in range(NCH):
                        inst = T.matmul(PS[psi][0:P, proj * 64:(proj + 1) * 64],
                                        xn1f[:, dc, tc * P:(tc + 1) * P],
                                        aqb[proj][:, dc, :], start=False,
                                        stop=(dc == NCH - 1),
                                        skip_group_check=True)
                hmm = fw.inc(inst, "pe")
                hqk = fw.op(A, A.activation, qkvf[:], PS[psi][0:P, 0:192],
                            AF.Identity, sem_name="act", waits=(hmm, fw.h("pe")))
                hv4 = fw.op(A, A.activation, v4[:, tc, :], PS[psi][0:P, 128:192],
                            AF.Identity, sem_name="act")
                ps_free[psi] = hv4
                for qk in range(2):
                    psi2 = rot.get()
                    fw.wait(T, hqk)
                    inst = T.transpose(PS[psi2][0:64, 0:P],
                                       qkvf[:, qk * 64:(qk + 1) * 64], eyef_s[:])
                    hmm2 = fw.inc(inst, "pe")
                    dst = (qfm4 if qk == 0 else kfm4)[:, tc * P:(tc + 1) * P]
                    hev = fw.op(A, A.activation, dst, PS[psi2][0:64, 0:P],
                                AF.Identity, sem_name="act", waits=(hmm2,))
                    ps_free[psi2] = hev
            qkv_done = fw.h("act")
            S4_free = None
            nm_free = None
            for it in range(16):
                nsl = it // 4 + 1
                cols = nsl * 512
                for sl in range(nsl):
                    psi = rot.get()
                    fw.wait(T, qkv_done)
                    inst = T.matmul(PS[psi][0:P, 0:512],
                                    qfm4[:, it * P:(it + 1) * P],
                                    kfm4[:, sl * 512:(sl + 1) * 512],
                                    start=True, stop=True)
                    hmm = fw.inc(inst, "pe")
                    hev = fw.op(A, A.activation, S4[:, sl * 512:(sl + 1) * 512],
                                PS[psi][0:P, 0:512], AF.Identity,
                                sem_name="act", waits=(hmm, S4_free))
                    ps_free[psi] = hev
                doff = it * P - (nsl - 1) * 512
                hmask = fw.op(V, V.tensor_add, S4[:, (nsl - 1) * 512 + doff:
                                                 (nsl - 1) * 512 + doff + P],
                              S4[:, (nsl - 1) * 512 + doff:
                                 (nsl - 1) * 512 + doff + P],
                              trineg_s[:], sem_name="dve", waits=(fw.h("act"),))
                if doff + P < 512:
                    hmask = fw.op(V, V.memset,
                                  S4[:, (nsl - 1) * 512 + doff + P:cols], NEG,
                                  sem_name="dve")
                hmx = fw.op(V, V.reduce_max, m4r[:], S4[:, 0:cols], axis=AX.X,
                            sem_name="dve")
                hnm = fw.op(V, V.tensor_scalar, nm4[:], m4r[:], -0.125, None,
                            ALU.mult, sem_name="dve", waits=(nm_free,))
                S4_free = hnm
                hexp = fw.op(A, A.activation, Pt4[:, 0:cols], S4[:, 0:cols],
                             AF.Exp, bias=nm4[:], scale=0.125,
                             accum_out=dal4[:, it:it + 1], sem_name="act",
                             waits=(hnm, Pt4_free, dal4_free))
                nm_free = hexp
                for g8 in range((it + 8) // 8):
                    nk = min(it + 1 - g8 * 8, 8)
                    psi = rot.get()
                    fw.wait(T, hexp, ps_free[psi])
                    pb16 = PS[psi][:, :].bitcast(BF16)
                    for k in range(nk):
                        jc = g8 * 8 + k
                        inst = T.transpose(pb16[0:P, k * P:(k + 1) * P],
                                           Pt4[:, jc * P:(jc + 1) * P],
                                           eyeb_s[:])
                        hmm = fw.inc(inst, "pe")
                    hcp = fw.op(V, V.tensor_copy,
                                PTf4[:, g8 * 8:g8 * 8 + nk, it * P:(it + 1) * P],
                                pb16[0:P, 0:nk * P], sem_name="dve",
                                waits=(hmm,))
                    ps_free[psi] = hcp
                Pt4_free = hmm
            hptf4 = fw.h("dve")
            psi = rot.get()
            fw.wait(T, fw.h("act"))
            inst = T.transpose(PS[psi][0:16, 0:P], dal4[:], eyef_s[:])
            hmm = fw.inc(inst, "pe")
            dal4_free = hmm
            hd16 = fw.op(A, A.activation, d164[:], PS[psi][0:16, 0:P],
                         AF.Identity, sem_name="act", waits=(hmm, d164_free))
            ps_free[psi] = hd16
            d164_free = fw.dma(SY, den_d[:], d164[:], "dn4", waits=(hd16,))
            hdf = fw.dma(SY, rdr4[:], AP(den_t4, 0, [[S, 1], [1, S]]), "dl4",
                         waits=(rdr4_free, d164_free))
            hrc = fw.op(V, V.reciprocal, rdr4[:], rdr4[:], sem_name="dve",
                        waits=(hdf,))
            hrd = fw.op(A, A.activation, rdb4[:], rdr4[:], AF.Identity,
                        sem_name="act", waits=(hrc, rdb4_free))
            rdr4_free = hrd
            for jc in range(16):
                fw.wait(T, hptf4, qkv_done)
                if jc == 0:
                    fw.wait(T, ps_free[4], ps_free[5], ps_free[6], ps_free[7])
                for isl in range(4):
                    inst = T.matmul(PS[4 + isl][0:64, 0:512], v4[:, jc, :],
                                    PTf4[:, jc, isl * 512:(isl + 1) * 512],
                                    start=(jc == 0), stop=(jc == 15))
                fw.inc(inst, "pe")
            hV = fw.h("pe")
            for isl in range(4):
                psi = rot.get()
                fw.wait(T, hrd)
                inst = T.matmul(PS[psi][0:64, 0:512], ones_bf[0:1, 0:64],
                                rdb4[0:1, isl * 512:(isl + 1) * 512],
                                start=True, stop=True)
                hmm = fw.inc(inst, "pe")
                rdb4_free = hmm
                hz = fw.op(A, A.activation, zsl4[:], PS[4 + isl][0:64, 0:512],
                           AF.Identity, sem_name="act", waits=(hV, zs4_free))
                ps_free[4 + isl] = hz
                hmul = fw.op(V, V.tensor_mul, zmul4[:], zsl4[:],
                             PS[psi][0:64, 0:512], sem_name="dve",
                             waits=(hmm, hz, zm4_free))
                ps_free[psi] = hmul
                zs4_free = hmul
                hzl = fw.op(A, A.activation, zln4[:], zmul4[:], AF.Identity,
                            sem_name="act", waits=(hmul, zl4_free))
                zm4_free = hzl
                for dc in range(NCH):
                    psi3 = rot.get()
                    fw.wait(T, hzl, ha4)
                    inst = T.matmul(PS[psi3][0:P, 0:512],
                                    awob[:, dc * P:(dc + 1) * P], zln4[:],
                                    start=True, stop=True)
                    hmm3 = fw.inc(inst, "pe")
                    if dc == NCH - 1:
                        zl4_free = hmm3
                    sb = dc % 2
                    hse = fw.op(A, A.activation, s1t[:, sb, :], PS[psi3][0:P, 0:512],
                                AF.Identity, bias=pcs["a1bo_pc"][:, dc:dc + 1],
                                sem_name="act", waits=(hmm3, s1t_free[sb]))
                    ps_free[psi3] = hse
                    s1t_free[sb] = fw.dma(
                        SY, s1_d[b, dc * P:(dc + 1) * P,
                                 isl * 512:(isl + 1) * 512],
                        s1t[:, sb, :], f"s1s{sb}", waits=(hse,))
            xn1f_free = fw.h("pe")
    ctx7.close()

    def xn1_tile(b, dc, t0, tile_ap, waits):
        return fw.dma(SY, tile_ap, xn1_d[b, dc * P:(dc + 1) * P, t0:t0 + 512],
                      f"lnx{dc}", waits=waits)

    emit_ln(xn1_tile, xnorm_d, gain=pcs["gam_pc"], bias=pcs["bet_pc"])

    fw.new_phase()
    ctx8 = ExitStack()
    e8 = ctx8.enter_context
    if True:
        wgb = e8(nc.sbuf_tensor("wgb", [P, NCH, 8], BF16))
        xg = [e8(nc.sbuf_tensor(f"xg{i}", [P, NCH, P], BF16)) for i in range(2)]
        eprob = e8(nc.sbuf_tensor("eprob", [P, 8], F32))
        curg = e8(nc.sbuf_tensor("curg", [P, 8], F32))
        mg1 = e8(nc.sbuf_tensor("mg1", [P, 1], F32))
        mg2 = e8(nc.sbuf_tensor("mg2", [P, 1], F32))
        sum8 = e8(nc.sbuf_tensor("sum8", [P, 1], F32))
        d2g = e8(nc.sbuf_tensor("d2g", [P, 1], F32))
        gt8 = e8(nc.sbuf_tensor("gt8", [P, 8], F32))
        gt8A = e8(nc.sbuf_tensor("gt8A", [P, 8], F32))
        g8all = e8(nc.sbuf_tensor("g8all", [8, S], F32))
        g8b = e8(nc.sbuf_tensor("g8b", [8, S], BF16))
        g8row = [e8(nc.sbuf_tensor(f"g8row{i}", [1, S], BF16)) for i in range(2)]
        b2b = e8(nc.sbuf_tensor("b2b", [P, 8, NCH], F32))
        w1r4 = [e8(nc.sbuf_tensor(f"w1r4{i}", [P, NCH, 512], BF16))
                for i in range(2)]
        w2r4 = [e8(nc.sbuf_tensor(f"w2r4{i}", [P, 4, D], BF16)) for i in range(2)]
        b1b4 = [e8(nc.sbuf_tensor(f"b1b4{i}", [P, 32], F32)) for i in range(2)]
        hall4 = e8(nc.sbuf_tensor("hall4", [P, 32, 512], BF16))
        xnt4 = [e8(nc.sbuf_tensor(f"xnt4{i}", [P, NCH, 512], BF16))
                for i in range(2)]
        gbs4 = e8(nc.sbuf_tensor("gbs4", [P, 512], BF16))
        accs4 = e8(nc.sbuf_tensor("accs4", [P, NCH, S], F32))
        uc = e8(nc.sbuf_tensor("uc", [P, 2, 512], F32))
        gc = e8(nc.sbuf_tensor("gc", [P, 2, 512], BF16))
        xn1c = e8(nc.sbuf_tensor("xn1c", [P, 2, 512], BF16))
        s1c = e8(nc.sbuf_tensor("s1c", [P, 2, 512], BF16))
        xnc = e8(nc.sbuf_tensor("xnc", [P, 2, 512], F32))
        t1c = e8(nc.sbuf_tensor("t1c", [P, 2, 512], F32))
        yc = e8(nc.sbuf_tensor("yc", [P, 2, 512], F32))
        hwg = fw.dma(G, wgb[:], WgT.rearrange("(c p) e -> p c e", p=P), "wg4")
        hb2 = fw.dma(SY, b2b[:], b2pc, "b24")
        rot = Rot(range(4))
        ep_free = None
        gt_free = None
        gt8A_free = None
        hex8_prev = None
        w1g_free = [None, None]
        hwc2 = [None, None]
        hgst = None
        g8a_free = None
        xg_free = None
        w1_free = None
        w2_free = [None, None]
        b1_free = [None, None]
        xnt_free = [None, None]
        gbs_free = None
        hall_free = None
        hw1 = None
        hb1 = None
        cmb4 = [[None, None] for _ in range(8)]
        xg_free = [None, None]
        for b in range(NB):
            for tc in range(16):
                gb_ = tc % 2
                hxg = fw.dma(SY, xg[gb_][:],
                             xnorm_d[b][:, tc * P:(tc + 1) * P].rearrange(
                                 "(c p) t -> p c t", p=P),
                             f"xg4{gb_}", waits=(xg_free[gb_],))
                psi = rot.get()
                fw.wait(T, hwg, hxg)
                for dc in range(NCH):
                    inst = T.matmul(PS[psi][0:P, 0:8],
                                    xg[gb_][:, dc, :],
                                    wgb[:, dc, :], start=(dc == 0),
                                    stop=(dc == NCH - 1))
                hmm = fw.inc(inst, "pe")
                xg_free[gb_] = hmm
                fw.wait(V, hmm, ep_free, hex8_prev)
                fw.op(V, V.reduce_max, mg1[:], PS[psi][0:P, 0:8], axis=AX.X,
                      sem_name="dve")
                fw.op(V, V.tensor_scalar, mg2[:], mg1[:], -1.0, None, ALU.mult,
                      sem_name="dve")
                hex8 = fw.op(A, A.activation, eprob[:], PS[psi][0:P, 0:8],
                             AF.Exp, bias=mg2[:], accum_out=sum8[:],
                             sem_name="act", waits=(fw.h("dve"),))
                ps_free[psi] = hex8
                hex8_prev = hex8
                fw.op(V, V.reduce_max, mg1[:], eprob[:], axis=AX.X,
                      sem_name="dve", waits=(hex8,))
                fw.op(V, V.scalar_tensor_tensor, curg[:], eprob[:], mg1[:],
                      eprob[:], ALU.is_lt, ALU.mult, sem_name="dve")
                fw.op(V, V.reduce_max, mg2[:], curg[:], axis=AX.X,
                      sem_name="dve")
                fw.op(V, V.tensor_add, d2g[:], mg1[:], mg2[:], sem_name="dve")
                fw.op(V, V.scalar_tensor_tensor, d2g[:], sum8[:], 1e-9,
                      d2g[:], ALU.mult, ALU.add, sem_name="dve")
                fw.op(V, V.reciprocal, d2g[:], d2g[:], sem_name="dve")
                fw.op(V, V.scalar_tensor_tensor, gt8[:], eprob[:], mg2[:],
                      eprob[:], ALU.is_ge, ALU.mult, sem_name="dve",
                      waits=(gt_free,))
                ep_free = fw.op(V, V.tensor_scalar, gt8[:], gt8[:], d2g[:],
                                None, ALU.mult, sem_name="dve")
                hga = fw.op(A, A.activation, gt8A[:], gt8[:], AF.Identity,
                            sem_name="act", waits=(ep_free, gt8A_free))
                psi2 = rot.get()
                fw.wait(T, hga)
                inst = T.transpose(PS[psi2][0:8, 0:P], gt8A[:], eyef_s[:])
                hmm2 = fw.inc(inst, "pe")
                gt_free = hmm2
                gt8A_free = hmm2
                hgl = fw.op(A, A.activation, g8all[:, tc * P:(tc + 1) * P],
                            PS[psi2][0:8, 0:P], AF.Identity, sem_name="act",
                            waits=(hmm2, g8a_free))
                ps_free[psi2] = hgl
            hg8b = fw.op(A, A.activation, g8b[:], g8all[:], AF.Identity,
                         sem_name="act")
            hgst = fw.dma(SY, g8_d[b], g8all[:], "g84", waits=(fw.h("act"),))
            g8a_free = hgst

            hacz = fw.op(V, V.memset, accs4[:], 0.0, sem_name="dve",
                         waits=(fw.h("pe"),))
            hgr = [None, None]
            hb1r = [None, None]
            for e in range(8):
                if e == 0:
                    hb1r[0] = fw.dma(SY, b1b4[0][:], b1_pc[0], "b14a",
                                     waits=(fw.h("act"),))
                    hgr[0] = fw.dma(G, g8row[0][:], g8_d[b, 0:1, :], "gr40",
                                    waits=(hgst, fw.h("pe")))
                if e + 1 < 8:
                    hb1r[(e + 1) % 2] = fw.dma(
                        SY, b1b4[(e + 1) % 2][:], b1_pc[e + 1],
                        f"b14{(e + 1) % 2}", waits=(fw.h("act"),))
                    hgr[(e + 1) % 2] = fw.dma(
                        G, g8row[(e + 1) % 2][:], g8_d[b, e + 1:e + 2, :],
                        f"gr4{(e + 1) % 2}", waits=(hgst, fw.h("pe")))
                for tt in range(4):
                    to = tt * 512
                    xb_ = tt % 2
                    hxt = []
                    for dc in range(NCH):
                        hxt.append(fw.dma(
                            SY, xnt4[xb_][:, dc, :],
                            xnorm_d[b, dc * P:(dc + 1) * P, to:to + 512],
                            f"xq{xb_}_{dc}", waits=(xnt_free[xb_],)))
                    psi = rot.get()
                    fw.wait(T, hgr[e % 2])
                    inst = T.matmul(PS[psi][0:P, 0:512], ones_bf[:],
                                    g8row[e % 2][0:1, to:to + 512],
                                    start=True, stop=True)
                    hmm = fw.inc(inst, "pe")
                    hgbs = fw.op(A, A.activation, gbs4[:], PS[psi][0:P, 0:512],
                                 AF.Identity, sem_name="act",
                                 waits=(hmm, gbs_free))
                    ps_free[psi] = hgbs
                    hw1g = [None, None]
                    for fc in range(32):
                        psi_h = fc % 2
                        if fc % 4 == 0:
                            fg = fc // 4
                            wr1 = fg % 2
                            fw.wait(SY, w1g_free[wr1])
                            hw1g[wr1] = fw.dma(
                                SY, w1r4[wr1][:].rearrange("p c f -> p (c f)"),
                                w1p_d[e, fg], f"w14{wr1}", waits=(wpc_done,))
                        fw.wait(T, ps_free[psi_h], hw1g[(fc // 4) % 2], *hxt)
                        for dc in range(NCH):
                            inst = T.matmul(PS[psi_h][0:P, 0:512],
                                            w1r4[(fc // 4) % 2][:, dc,
                                                (fc % 4) * P:(fc % 4 + 1) * P],
                                            xnt4[xb_][:, dc, :],
                                            start=(dc == 0), stop=(dc == NCH - 1))
                        hmm = fw.inc(inst, "pe")
                        if fc % 4 == 3:
                            w1g_free[(fc // 4) % 2] = hmm
                        hge = fw.op(A, A.activation, hall4[:, fc, :],
                                    PS[psi_h][0:P, 0:512], AF.Gelu_apprx_tanh,
                                    bias=b1b4[e % 2][:, fc:fc + 1],
                                    sem_name="act",
                                    waits=(hmm, hb1r[e % 2], hall_free))
                        ps_free[psi_h] = hge
                    hW1done = fw.h("pe")
                    xnt_free[xb_] = hW1done
                    for fc in range(32):
                        if fc % 4 == 0:
                            fg = fc // 4
                            wr = fg % 2
                            fw.wait(SY, w2_free[wr])
                            hwc2[wr] = fw.dma(
                                SY, w2r4[wr][:].rearrange("p c d -> p (c d)"),
                                w2p_d[e, fg], f"w24{wr}", waits=(wpc_done,))
                        wr = (fc // 4) % 2
                        fw.wait(T, hwc2[wr], fw.h("act"))
                        if fc == 0:
                            fw.wait(T, *[ps_free[i] for i in range(8)])
                        for dc in range(NCH):
                            inst = T.matmul(PS[dc][0:P, 0:512],
                                            w2r4[wr][:, fc % 4,
                                                dc * P:(dc + 1) * P],
                                            hall4[:, fc, :],
                                            start=(fc == 0), stop=(fc == 31))
                        hmmw = fw.inc(inst, "pe")
                        if fc % 4 == 3:
                            w2_free[wr] = hmmw
                    hall_free = fw.h("pe")
                    for dc in range(NCH):
                        hu = fw.op(V, V.scalar_tensor_tensor, uc[:, dc % 2, :],
                                   PS[dc][0:P, 0:512], b2b[:, e, dc:dc + 1],
                                   gbs4[:], ALU.add, ALU.mult, sem_name="dve",
                                   waits=(fw.h("pe"), hgbs, hacz, hb2,
                                          cmb4[0][dc % 2]))
                        hac = fw.op(V, V.tensor_add,
                                    accs4[:, dc, to:to + 512],
                                    accs4[:, dc, to:to + 512], uc[:, dc % 2, :],
                                    sem_name="dve")
                        cmb4[0][dc % 2] = hac
                        ps_free[dc] = hac
                    gbs_free = fw.h("dve")
            moe_done = fw.h("dve")
            w1_free = fw.h("pe")
            for dc in range(NCH):
                for ts in range(4):
                    to = ts * 512
                    cb = ts % 2
                    h1 = fw.dma(SY, xn1c[:, cb, :],
                                xn1_d[b, dc * P:(dc + 1) * P, to:to + 512],
                                f"f1{cb}", waits=(cmb4[1][cb],))
                    h2 = fw.dma(SY, s1c[:, cb, :],
                                s1_d[b, dc * P:(dc + 1) * P, to:to + 512],
                                f"f2{cb}", waits=(cmb4[2][cb],))
                    h3 = fw.dma(SY, xnc[:, cb, :],
                                xnew_d[b, dc * P:(dc + 1) * P, to:to + 512],
                                f"f3{cb}", waits=(cmb4[3][cb],))
                    hu2 = fw.op(V, V.tensor_add, uc[:, cb, :],
                                xn1c[:, cb, :], accs4[:, dc, to:to + 512],
                                sem_name="dve", waits=(moe_done, h1,
                                                       cmb4[4][cb]))
                    hsi = fw.op(A, A.activation, gc[:, cb, :], uc[:, cb, :],
                                AF.Silu, sem_name="act",
                                waits=(hu2, cmb4[5][cb]))
                    ht1 = fw.op(V, V.tensor_mul, t1c[:, cb, :], s1c[:, cb, :],
                                gc[:, cb, :], sem_name="dve",
                                waits=(hsi, h2, cmb4[6][cb]))
                    hy = fw.op(V, V.tensor_add, yc[:, cb, :], xnc[:, cb, :],
                               t1c[:, cb, :], sem_name="dve",
                               waits=(h3, cmb4[7][cb]))
                    hys = fw.dma(SY, y_out[b, dc * P:(dc + 1) * P, to:to + 512],
                                 yc[:, cb, :], f"ys{cb}", waits=(hy,))
                    cmb4[1][cb] = hu2
                    cmb4[2][cb] = ht1
                    cmb4[3][cb] = hy
                    cmb4[4][cb] = hsi
                    cmb4[5][cb] = ht1
                    cmb4[6][cb] = hy
                    cmb4[7][cb] = hys
    ctx8.close()

    return finish()


def kernel(**inputs):
    in_maps = _host_prep(inputs)
    nc = build_program(sim_strict=True)
    res = run_bass_kernel_spmd(nc, in_maps, core_ids=list(range(NCORE)))
    out = np.concatenate(
        [r["y"].transpose(0, 2, 1) for r in res.results], axis=0)
    return np.ascontiguousarray(out)


def _run_debug(inputs, dbg=(), limit=99, cores=(0,), serialize=False, p2_stop=False):
    in_maps = _host_prep(inputs)
    nc = build_program(dbg=dbg, limit=limit, serialize=serialize, p2_stop=p2_stop)
    res = run_bass_kernel_spmd(nc, [in_maps[c] for c in cores],
                               core_ids=list(cores))
    return res.results
